# version 10 (final)
# speedup vs baseline: 1.1645x; 1.0041x over previous
import contextlib
import os
import numpy as np
import ml_dtypes
import concourse.bass as bass
import concourse.mybir as mybir
from concourse.bass_utils import run_bass_kernel_spmd
from concourse.aot_env import allow_ici_all_to_all

F32 = mybir.dt.float32
BF16 = mybir.dt.bfloat16
AF = mybir.ActivationFunctionType
ALU = mybir.AluOpType
AXX = mybir.AxisListType.X

NCORES = 8
T = 1024
D = 2048
DC = D // 128
NTC = T // 128
bfdt = ml_dtypes.bfloat16

BUILD_STAGE = 1
DEBUG_DUMP = bool(int(os.environ.get("MK_DEBUG", "0")))
P3_PART = int(os.environ.get("MK_P3PART", "4"))
P3X = int(os.environ.get("MK_P3X", "7"))


class Ctx:
    def __init__(self, nc, block, stack):
        self.nc = nc
        self.block = block
        self.stack = stack
        self.sem = {}
        self.cnt = {}
        self.buf = {}

    def new_sem(self, name):
        self.sem[name] = self.stack.enter_context(self.nc.semaphore(name))
        self.cnt[name] = 0

    def inc(self, instr, name, n=1):
        instr.then_inc(self.sem[name], n)
        self.cnt[name] += n
        return self.cnt[name]

    def wait(self, eng, name, v=None):
        eng.wait_ge(self.sem[name], self.cnt[name] if v is None else v)

    def sec(self, engine_name, fn):
        getattr(self.block, engine_name)(fn)

    _uniq = [0]

    def sb(self, stack, name, shape, dt):
        self._uniq[0] += 1
        h = stack.enter_context(self.nc.sbuf_tensor(f"{name}_u{self._uniq[0]}", list(shape), dt))
        self.buf[name] = h
        return h


def emit_p0(c):
    b = c.buf

    def sy(sy):
        ins = sy.dma_start(out=b["R"][:].rearrange("p (dc t) -> p dc t", dc=DC),
                           in_=b["xT"].rearrange("(dc p) t -> p dc t", p=128))
        c.inc(ins, "s_ld", 16)

    def v(v):
        v.memset(b["ones128"][:], 1.0)
        v.memset(b["ones1"][:], 1.0)
        v.memset(b["c_one"][:], 1.0)
        v.memset(b["c_lrb"][:], float(0.01 + np.log(-np.expm1(-0.01))))
        ins = v.memset(b["c_eps"][:], 1e-5)
        c.inc(ins, "init", 1)

    c.sec("sync", sy)
    c.sec("vector", v)


def emit_ln(c, dst_name):
    b = c.buf
    res0 = c.cnt["res"]
    act0 = c.cnt["act"]
    dve0 = c.cnt["dve"]
    pe0 = c.cnt["pe"]

    def s_sq(s):
        c.wait(s, "res", res0)
        c.wait(s, "s_ld", 16)
        for dc in range(DC):
            if dc >= 2:
                c.wait(s, "pe", pe0 + dc - 1)
            ins = s.activation(out=b["sq_scr"][:, dc % 2, :], in_=b["R"][:, dc * T:(dc + 1) * T],
                               func=AF.Square, scale=1.0)
            c.inc(ins, "act", 1)

    def v_rb(v):
        c.wait(v, "res", res0)
        c.wait(v, "s_ld", 16)
        for dc in range(DC):
            if dc >= 2:
                c.wait(v, "pe", pe0 + dc - 1)
            ins = v.tensor_copy(out=b["rb_scr"][:, dc % 2, :], in_=b["R"][:, dc * T:(dc + 1) * T])
            c.inc(ins, "dve", 1)

    c.sec("scalar", s_sq)
    c.sec("vector", v_rb)

    def t_sums(t):
        for dc in range(DC):
            c.wait(t, "act", act0 + dc + 1)
            c.wait(t, "dve", dve0 + dc + 1)
            for j in range(2):
                t.matmul(b["pb"][j][:], b["ones128"][:],
                         b["rb_scr"][:, dc % 2, j * 512:(j + 1) * 512],
                         start=(dc == 0), stop=(dc == DC - 1))
                ins = t.matmul(b["pb"][2 + j][:], b["ones128"][:],
                               b["sq_scr"][:, dc % 2, j * 512:(j + 1) * 512],
                               start=(dc == 0), stop=(dc == DC - 1))
            c.inc(ins, "pe", 1)

    c.sec("tensor", t_sums)
    pe1 = c.cnt["pe"]

    def v_stats(v):
        c.wait(v, "pe", pe1)
        for j in range(2):
            v.tensor_scalar(out=b["lnm"][:, j * 512:(j + 1) * 512], in0=b["pb"][j][:],
                            scalar1=1.0 / D, scalar2=None, op0=ALU.mult)
        v.tensor_tensor(out=b["lnv"][:], in0=b["lnm"][:], in1=b["lnm"][:], op=ALU.mult)
        for j in range(2):
            ins = v.scalar_tensor_tensor(out=b["lnv"][:, j * 512:(j + 1) * 512],
                                         in0=b["pb"][2 + j][:], scalar=1.0 / D,
                                         in1=b["lnv"][:, j * 512:(j + 1) * 512],
                                         op0=ALU.mult, op1=ALU.subtract)
        c.inc(ins, "dve", 1)

    c.sec("vector", v_stats)
    dve2 = c.cnt["dve"]

    def s_rstd(s):
        c.wait(s, "dve", dve2)
        ins = s.activation(out=b["lnv"][:], in_=b["lnv"][:], func=AF.Sqrt, scale=1.0,
                           bias=b["c_eps"][:])
        c.inc(ins, "act", 1)

    c.sec("scalar", s_rstd)
    act2 = c.cnt["act"]

    def v_norm(v):
        c.wait(v, "act", act2)
        v.reciprocal(out=b["lnr"][:], in_=b["lnv"][:])
        for dc in range(DC):
            v.tensor_tensor(out=b["t1"][:], in0=b["R"][:, dc * T:(dc + 1) * T],
                            in1=b["lnm"][:], op=ALU.subtract)
            ins = v.tensor_tensor(out=b[dst_name][:, dc, :], in0=b["t1"][:],
                                  in1=b["lnr"][:], op=ALU.mult)
        c.inc(ins, "dve", 1)

    c.sec("vector", v_norm)


def emit_p1_B(c):
    b = c.buf
    emit_ln(c, "xn")
    ln_done = c.cnt["dve"]

    NB = 16
    gld0 = c.cnt["g_ld"]
    pe_kv0 = c.cnt["pe"]

    def g_ldkv(g):
        for blk in range(NB):
            if blk >= 1:
                c.wait(g, "pe", pe_kv0 + blk * NTC)
            ins = g.dma_start(
                out=b["wkv"][:, blk % 2, 0, :, :],
                in_=b["in_aWkT"][:, blk * 128:(blk + 1) * 128].rearrange("(dc p) e -> p dc e", p=128),
                max_dma_last_dim=2048)
            c.inc(ins, "g_ld", 16)
            ins = g.dma_start(
                out=b["wkv"][:, blk % 2, 1, :, :],
                in_=b["in_aWvT"][:, blk * 128:(blk + 1) * 128].rearrange("(dc p) e -> p dc e", p=128),
                max_dma_last_dim=2048)
            c.inc(ins, "g_ld", 16)

    c.sec("gpsimd", g_ldkv)

    def t_kv(t):
        c.wait(t, "dve", ln_done)
        n = 0
        for blk in range(NB):
            c.wait(t, "g_ld", gld0 + 32 * (blk + 1))
            for tc in range(NTC):
                bank = b["pb"][n % 4]
                if n >= 4:
                    c.wait(t, "dve_kv", n - 3)
                for kv in range(2):
                    for dc in range(DC):
                        ins = t.matmul(bank[:, kv * 128:(kv + 1) * 128],
                                       b["xn"][:, dc, tc * 128:(tc + 1) * 128],
                                       b["wkv"][:, blk % 2, kv, dc, :],
                                       start=(dc == 0), stop=(dc == DC - 1))
                c.inc(ins, "pe", 1)
                n += 1

    c.sec("tensor", t_kv)

    def v_kv(v):
        n = 0
        for blk in range(NB):
            for tc in range(NTC):
                bank = b["pb"][n % 4]
                c.wait(v, "pe", pe_kv0 + n + 1)
                v.tensor_tensor(out=b["kv_scr"][:, n % 2, :], in0=bank[:, 0:256],
                                in1=b["bkv"][:, blk * 256:(blk + 1) * 256], op=ALU.add)
                ins = v.tensor_tensor(out=b["kv_tm"][:, tc, blk * 128:(blk + 1) * 128],
                                      in0=b["kv_scr"][:, n % 2, 0:128],
                                      in1=b["kv_scr"][:, n % 2, 128:256], op=ALU.mult)
                c.inc(ins, "dve_kv", 1)
                n += 1

    c.sec("vector", v_kv)
    dve_kv_b = c.cnt["dve_kv"]

    def t_tot(t):
        c.wait(t, "dve_kv", dve_kv_b)
        for col in range(4):
            for tc in range(NTC):
                ins = t.matmul(b["pb"][col][:1, :], b["ones1"][:],
                               b["kv_tm"][:, tc, col * 512:(col + 1) * 512],
                               start=(tc == 0), stop=(tc == NTC - 1))
        c.inc(ins, "pe", 1)

    c.sec("tensor", t_tot)
    pe_tot = c.cnt["pe"]

    def v_tot(v):
        c.wait(v, "pe", pe_tot)
        for col in range(4):
            ins = v.tensor_copy(out=b["tot_row"][:1, col * 512:(col + 1) * 512],
                                in_=b["pb"][col][:1, :])
        c.inc(ins, "dve_tot", 1)

    c.sec("vector", v_tot)
    dve_tot = c.cnt["dve_tot"]

    def g_cc1(g):
        c.wait(g, "dve_tot", dve_tot)
        ins = g.dma_start(out=b["cc1_in"].ap().rearrange("(one d) x -> one (d x)", one=1),
                          in_=b["tot_row"][:1, :])
        c.inc(ins, "g_ld", 16)
        c.wait(g, "g_ld")
        ins = g.collective_compute("AllGather", ALU.bypass,
                                   replica_groups=[list(range(NCORES))],
                                   ins=[b["cc1_in"].ap().opt()], outs=[b["cc1_out"].ap().opt()])
        c.inc(ins, "cc", 1)
        c.wait(g, "cc")
        ins = g.dma_start(out=b["tot_all"][:, :],
                          in_=b["cc1_out"].ap().rearrange("(r d) one -> r (d one)", r=NCORES))
        c.inc(ins, "g_ld", 16)

    c.sec("gpsimd", g_cc1)
    gld_cc1 = c.cnt["g_ld"]

    def t_offs(t):
        c.wait(t, "g_ld", gld_cc1)
        for dc in range(DC):
            ins = t.matmul(b["pb"][4][:, dc:dc + 1], b["tot_all"][:, dc * 128:(dc + 1) * 128],
                           b["m8"][:, :], start=True, stop=True)
        c.inc(ins, "pe", 1)

    c.sec("tensor", t_offs)
    pe_offs = c.cnt["pe"]

    def v_offs(v):
        c.wait(v, "pe", pe_offs)
        ins = v.tensor_copy(out=b["offs"][:, :], in_=b["pb"][4][:, :DC])
        c.inc(ins, "dve_offs", 1)

    c.sec("vector", v_offs)


def emit_p1_C(c):
    b = c.buf
    gld1 = c.cnt["g_ld"]
    pe_c0 = c.cnt["pe"]
    dve_kv_all = c.cnt["dve_kv"]

    def g_ldqg(g):
        for ec in range(DC):
            if ec == 0:
                c.wait(g, "dve_kv", dve_kv_all)
            if ec >= 1:
                c.wait(g, "pe", pe_c0 + 2 * ec)
            ins = g.dma_start(
                out=b["wqg"][:, ec % 2, 0, :, :],
                in_=b["in_aWqT"][:, ec * 128:(ec + 1) * 128].rearrange("(dc p) e -> p dc e", p=128),
                max_dma_last_dim=2048)
            c.inc(ins, "g_ld", 16)
            ins = g.dma_start(
                out=b["wqg"][:, ec % 2, 1, :, :],
                in_=b["in_aWgT"][:, ec * 128:(ec + 1) * 128].rearrange("(dc p) e -> p dc e", p=128),
                max_dma_last_dim=2048)
            c.inc(ins, "g_ld", 16)

    c.sec("gpsimd", g_ldqg)

    def t_qg(t):
        for ec in range(DC):
            c.wait(t, "g_ld", gld1 + 32 * (ec + 1))
            for j in range(2):
                if ec >= 1:
                    c.wait(t, "dve_c", 2 * (ec - 1) + j + 1)
                for dc in range(DC):
                    t.matmul(b["pb"][j][:], b["wqg"][:, ec % 2, 0, dc, :],
                             b["xn"][:, dc, j * 512:(j + 1) * 512],
                             start=(dc == 0), stop=(dc == DC - 1))
                for dc in range(DC):
                    t.matmul(b["pb"][2 + j][:], b["wqg"][:, ec % 2, 1, dc, :],
                             b["xn"][:, dc, j * 512:(j + 1) * 512],
                             start=(dc == 0), stop=(dc == DC - 1))
                first = True
                for tc in range(4 * j):
                    t.matmul(b["pb"][4 + j][:],
                             b["kv_tm"][:, tc, ec * 128:(ec + 1) * 128],
                             b["ones512"][:],
                             start=first, stop=False)
                    first = False
                for i, tc in enumerate(range(4 * j, 4 * j + 4)):
                    ins = t.matmul(b["pb"][4 + j][:],
                                   b["kv_tm"][:, tc, ec * 128:(ec + 1) * 128],
                                   b["UTmix"][:, tc % 4, :],
                                   start=first, stop=(i == 3))
                    first = False
                c.inc(ins, "pe", 1)

    c.sec("tensor", t_qg)

    def s_g(s):
        for ec in range(DC):
            for j in range(2):
                c.wait(s, "pe", pe_c0 + 2 * ec + j + 1)
                ins = s.activation(out=b["g_sb"][:, j * 512:(j + 1) * 512], in_=b["pb"][2 + j][:],
                                   func=AF.Sigmoid, bias=b["bg16"][:, ec:ec + 1], scale=1.0)
                c.inc(ins, "act_g", 1)

    c.sec("scalar", s_g)

    def v_chain(v):
        c.wait(v, "dve_offs", 1)
        for ec in range(DC):
            for j in range(2):
                c.wait(v, "pe", pe_c0 + 2 * ec + j + 1)
                c.wait(v, "act_g", 2 * ec + j + 1)
                sl = slice(j * 512, (j + 1) * 512)
                v.tensor_scalar(out=b["q_sb"][:, sl], in0=b["pb"][j][:],
                                scalar1=b["bq16"][:, ec:ec + 1], scalar2=None, op0=ALU.add)
                v.tensor_scalar(out=b["t1"][:, sl], in0=b["pb"][4 + j][:],
                                scalar1=b["offs"][:, ec:ec + 1], scalar2=None, op0=ALU.add)
                ins = v.tensor_tensor(out=b["t1"][:, sl], in0=b["t1"][:, sl], in1=b["q_sb"][:, sl],
                                      op=ALU.mult)
                c.inc(ins, "dve_c", 1)
                v.tensor_tensor(out=b["t1"][:, sl], in0=b["t1"][:, sl], in1=b["g_sb"][:, sl],
                                op=ALU.mult)
                ins = v.tensor_tensor(out=b["R"][:, ec * T + j * 512: ec * T + (j + 1) * 512],
                                      in0=b["R"][:, ec * T + j * 512: ec * T + (j + 1) * 512],
                                      in1=b["t1"][:, sl], op=ALU.add)
        c.inc(ins, "res", 1)

    c.sec("vector", v_chain)




def emit_p2(c):
    b = c.buf
    emit_ln(c, "xn")
    ln_done = c.cnt["dve"]
    gldw = c.cnt["g_ld"]
    pe_h0 = c.cnt["pe"]
    actg0 = c.cnt["act_g"]

    def g_ldw1(g):
        for ec in range(DC):
            if ec >= 1:
                c.wait(g, "pe", pe_h0 + 2 * ec)
            ins = g.dma_start(
                out=b["wf"][:, 0, :, :],
                in_=b["in_fW1"][:, ec * 128:(ec + 1) * 128].rearrange("(dc p) e -> p dc e", p=128),
                max_dma_last_dim=2048)
            c.inc(ins, "g_ld", 16)

    c.sec("gpsimd", g_ldw1)

    def t_h(t):
        c.wait(t, "dve", ln_done)
        for ec in range(DC):
            c.wait(t, "g_ld", gldw + 16 * (ec + 1))
            for j in range(2):
                if ec >= 1:
                    c.wait(t, "act_g", actg0 + 2 * (ec - 1) + j + 1)
                for dc in range(DC):
                    ins = t.matmul(b["pb"][j][:], b["wf"][:, 0, dc, :],
                                   b["xn"][:, dc, j * 512:(j + 1) * 512],
                                   start=(dc == 0), stop=(dc == DC - 1))
                c.inc(ins, "pe", 1)

    c.sec("tensor", t_h)

    def s_h(s):
        for ec in range(DC):
            for j in range(2):
                c.wait(s, "pe", pe_h0 + 2 * ec + j + 1)
                ins = s.activation(out=b["h_sb"][:, ec, j * 512:(j + 1) * 512], in_=b["pb"][j][:],
                                   func=AF.Relu, bias=b["fb1"][:, ec:ec + 1], scale=1.0)
                c.inc(ins, "act_g", 1)

    c.sec("scalar", s_h)
    actg_h = c.cnt["act_g"]

    pe_l0 = c.cnt["pe"]
    dve_l0 = c.cnt["dve_l"]

    def t_logits(t):
        c.wait(t, "act_g", actg_h)
        for tc in range(NTC):
            if tc >= 1:
                c.wait(t, "dve_l", dve_l0 + tc)
            for ec in range(DC):
                ins = t.matmul(b["pb"][2][:, 0:8], b["h_sb"][:, ec, tc * 128:(tc + 1) * 128],
                               b["w2sb"][:, ec, :], start=(ec == 0), stop=(ec == DC - 1))
            c.inc(ins, "pe", 1)

    c.sec("tensor", t_logits)

    def v_logits(v):
        for tc in range(NTC):
            c.wait(v, "pe", pe_l0 + tc + 1)
            ins = v.tensor_tensor(out=b["lg_tm"][:, tc, :], in0=b["pb"][2][:, 0:8],
                                  in1=b["b2b"][:, :], op=ALU.add)
            c.inc(ins, "dve_l", 1)

    c.sec("vector", v_logits)

    def v_smax1(v):
        for tc in range(NTC):
            ins = v.tensor_reduce(out=b["smx"][:, tc:tc + 1], in_=b["lg_tm"][:, tc, :],
                                  axis=AXX, op=ALU.max, negate=True)
            c.inc(ins, "dve_s", 1)

    c.sec("vector", v_smax1)
    dve_s1 = c.cnt["dve_s"]

    def s_sexp(s):
        c.wait(s, "dve_s", dve_s1)
        for tc in range(NTC):
            s.activation(out=b["alpha_tm"][:, tc, :], in_=b["lg_tm"][:, tc, :],
                         func=AF.Exp, bias=b["smx"][:, tc:tc + 1], scale=1.0,
                         accum_out=b["ssum"][:, tc:tc + 1])
            ins = s.activation(out=b["ssum"][:, tc:tc + 1], in_=b["ssum"][:, tc:tc + 1],
                               func=AF.Copy, scale=1.0)
            c.inc(ins, "act_s", 1)

    c.sec("scalar", s_sexp)
    act_s1 = c.cnt["act_s"]

    def v_salpha(v):
        c.wait(v, "act_s", act_s1)
        ins = v.reciprocal(out=b["srec"][:, :], in_=b["ssum"][:, :])
        c.inc(ins, "dve_s", 1)
        c.wait(v, "dve_s")
        for tc in range(NTC):
            ins = v.tensor_scalar(out=b["alpha_tm"][:, tc, :], in0=b["alpha_tm"][:, tc, :],
                                  scalar1=b["srec"][:, tc:tc + 1], scalar2=None, op0=ALU.mult)
        c.inc(ins, "dve_s", 1)

    c.sec("vector", v_salpha)
    dve_s2 = c.cnt["dve_s"]

    def t_atr(t):
        c.wait(t, "dve_s", dve_s2)
        for tc in range(NTC):
            ins = t.transpose(b["pbt"][0:8, tc * 128:(tc + 1) * 128],
                              b["alpha_tm"][:, tc, :], b["idnt"][:])
        c.inc(ins, "pe", 1)

    c.sec("tensor", t_atr)
    pe_atr = c.cnt["pe"]

    def v_afm(v):
        c.wait(v, "pe", pe_atr)
        ins = v.tensor_copy(out=b["alpha_fm"][0:8, :], in_=b["pbt"][0:8, :])
        c.inc(ins, "dve_s", 1)

    c.sec("vector", v_afm)
    dve_s3 = c.cnt["dve_s"]

    pe_x0 = c.cnt["pe"]
    dve_x0 = c.cnt["dve_x"]

    def t_vx(t):
        c.wait(t, "dve_s", dve_s3)
        for krc in range(4):
            for j in range(2):
                if krc >= 1:
                    c.wait(t, "dve_x", dve_x0 + 2 * (krc - 1) + j + 1)
                t.matmul(b["pb"][j][:], b["e8sb"][:, krc * 128:(krc + 1) * 128],
                         b["alpha_fm"][0:8, j * 512:(j + 1) * 512],
                         start=True, stop=True)
                for dc in range(DC):
                    ins = t.matmul(b["pb"][2 + j][:], b["vtsb"][:, dc, krc * 128:(krc + 1) * 128],
                                   b["xn"][:, dc, j * 512:(j + 1) * 512],
                                   start=(dc == 0), stop=(dc == DC - 1))
                c.inc(ins, "pe", 1)

    c.sec("tensor", t_vx)

    def v_wvx(v):
        n = 0
        for krc in range(4):
            for j in range(2):
                c.wait(v, "pe", pe_x0 + n + 1)
                v.tensor_copy(out=b["ab_sb"][:, :], in_=b["pb"][j][:])
                ins = v.tensor_tensor(out=b["wvx"][:, krc, j * 512:(j + 1) * 512],
                                      in0=b["pb"][2 + j][:], in1=b["ab_sb"][:, :], op=ALU.mult)
                c.inc(ins, "dve_x", 1)
                n += 1

    c.sec("vector", v_wvx)
    dve_x1 = c.cnt["dve_x"]

    pe_o0 = c.cnt["pe"]

    def t_out(t):
        c.wait(t, "dve_x", dve_x1)
        for dc in range(DC):
            for j in range(2):
                if dc >= 1:
                    c.wait(t, "dve_x", dve_x1 + 2 * (dc - 1) + j + 1)
                po = b["pb"][4 + j]
                for krc in range(4):
                    ins = t.matmul(po[:], b["utsb"][:, krc, dc * 128:(dc + 1) * 128],
                                   b["wvx"][:, krc, j * 512:(j + 1) * 512],
                                   start=(krc == 0), stop=(krc == 3))
                c.inc(ins, "pe", 1)

    c.sec("tensor", t_out)

    def v_rout(v):
        n = 0
        for dc in range(DC):
            for j in range(2):
                c.wait(v, "pe", pe_o0 + n + 1)
                sl = slice(dc * T + j * 512, dc * T + (j + 1) * 512)
                ins = v.tensor_tensor(out=b["R"][:, sl], in0=b["R"][:, sl],
                                      in1=b["pb"][4 + j][:], op=ALU.add)
                n += 1
                if n < 2 * DC:
                    c.inc(ins, "dve_x", 1)
        c.inc(ins, "res", 1)

    c.sec("vector", v_rout)



def emit_p3(c):
    b = c.buf
    emit_ln(c, "xn")
    ln_done = c.cnt["dve"]
    res_p2 = c.cnt["res"]

    def g_c3(g):
        c.wait(g, "res", res_p2)
        for dst, srcn in (("qks", "in_qks"), ("qko", "in_qko"), ("bm1", "in_bm"),
                          ("bs1", "in_bs"), ("dyn", "in_dyn")):
            ins = g.dma_start(out=b[dst][:], in_=b[srcn])
            c.inc(ins, "g_ld", 16)
        ins = g.dma_start(out=b["wsm"][:], in_=b["in_Wsm"].rearrange("(dc p) e -> p dc e", p=128))
        c.inc(ins, "g_ld", 16)

    c.sec("gpsimd", g_c3)
    gld_c3 = c.cnt["g_ld"]

    def v_c3(v):
        c.wait(v, "g_ld", gld_c3)
        isq = 1.0 / np.sqrt(128.0)
        v.tensor_scalar(out=b["s0e"][:, :], in0=b["qks"][:, :, 0], scalar1=isq, scalar2=None, op0=ALU.mult)
        ins = v.tensor_scalar(out=b["o0e"][:, :], in0=b["qko"][:, :, 0], scalar1=isq, scalar2=None, op0=ALU.mult)
        c.inc(ins, "dve_q", 1)

    c.sec("vector", v_c3)

    if P3_PART < 1:
        return
    pe_q0 = c.cnt["pe"]
    pen0 = c.cnt["pe_n"]
    dveq0 = c.cnt["dve_q"]
    actq0 = c.cnt["act_q"]
    actsq0 = c.cnt["act_sq"]
    actn0 = c.cnt["act_n"]
    dven0 = c.cnt["dve_n"]
    st1_0 = c.cnt["s_st"]
    st2_0 = c.cnt["s_st2"]
    st3_0 = c.cnt["s_st3"]

    def g_ldq(g):
        for eco in range(48):
            if eco >= 1:
                c.wait(g, "pe", pe_q0 + 2 * eco)
            ins = g.dma_start(
                out=b["wq3"][:, eco % 2, :, :],
                in_=b["in_Wqkv"][:, eco * 128:(eco + 1) * 128].rearrange("(dc p) e -> p dc e", p=128),
                max_dma_last_dim=2048)
            c.inc(ins, "g_ld", 16)

    c.sec("gpsimd", g_ldq)

    def t_q(t):
        c.wait(t, "dve", ln_done)
        for eco in range(48):
            c.wait(t, "g_ld", gld_c3 + 16 * (eco + 1))
            for j in range(2):
                if eco >= 1:
                    c.wait(t, "dve_q", dveq0 + 2 * (eco - 1) + j + 1)
                    if P3X & 2:
                        c.wait(t, "act_q", actq0 + 2 * (eco - 1) + j + 1)
                for dc in range(DC):
                    ins = t.matmul(b["pb"][j][:], b["wq3"][:, eco % 2, dc, :],
                                   b["xn"][:, dc, j * 512:(j + 1) * 512],
                                   start=(dc == 0), stop=(dc == DC - 1))
                c.inc(ins, "pe", 1)
            if (P3X & 4) and eco < 32 and eco % 4 == 3:
                h4 = eco // 4
                c.wait(t, "act_sq", actsq0 + (h4 + 1) * 4)
                if h4 >= 1:
                    c.wait(t, "act_n", actn0 + h4)
                for j in range(2):
                    for ch in range(4):
                        ins = t.matmul(b["pb"][2 + j][:], b["ones128"][:],
                                       b["sq2_scr"][:, ch, j * 512:(j + 1) * 512],
                                       start=(ch == 0), stop=(ch == 3))
                    c.inc(ins, "pe_n", 1)

    c.sec("tensor", t_q)

    def v_all(v):
        for eco in range(48):
            w = eco % 2
            if (P3X & 1) and eco >= 2:
                nd = eco + max(0, eco - 16)
                c.wait(v, "s_st", st1_0 + 16 * nd)
            for j in range(2):
                c.wait(v, "pe", pe_q0 + 2 * eco + j + 1)
                sl = slice(j * 512, (j + 1) * 512)
                if eco < 16:
                    ins = v.tensor_scalar(out=b["wout"][:, w, sl], in0=b["pb"][j][:],
                                          scalar1=b["s0e"][:, eco:eco + 1],
                                          scalar2=b["o0e"][:, eco:eco + 1],
                                          op0=ALU.mult, op1=ALU.add)
                elif eco < 32:
                    ins = v.tensor_scalar(out=b["wout"][:, w, sl], in0=b["pb"][j][:],
                                          scalar1=b["qks"][:, eco - 16, 1:2],
                                          scalar2=b["qko"][:, eco - 16, 1:2],
                                          op0=ALU.mult, op1=ALU.add)
                else:
                    ins = v.tensor_copy(out=b["wout"][:, w, sl], in_=b["pb"][j][:])
                c.inc(ins, "dve_q", 1)
            if (P3X & 4) and eco < 32 and eco % 4 == 3:
                h4 = eco // 4
                c.wait(v, "act_n", actn0 + h4 + 1)
                v.tensor_scalar(out=b["nrm"][:, :], in0=b["nrm"][:, :], scalar1=1e-5,
                                scalar2=None, op0=ALU.add)
                ins = v.reciprocal(out=b["rstd"][:, :], in_=b["nrm"][:, :])
                c.inc(ins, "dve_n0", 1)
                c.wait(v, "dve_n0")
                for ch in range(4):
                    ins = v.tensor_tensor(out=b["hbuf"][:, h4 % 2, ch, :],
                                          in0=b["hbuf"][:, h4 % 2, ch, :],
                                          in1=b["rstd"][:, :], op=ALU.mult)
                    c.inc(ins, "dve_n", 1)

    c.sec("vector", v_all)

    def s_q(s):
        if not (P3X & 2):
            return
        for eco in range(48):
            if eco < 32:
                hb = (eco // 4) % 2
                if (P3X & 4) and eco >= 8:
                    c.wait(s, "s_st2", st2_0 + 16 * 4 * (eco // 4))
                if (P3X & 4) and eco >= 4:
                    c.wait(s, "pe_n", pen0 + 2 * (eco // 4))
                for j in range(2):
                    c.wait(s, "pe", pe_q0 + 2 * eco + j + 1)
                    c.wait(s, "dve_q", dveq0 + 2 * eco + j + 1)
                    sl = slice(j * 512, (j + 1) * 512)
                    ins = s.activation(out=b["hbuf"][:, hb, eco % 4, sl], in_=b["pb"][j][:],
                                       func=AF.Silu, scale=1.0)
                    c.inc(ins, "act_q", 1)
                if P3X & 4:
                    ins = s.activation(out=b["sq2_scr"][:, eco % 4, :], in_=b["hbuf"][:, hb, eco % 4, :],
                                       func=AF.Square, scale=1.0)
                    c.inc(ins, "act_sq", 1)
                    if eco % 4 == 3:
                        h4 = eco // 4
                        c.wait(s, "pe_n", pen0 + 2 * (h4 + 1))
                        for j in range(2):
                            ins = s.activation(out=b["nrm"][:, j * 512:(j + 1) * 512],
                                               in_=b["pb"][2 + j][:], func=AF.Sqrt, scale=1.0)
                        c.inc(ins, "act_n", 1)
            else:
                if (P3X & 2) and not (P3X & 8) and eco >= 34:
                    c.wait(s, "s_st3", st3_0 + 16 * (eco - 32))
                for j in range(2):
                    c.wait(s, "pe", pe_q0 + 2 * eco + j + 1)
                    c.wait(s, "dve_q", dveq0 + 2 * eco + j + 1)
                    sl = slice(j * 512, (j + 1) * 512)
                    ins = s.activation(out=b["vout"][:, eco % 2, sl], in_=b["pb"][j][:],
                                       func=AF.Silu, scale=1.0)
                    c.inc(ins, "act_q", 1)

    c.sec("scalar", s_q)

    def sy_all(sy):
        for eco in range(48):
            w = eco % 2
            if P3X & 1:
                c.wait(sy, "dve_q", dveq0 + 2 * (eco + 1))
                ins = sy.dma_start(out=b["wloc"].ap()[eco * 128:(eco + 1) * 128, :],
                                   in_=b["wout"][:, w, :])
                c.inc(ins, "s_st", 16)
                if eco >= 16:
                    ins = sy.dma_start(out=b["cc2_in"].ap()[(eco - 16) * 128:(eco - 15) * 128, :],
                                       in_=b["wout"][:, w, :])
                    c.inc(ins, "s_st", 16)
            if (P3X & 2) and not (P3X & 8) and eco >= 32:
                c.wait(sy, "act_q", actq0 + 2 * (eco + 1))
                h = (eco - 32) // 4
                ch = eco % 4
                base = h * 1536 + 1024 + ch * 128
                ins = sy.dma_start(out=b["cc3_in"].ap()[base:base + 128, :],
                                   in_=b["vout"][:, eco % 2, :])
                c.inc(ins, "s_st3", 16)
            if P3X & 4 and eco < 32 and eco % 4 == 3:
                h4 = eco // 4
                qk = h4 // 4
                h = h4 % 4
                for ch in range(4):
                    c.wait(sy, "dve_n", dven0 + h4 * 4 + ch + 1)
                    base = h * 1536 + qk * 512 + ch * 128
                    ins = sy.dma_start(out=b["cc3_in"].ap()[base:base + 128, :],
                                       in_=b["hbuf"][:, h4 % 2, ch, :])
                    c.inc(ins, "s_st2", 16)

    c.sec("sync", sy_all)

    if P3_PART < 2:
        return
    pe_sm0 = c.cnt["pe"]
    actsm0 = c.cnt["act_sm"]
    dveq_all = c.cnt["dve_q"] + 2 * 48
    actq_all = actq0 + 2 * 48

    def t_sm(t):
        for w in range(3):
            col0 = (0, 12, 16)[w]
            m = (12, 4, 4)[w]
            if w >= 1:
                c.wait(t, "act_sm", actsm0 + 2 * w)
            for j in range(2):
                for dc in range(DC):
                    ins = t.matmul(b["pb"][4 + j][:m, :], b["wsm"][:, dc, col0:col0 + m],
                                   b["xn"][:, dc, j * 512:(j + 1) * 512],
                                   start=(dc == 0), stop=(dc == DC - 1))
                c.inc(ins, "pe", 1)

    c.sec("tensor", t_sm)

    def s_sm(s):
        for j in range(2):
            c.wait(s, "pe", pe_sm0 + j + 1)
            sl = slice(j * 512, (j + 1) * 512)
            s.activation(out=b["spa"][:12, sl], in_=b["pb"][4 + j][:12, :], func=AF.Abs,
                         scale=1.0, bias=b["c_lrb"][:12, :])
            s.activation(out=b["spa"][:12, sl], in_=b["spa"][:12, sl], func=AF.Exp, scale=-1.0)
            s.activation(out=b["spa"][:12, sl], in_=b["spa"][:12, sl], func=AF.Ln, scale=1.0,
                         bias=b["c_one"][:12, :])
            ins = s.activation(out=b["spb"][:12, sl], in_=b["pb"][4 + j][:12, :], func=AF.Relu,
                               scale=1.0, bias=b["c_lrb"][:12, :])
            c.inc(ins, "act_sm", 1)
        for j in range(2):
            c.wait(s, "pe", pe_sm0 + 2 + j + 1)
            sl = slice(j * 512, (j + 1) * 512)
            ins = s.activation(out=b["mom_sb"][:4, sl], in_=b["pb"][4 + j][:4, :],
                               func=AF.Sigmoid, scale=1.0, bias=b["bm1"][:4, :])
            c.inc(ins, "act_sm", 1)
        for j in range(2):
            c.wait(s, "pe", pe_sm0 + 4 + j + 1)
            sl = slice(j * 512, (j + 1) * 512)
            ins = s.activation(out=b["scale_sb"][:4, sl], in_=b["pb"][4 + j][:4, :],
                               func=AF.Silu, scale=1.0, bias=b["bs1"][:4, :])
            c.inc(ins, "act_sm", 1)

    c.sec("scalar", s_sm)
    act_sm1 = c.cnt["act_sm"]

    def v_sm(v):
        c.wait(v, "act_sm", act_sm1)
        ins = v.tensor_tensor(out=b["lr_sb"][:12, :], in0=b["spa"][:12, :], in1=b["spb"][:12, :],
                              op=ALU.add)
        c.inc(ins, "dve_sm", 1)

    c.sec("vector", v_sm)
    dve_sm1 = c.cnt["dve_sm"]

    def sy_sm(sy):
        c.wait(sy, "dve_sm", dve_sm1)
        for h in range(4):
            ins = sy.dma_start(out=b["cc3b_in"].ap()[4 * h:4 * h + 3, :],
                               in_=b["lr_sb"][3 * h:3 * h + 3, :])
            c.inc(ins, "s_st2", 16)
            ins = sy.dma_start(out=b["cc3b_in"].ap()[4 * h + 3:4 * h + 4, :],
                               in_=b["mom_sb"][h:h + 1, :])
            c.inc(ins, "s_st2", 16)
        ins = sy.dma_start(out=b["scdram"].ap(), in_=b["scale_sb"][0:4, :])
        c.inc(ins, "s_st2", 16)

    c.sec("sync", sy_sm)

    st_all = (c.cnt["s_st"], c.cnt["s_st2"], c.cnt["s_st3"])

    def g_cc23(g):
        c.wait(g, "s_st", st_all[0])
        c.wait(g, "s_st2", st_all[1])
        c.wait(g, "s_st3", st_all[2])
        if P3_PART >= 3:
            ins = g.collective_compute("AllGather", ALU.bypass,
                                       replica_groups=[list(range(NCORES))],
                                       ins=[b["cc2_in"].ap().opt()], outs=[b["cc2_out"].ap().opt()])
            c.inc(ins, "cc", 1)
        c.cc_c2 = c.cnt["cc"]
        if P3_PART >= 4:
            ins = g.collective_compute("AllGather", ALU.bypass,
                                       replica_groups=[list(range(NCORES))],
                                       ins=[b["cc3_in"].ap().opt()], outs=[b["cc3_out"].ap().opt()])
            c.inc(ins, "cc", 1)
            ins = g.collective_compute("AllGather", ALU.bypass,
                                       replica_groups=[list(range(NCORES))],
                                       ins=[b["cc3b_in"].ap().opt()], outs=[b["cc3b_out"].ap().opt()])
            c.inc(ins, "cc", 1)

    c.sec("gpsimd", g_cc23)



def emit_p4(c):
    b = c.buf
    cc_need = c.cnt["cc"]
    st0 = c.cnt["s_st"]

    sp_done4 = c.cnt["s_sp"]
    pe_all4 = c.cnt["pe"]
    gld_all4 = c.cnt["g_ld"]

    def g_c4(g):
        c.wait(g, "s_sp", sp_done4)
        c.wait(g, "pe", pe_all4)
        c.wait(g, "g_ld", gld_all4)
        ins = g.dma_start(out=b["wm"][:], in_=b["in_wm"].rearrange("i p k -> p i k"),
                          max_dma_last_dim=8192)
        c.inc(ins, "g_ld", 16)
        ins = g.dma_start(out=b["idn4"][:], in_=b["in_idn"])
        c.inc(ins, "g_ld", 16)
        c.wait(g, "cc", c.cc_c2)
        with g.register("rpv4") as rpv:
            g.load(rpv, b["dyn"][1:2, 0:1])
            ins = g.dma_start(out=b["prevloc"].ap(),
                              in_=b["cc2_out"].ap()[bass.ds(g.bass.snap(rpv), 2 * D), :])
            c.inc(ins, "g_ld", 16)

    c.sec("gpsimd", g_c4)
    gld_m = c.cnt["g_ld"]

    def g_cp5(g):
        c.wait(g, "cc", cc_need)
        with g.register("rh5") as rh:
            g.load(rh, b["dyn"][0:1, 0:1])
            ins = g.dma_start(
                out=b["scanloc"].ap().rearrange("(ch r) t -> ch r t", r=1536),
                in_=b["cc3_out"].ap().rearrange("(ch r) t -> ch r t", r=6144)
                [:, bass.ds(g.bass.snap(rh), 1536), :])
            c.inc(ins, "g_cp", 16)
        with g.register("rb5") as rb5:
            g.load(rb5, b["dyn"][3:4, 0:1])
            ins = g.dma_start(
                out=b["scanlocb"].ap().rearrange("(ch r) t -> ch r t", r=4),
                in_=b["cc3b_out"].ap().rearrange("(ch r) t -> ch r t", r=16)
                [:, bass.ds(g.bass.snap(rb5), 4), :])
            c.inc(ins, "g_cp", 16)

    c.sec("gpsimd", g_cp5)

    m0 = c.cnt["dve_w"]
    a0 = c.cnt["act_w"]
    psc0 = c.cnt["pe_sc"]
    ptr0 = c.cnt["pe_tr"]
    ppv0 = c.cnt["pe_pv"]
    pav0 = c.cnt["pe_av"]
    stw0 = c.cnt["s_stw"]

    def base(n):
        return m0 + (n // 8) * 57 + 1 + (n % 8) * 7

    def sy_4(sy):
        c.wait(sy, "cc", c.cc_c2)
        c.wait(sy, "g_ld", gld_m)
        if True:
            for ah in range(16):
                if ah >= 1:
                    c.wait(sy, "pe_av", pav0 + ah)
                if ah >= 2:
                    c.wait(sy, "pe_sc", psc0 + (ah - 1) * 8)
                hb = ah % 2
                ins = sy.dma_start(out=b["aqh"][:, hb, :],
                                   in_=b["wloc"].ap()[ah * 128:(ah + 1) * 128, :])
                c.inc(ins, "s_st", 16)
                ins = sy.dma_start(out=b["kwin"][:, hb, 1024:2048],
                                   in_=b["wloc"].ap()[2048 + ah * 128:2048 + (ah + 1) * 128, :])
                c.inc(ins, "s_st", 16)
                ins = sy.dma_start(out=b["avwin"][:, hb, 1024:2048],
                                   in_=b["wloc"].ap()[4096 + ah * 128:4096 + (ah + 1) * 128, :])
                c.inc(ins, "s_st", 16)
                ins = sy.dma_start(out=b["kwin"][:, hb, 0:1024],
                                   in_=b["prevloc"].ap()[ah * 128:(ah + 1) * 128, :])
                c.inc(ins, "s_st", 16)
                ins = sy.dma_start(out=b["avwin"][:, hb, 0:1024],
                                   in_=b["prevloc"].ap()[2048 + ah * 128:2048 + (ah + 1) * 128, :])
                c.inc(ins, "s_st", 16)
                if ah >= 2:
                    prev = ah - 2
                    c.wait(sy, "dve_w", base(prev * 8 + 7) + 7)
                    ins = sy.dma_start(out=b["attn_loc"].ap()[prev * 128:(prev + 1) * 128, :],
                                       in_=b["wo_sb"][:, prev % 2, :])
                    c.inc(ins, "s_stw", 16)
            for prev in (14, 15):
                c.wait(sy, "dve_w", base(prev * 8 + 7) + 7)
                ins = sy.dma_start(out=b["attn_loc"].ap()[prev * 128:(prev + 1) * 128, :],
                                   in_=b["wo_sb"][:, prev % 2, :])
                c.inc(ins, "s_stw", 16)

    c.sec("sync", sy_4)

    def t_4(t):
        for ah in range(16):
            hb = ah % 2
            n0h = ah * 8
            c.wait(t, "s_st", st0 + 16 * 5 * (ah + 1))
            if ah >= 1:
                c.wait(t, "dve_w", base(n0h - 1) + 6)
            for kc in range(16):
                ins = t.transpose(b["pbt2"][:, kc * 128:(kc + 1) * 128],
                                  b["avwin"][:, hb, kc * 128:(kc + 1) * 128], b["idn4"][:])
            c.inc(ins, "pe_av", 1)
            for i in range(NTC):
                n = n0h + i
                if n >= 1:
                    c.wait(t, "dve_w", base(n - 1) + 1)
                ins = t.matmul(b["pb"][0][:], b["aqh"][:, hb, i * 128:(i + 1) * 128],
                               b["kwin"][:, hb, 128 * i:128 * i + 512], start=True, stop=True)
                ins = t.matmul(b["pb"][1][:], b["aqh"][:, hb, i * 128:(i + 1) * 128],
                               b["kwin"][:, hb, 128 * i + 512:128 * i + 1024], start=True, stop=True)
                ins = t.matmul(b["pb"][2][:, 0:128], b["aqh"][:, hb, i * 128:(i + 1) * 128],
                               b["kwin"][:, hb, 128 * i + 1024:128 * i + 1152], start=True, stop=True)
                c.inc(ins, "pe_sc", 1)
                c.wait(t, "dve_w", base(n) + 5)
                for kc in range(9):
                    ins = t.transpose(b["pbt2"][:, kc * 128:(kc + 1) * 128],
                                      b["probs_n"][:, kc * 128:(kc + 1) * 128], b["idn4"][:])
                c.inc(ins, "pe_tr", 1)
                c.wait(t, "dve_w", base(n) + 6)
                for kc in range(9):
                    ins = t.matmul(b["pb"][3][:, 0:128],
                                   b["avtm"][:, (i + kc) * 128:(i + kc + 1) * 128],
                                   b["probsT"][:, kc * 128:(kc + 1) * 128],
                                   start=(kc == 0), stop=(kc == 8))
                c.inc(ins, "pe_pv", 1)

    c.sec("tensor", t_4)

    def v_4(v):
        for ah in range(16):
            hb = ah % 2
            c.wait(v, "pe_av", pav0 + ah + 1)
            v.tensor_copy(out=b["avtm"][:, 0:1024], in_=b["pbt2"][:, 0:1024])
            ins = v.tensor_copy(out=b["avtm"][:, 1024:2048], in_=b["pbt2"][:, 1024:2048])
            c.inc(ins, "dve_w", 1)
            for i in range(NTC):
                n = ah * 8 + i
                c.wait(v, "pe_sc", psc0 + n + 1)
                if n >= 1:
                    c.wait(v, "act_w", a0 + n)
                if i == 0 and ah >= 1:
                    c.wait(v, "s_stw", stw0 + 16 * max(0, ah - 1))
                v.tensor_tensor(out=b["scores"][:, 0:512], in0=b["pb"][0][:],
                                in1=b["wm"][:, i, 0:512], op=ALU.add)
                v.tensor_tensor(out=b["scores"][:, 512:1024], in0=b["pb"][1][:],
                                in1=b["wm"][:, i, 512:1024], op=ALU.add)
                ins = v.tensor_tensor(out=b["scores"][:, 1024:1152], in0=b["pb"][2][:, 0:128],
                                      in1=b["wm"][:, i, 1024:1152], op=ALU.add)
                c.inc(ins, "dve_w", 1)
                ins = v.tensor_reduce(out=b["wmax"][:, :], in_=b["scores"][:, :], axis=AXX,
                                      op=ALU.max)
                c.inc(ins, "dve_w", 1)
                c.wait(v, "dve_w")
                ins = v.tensor_scalar(out=b["wmaxn"][:, :], in0=b["wmax"][:, :], scalar1=-1.0,
                                      scalar2=None, op0=ALU.mult)
                c.inc(ins, "dve_w", 1)
                c.wait(v, "act_w", a0 + n + 1)
                ins = v.reciprocal(out=b["wrs"][:, :], in_=b["wsum2"][:, :])
                c.inc(ins, "dve_w", 1)
                c.wait(v, "dve_w")
                ins = v.tensor_scalar(out=b["probs_n"][:, :], in0=b["probs"][:, :],
                                      scalar1=b["wrs"][:, :], scalar2=None, op0=ALU.mult)
                c.inc(ins, "dve_w", 1)
                c.wait(v, "pe_tr", ptr0 + n + 1)
                v.tensor_copy(out=b["probsT"][:, 0:1024], in_=b["pbt2"][:, 0:1024])
                ins = v.tensor_copy(out=b["probsT"][:, 1024:1152], in_=b["pbt2"][:, 1024:1152])
                c.inc(ins, "dve_w", 1)
                c.wait(v, "pe_pv", ppv0 + n + 1)
                ins = v.tensor_copy(out=b["wo_sb"][:, hb, i * 128:(i + 1) * 128],
                                    in_=b["pb"][3][:, 0:128])
                c.inc(ins, "dve_w", 1)

    c.sec("vector", v_4)

    def s_4(s):
        for ah in range(16):
            for i in range(NTC):
                n = ah * 8 + i
                c.wait(s, "dve_w", base(n) + 3)
                if n >= 1:
                    c.wait(s, "dve_w", base(n - 1) + 5)
                s.activation(out=b["probs"][:, :], in_=b["scores"][:, :], func=AF.Exp,
                             scale=1.0, bias=b["wmaxn"][:, :], accum_out=b["wsum"][:, :])
                ins = s.activation(out=b["wsum2"][:, :], in_=b["wsum"][:, :], func=AF.Copy,
                                   scale=1.0)
                c.inc(ins, "act_w", 1)

    c.sec("scalar", s_4)



def emit_p5(c):
    b = c.buf
    NS_COEFFS = [(4.0848, -6.8946, 2.927), (3.9505, -6.3029, 2.6377),
                 (3.7418, -5.5913, 2.3037), (2.8769, -3.1427, 1.2046),
                 (2.8366, -3.0525, 1.2012)]
    cc_need = c.cnt["cc"]

    b["ki_tm"] = b["qiT"][:].rearrange("p a (x t) -> p (a x) t", x=2)
    b["vi_tm"] = b["kiT"][:].rearrange("p a (x t) -> p (a x) t", x=2)
    b["dgl0"] = b["gba"]
    b["dhl2"] = b["dsil"]

    def g_5(g):
        for w in range(3):
            ins = g.dma_start(out=b["wT"][w][:],
                              in_=b["in_wT"][w * 512:(w + 1) * 512, :].rearrange("(dc p) h -> p dc h", p=128))
            c.inc(ins, "g_ld", 16)
        ins = g.dma_start(out=b["w1n"][:],
                          in_=b["in_w1n"].rearrange("(dc p) h -> p dc h", p=128))
        c.inc(ins, "g_ld", 16)

    c.sec("gpsimd", g_5)
    gld_5 = c.cnt["g_ld"]

    def COLSUM(t, bank, waitv):
        c.wait(t, "act5", waitv)
        for a in range(4):
            ins = t.matmul(b["pb"][bank][:], b["ones128"][:], b["sq5"][:, a, :],
                           start=(a == 0), stop=(a == 3))
        c.inc(ins, "pe5", 1)

    a0 = c.cnt["act5"]
    p0 = c.cnt["pe5"]
    d0 = c.cnt["dve5"]

    def s_ni2(s):
        c.wait(s, "g_ld", gld_5)
        for w in range(3):
            if w > 0:
                c.wait(s, "pe5", p0 + w)
            ins = s.activation(out=b["sq5"][:].rearrange("p a h -> p (a h)"),
                               in_=b["wT"][w][:].rearrange("p a h -> p (a h)"),
                               func=AF.Square, scale=1.0)
            c.inc(ins, "act5", 1)
        c.wait(s, "pe5", p0 + 3)
        ins = s.activation(out=b["sq5"][:].rearrange("p a h -> p (a h)"),
                           in_=b["w1n"][:].rearrange("p a h -> p (a h)"),
                           func=AF.Square, scale=1.0)
        c.inc(ins, "act5", 1)

    c.sec("scalar", s_ni2)

    def t_ni(t):
        for w in range(3):
            COLSUM(t, w, a0 + w + 1)
        c.wait(t, "act5", a0 + 4)

    c.sec("tensor", t_ni)

    def v_ni(v):
        c.wait(v, "pe5", p0 + 3)
        c.wait(v, "act5", a0 + 4)
        for w in range(3):
            ins = v.tensor_copy(out=b["nwb"][w][:], in_=b["pb"][w][:])
            c.inc(ins, "dve5", 1)
        for a in range(4):
            ins = v.reduce_sum(out=b["n1r"][:, a:a + 1], in_=b["sq5"][:, a, :], axis=AXX)
        c.inc(ins, "dve5", 1)

    c.sec("vector", v_ni)
    d_ni = c.cnt["dve5"]

    def s_ni3(s):
        c.wait(s, "dve5", d_ni)
        for w in range(3):
            s.activation(out=b["nwb"][w][:], in_=b["nwb"][w][:], func=AF.Sqrt, scale=1.0)
        ins = s.activation(out=b["n1r"][:, 0:4], in_=b["n1r"][:, 0:4], func=AF.Sqrt, scale=1.0)
        c.inc(ins, "act5", 1)

    c.sec("scalar", s_ni3)

    d_gate_in = c.cnt["dve5"]
    for ch in range(8):
        base_r = ch * 1536

        d_prev = c.cnt["dve5"]
        a_prev = c.cnt["act5"]

        def g_in(g, ch=ch, base_r=base_r, d_prev=d_gate_in):
            if ch == 0:
                c.wait(g, "g_cp", 32)
            c.wait(g, "dve5", d_prev)
            for wi, dst in ((0, "qiT"), (1, "kiT"), (2, "vi")):
                ins = g.dma_start(out=b[dst][:],
                                  in_=b["scanloc"].ap()[base_r + wi * 512: base_r + (wi + 1) * 512, :]
                                  .rearrange("(dc p) t -> p dc t", p=128))
                c.inc(ins, "g_ld", 16)
            ins = g.dma_start(out=b["scb"][:], in_=b["scanlocb"].ap()[ch * 4:(ch + 1) * 4, :])
            c.inc(ins, "g_ld", 16)

        c.sec("gpsimd", g_in)
        g_in_done = c.cnt["g_ld"]

        def s_cast(s, ch=ch, g_in_done=g_in_done, d_prev=d_prev):
            c.wait(s, "g_ld", g_in_done)
            c.wait(s, "dve5", d_prev)
            for w in range(3):
                s.activation(out=b["wTb"][w][:].rearrange("p a h -> p (a h)"),
                             in_=b["wT"][w][:].rearrange("p a h -> p (a h)"),
                             func=AF.Copy, scale=1.0)
            ins = s.activation(out=b["w1nb"][:].rearrange("p a h -> p (a h)"),
                               in_=b["w1n"][:].rearrange("p a h -> p (a h)"),
                               func=AF.Copy, scale=1.0)
            c.inc(ins, "act5", 1)

        c.sec("scalar", s_cast)
        a_cast = c.cnt["act5"]

        p_lm0 = c.cnt["pe5"]

        def t_lm(t, ch=ch, g_in_done=g_in_done, d_prev=d_prev):
            c.wait(t, "g_ld", g_in_done)
            c.wait(t, "dve5", d_prev)
            for tc in range(NTC):
                ins = t.transpose(b["pb"][4][:, tc * 4:(tc + 1) * 4],
                                  b["scb"][0:4, tc * 128:(tc + 1) * 128], b["idn5f"][0:4, 0:4])
            c.inc(ins, "pe5", 1)

        c.sec("tensor", t_lm)
        p_lm = c.cnt["pe5"]

        def v_lm(v, ch=ch, p_lm=p_lm):
            c.wait(v, "pe5", p_lm)
            ins = v.tensor_copy(out=b["lmt"][:].rearrange("p a w -> p (a w)"),
                                in_=b["pb"][4][:, 0:32])
            c.inc(ins, "dve5", 1)
            ins = v.reduce_sum(out=b["mmr"][:, 0:1], in_=b["lmt"][:, :, 3], axis=AXX)
            c.inc(ins, "dve5", 1)

        c.sec("vector", v_lm)
        d_lm = c.cnt["dve5"]
        p_mm0 = c.cnt["pe5"]

        def t_mm(t, ch=ch, d_lm=d_lm):
            c.wait(t, "dve5", d_lm)
            t.matmul(b["pb"][5][0:1, 0:1], b["mmr"][:, 0:1], b["c_one"][:, 0:1],
                     start=True, stop=True)
            ins = t.matmul(b["pb"][5][0:1, 4:5], b["mmr"][:, 0:1], b["c_one"][:, 0:1],
                          start=True, stop=True)
            c.inc(ins, "pe5", 1)

        c.sec("tensor", t_mm)
        p_mm = c.cnt["pe5"]

        def v_mm(v, ch=ch, p_mm=p_mm):
            c.wait(v, "pe5", p_mm)
            ins = v.tensor_scalar(out=b["mm1"][0:1, 0:1], in0=b["pb"][5][0:1, 0:1],
                                  scalar1=1.0 / 1024.0, scalar2=None, op0=ALU.mult)
            c.inc(ins, "dve5", 1)

        c.sec("vector", v_mm)
        d_mm = c.cnt["dve5"]

        def t_mmb(t, ch=ch, d_mm=d_mm):
            c.wait(t, "dve5", d_mm)
            ins = t.matmul(b["pb"][5][0:128, 8:9], b["ones1w"][0:1, 0:128], b["mm1"][0:1, 0:1],
                           start=True, stop=True)
            c.inc(ins, "pe5", 1)

        c.sec("tensor", t_mmb)
        p_mmb = c.cnt["pe5"]

        def v_mmb(v, ch=ch, p_mmb=p_mmb):
            c.wait(v, "pe5", p_mmb)
            ins = v.tensor_copy(out=b["mmb"][:, 0:1], in_=b["pb"][0:0] if False else b["pb"][5][0:128, 8:9])
            c.inc(ins, "dve5", 1)

        c.sec("vector", v_mmb)

        p_f0 = c.cnt["pe5"]
        a_f0 = c.cnt["act5"]
        d_f0 = c.cnt["dve5"]
        for hc in range(4):
            for j in range(2):
                idx = hc * 2 + j

                def t_g(t, hc=hc, j=j, idx=idx, ch=ch):
                    if idx == 0:
                        c.wait(t, "act5", a_cast)
                    if idx >= 2:
                        c.wait(t, "act5", a_f0 + idx - 1)
                        c.wait(t, "dve5", d_f0 + idx - 1)
                    for dc in range(4):
                        t.matmul(b["pb"][j][:], b["wTb"][2][:, dc, hc * 128:(hc + 1) * 128],
                                 b["qiT"][:, dc, j * 512:(j + 1) * 512],
                                 start=(dc == 0), stop=(dc == 3))
                    for dc in range(4):
                        ins = t.matmul(b["pb"][2 + j][:], b["wTb"][0][:, dc, hc * 128:(hc + 1) * 128],
                                       b["qiT"][:, dc, j * 512:(j + 1) * 512],
                                       start=(dc == 0), stop=(dc == 3))
                    c.inc(ins, "pe5", 1)

                def s_g(s, j=j, idx=idx, ch=ch):
                    c.wait(s, "pe5", p_f0 + idx + 1)
                    ins = s.activation(out=b["scr5"][:, j, :], in_=b["pb"][2 + j][:],
                                       func=AF.Silu, scale=1.0)
                    c.inc(ins, "act5", 1)

                def v_g(v, hc=hc, j=j, idx=idx, ch=ch):
                    c.wait(v, "pe5", p_f0 + idx + 1)
                    c.wait(v, "act5", a_f0 + idx + 1)
                    ins = v.tensor_tensor(out=b["gh"][:, hc, j * 512:(j + 1) * 512],
                                          in0=b["pb"][j][:], in1=b["scr5"][:, j, :], op=ALU.mult)
                    c.inc(ins, "dve5", 1)

                c.sec("tensor", t_g)
                c.sec("scalar", s_g)
                c.sec("vector", v_g)

        d_gh = c.cnt["dve5"]
        p_oc0 = c.cnt["pe5"]
        d_oc0 = c.cnt["dve5"]
        for oc in range(4):
            for j in range(2):
                idx = oc * 2 + j

                def t_o(t, oc=oc, j=j, idx=idx, ch=ch, d_gh=d_gh):
                    if idx == 0:
                        c.wait(t, "dve5", d_gh)
                    if idx >= 2:
                        c.wait(t, "dve5", d_oc0 + idx - 1)
                    for hc in range(4):
                        ins = t.matmul(b["pb"][4 + idx % 2][:],
                                       b["wTb"][1][:, hc, oc * 128:(oc + 1) * 128],
                                       b["gh"][:, hc, j * 512:(j + 1) * 512],
                                       start=(hc == 0), stop=(hc == 3))
                    c.inc(ins, "pe5", 1)

                def v_o(v, oc=oc, j=j, idx=idx, ch=ch):
                    c.wait(v, "pe5", p_oc0 + idx + 1)
                    ins = v.tensor_copy(out=b["ocb"][:, oc, j * 512:(j + 1) * 512],
                                        in_=b["pb"][4 + idx % 2][:])
                    c.inc(ins, "dve5", 1)

                c.sec("tensor", t_o)
                c.sec("vector", v_o)

        d_oc = c.cnt["dve5"]

        def sy_oc(sy, d_oc=d_oc, ch=ch):
            c.wait(sy, "dve5", d_oc)
            ins = sy.dma_start(out=b["cc4_in"].ap()[ch * 512:(ch + 1) * 512, :]
                               .rearrange("(oc p) t -> p oc t", p=128),
                               in_=b["ocb"][:])
            c.inc(ins, "s_st5", 16)

        c.sec("sync", sy_oc)

        p_k0 = c.cnt["pe5"]
        d_k0 = c.cnt["dve5"]
        n_k = 0
        for name, wsrc, xsrc in (("gba", 0, "kiT"), ("hbm", 2, "kiT"), ("dhid", 1, "vi")):
            for tc in range(NTC):
                nk = n_k

                def t_k(t, name=name, wsrc=wsrc, xsrc=xsrc, tc=tc, nk=nk, ch=ch):
                    if nk >= 2:
                        c.wait(t, "dve5", d_k0 + nk - 1)
                    wt = b["w1nb"] if name == "dhid" else b["wTb"][wsrc]
                    for dc in range(4):
                        ins = t.matmul(b["pb"][nk % 2][:],
                                       b[xsrc][:, dc, tc * 128:(tc + 1) * 128],
                                       wt[:, dc, :], start=(dc == 0), stop=(dc == 3))
                    c.inc(ins, "pe5", 1)

                def v_k(v, name=name, tc=tc, nk=nk, ch=ch):
                    c.wait(v, "pe5", p_k0 + nk + 1)
                    ins = v.tensor_copy(out=b[name][:, tc, :], in_=b["pb"][nk % 2][:])
                    c.inc(ins, "dve5", 1)

                c.sec("tensor", t_k)
                c.sec("vector", v_k)
                n_k += 1

        d_k = c.cnt["dve5"]

        p_tr0 = c.cnt["pe5"]
        d_tr0 = c.cnt["dve5"]
        n_t = 0
        for srcn, dstn in (("kiT", "ki_tm"), ("vi", "vi_tm")):
            for dc in range(4):
                nt = n_t

                def t_tr(t, srcn=srcn, dc=dc, nt=nt, ch=ch, d_k=d_k):
                    if nt == 0:
                        c.wait(t, "dve5", d_k)
                    if nt >= 1:
                        c.wait(t, "dve5", d_tr0 + nt)
                    for tc in range(NTC):
                        ins = t.transpose(b["pbt5"][:, tc * 128:(tc + 1) * 128],
                                          b[srcn][:, dc, tc * 128:(tc + 1) * 128], b["idn5"][:])
                    c.inc(ins, "pe5", 1)

                def v_tr(v, dstn=dstn, dc=dc, nt=nt, ch=ch):
                    c.wait(v, "pe5", p_tr0 + nt + 1)
                    for tc in range(NTC):
                        ins = v.tensor_copy(out=b[dstn][:, tc, dc * 128:(dc + 1) * 128],
                                            in_=b["pbt5"][:, tc * 128:(tc + 1) * 128])
                    c.inc(ins, "dve5", 1)

                c.sec("tensor", t_tr)
                c.sec("vector", v_tr)
                n_t += 1

        d_tr = c.cnt["dve5"]

        a_b0 = c.cnt["act5"]

        def s_bwd(s, d_tr=d_tr, ch=ch):
            c.wait(s, "dve5", d_tr)
            s.activation(out=b["sg"][:].rearrange("p a h -> p (a h)"),
                         in_=b["gba"][:].rearrange("p a h -> p (a h)"), func=AF.Silu, scale=1.0)
            ins = s.activation(out=b["dsil"][:].rearrange("p a h -> p (a h)"),
                               in_=b["gba"][:].rearrange("p a h -> p (a h)"),
                               func=AF.Sigmoid, scale=1.0)
            c.inc(ins, "act5", 1)

        c.sec("scalar", s_bwd)
        a_b = c.cnt["act5"]

        def v_bwd(v, a_b=a_b, ch=ch):
            c.wait(v, "act5", a_b)
            for tc in range(NTC):
                v.tensor_tensor(out=b["hl1"][:, tc, :], in0=b["sg"][:, tc, :],
                                in1=b["dsil"][:, tc, :], op=ALU.mult)
                v.tensor_tensor(out=b["dsil"][:, tc, :], in0=b["dsil"][:, tc, :],
                                in1=b["hl1"][:, tc, :], op=ALU.subtract)
                v.tensor_tensor(out=b["dsil"][:, tc, :], in0=b["dsil"][:, tc, :],
                                in1=b["sg"][:, tc, :], op=ALU.add)
            for tc in range(NTC):
                v.tensor_tensor(out=b["dgl0"][:, tc, :], in0=b["dhid"][:, tc, :],
                                in1=b["hbm"][:, tc, :], op=ALU.mult)
                v.tensor_tensor(out=b["dgl0"][:, tc, :], in0=b["dgl0"][:, tc, :],
                                in1=b["dsil"][:, tc, :], op=ALU.mult)
                ins = v.tensor_scalar(out=b["dgl0"][:, tc, :], in0=b["dgl0"][:, tc, :],
                                      scalar1=b["lmt"][:, tc, 0:1], scalar2=None, op0=ALU.mult)
            c.inc(ins, "dve5", 1)
            for tc in range(NTC):
                v.tensor_tensor(out=b["hl1"][:, tc, :], in0=b["sg"][:, tc, :],
                                in1=b["hbm"][:, tc, :], op=ALU.mult)
                v.tensor_scalar(out=b["hl1"][:, tc, :], in0=b["hl1"][:, tc, :],
                                scalar1=b["lmt"][:, tc, 1:2], scalar2=None, op0=ALU.mult)
            for tc in range(NTC):
                v.tensor_tensor(out=b["dhl2"][:, tc, :], in0=b["dhid"][:, tc, :],
                                in1=b["sg"][:, tc, :], op=ALU.mult)
                ins = v.tensor_scalar(out=b["dhl2"][:, tc, :], in0=b["dhl2"][:, tc, :],
                                      scalar1=b["lmt"][:, tc, 2:3], scalar2=None, op0=ALU.mult)
            c.inc(ins, "dve5", 1)

        c.sec("vector", v_bwd)
        d_bwd = c.cnt["dve5"]

        SLB = [
            dict(X="X", Xb="Xb", XTb="XTb", Ab="Ab", Bb="Bb", bk=(0, 1), trp="pbt5"),
            dict(X="X2", Xb="Xb2", XTb="XTb2", Ab="Ab2", Bb="Bb2", bk=(2, 3), trp="pbt5c"),
        ]

        def emit_dw(wsel, sl, d_gate, ch=ch):
            if ch > 0:
                def g_m(g, wsel=wsel):
                    c.wait(g, "dve5", c.cnt["dve5"])
                    ins = g.dma_start(out=b["mst"][:],
                                      in_=b["mdram"].ap()[wsel * 512:(wsel + 1) * 512, :]
                                      .rearrange("(dc p) h -> p dc h", p=128))
                    c.inc(ins, "g_ld", 16)
                c.sec("gpsimd", g_m)
            gm = c.cnt["g_ld"]
            A_name, B_name = (("dgl0", "ki_tm"), ("vi_tm", "hl1"), ("dhl2", "ki_tm"))[wsel]
            p_dw0 = c.cnt["pe5"]
            for oc in range(4):
                def t_dw(t, oc=oc, A_name=A_name, B_name=B_name, d_gate=d_gate):
                    if oc == 0:
                        c.wait(t, "dve5", d_gate)
                    for tc in range(NTC):
                        ins = t.matmul(b["pb"][oc][:],
                                       b[A_name][:, tc, oc * 128:(oc + 1) * 128],
                                       b[B_name][:, tc, :],
                                       start=(tc == 0), stop=(tc == NTC - 1))
                    c.inc(ins, "pe5", 1)

                def v_dw(v, oc=oc, sl=sl, gm=gm, p_dw0=p_dw0, ch=ch):
                    c.wait(v, "pe5", p_dw0 + oc + 1)
                    if oc == 0 and ch > 0:
                        c.wait(v, "g_ld", gm)
                    if ch == 0:
                        ins = v.tensor_copy(out=b[sl["X"]][:, oc, :], in_=b["pb"][oc][:])
                    else:
                        v.tensor_scalar(out=b["scr5f"][:, :], in0=b["mst"][:, oc, :],
                                        scalar1=b["mmb"][:, 0:1], scalar2=None, op0=ALU.mult)
                        ins = v.tensor_tensor(out=b[sl["X"]][:, oc, :], in0=b["pb"][oc][:],
                                              in1=b["scr5f"][:, :], op=ALU.add)
                    c.inc(ins, "dve5", 1)

                c.sec("tensor", t_dw)
                c.sec("vector", v_dw)
            d_x = c.cnt["dve5"]

            def sy_m(sy, wsel=wsel, sl=sl, d_x=d_x):
                c.wait(sy, "dve5", d_x)
                ins = sy.dma_start(out=b["mdram"].ap()[wsel * 512:(wsel + 1) * 512, :]
                                   .rearrange("(dc p) h -> p dc h", p=128),
                                   in_=b[sl["X"]][:])
                c.inc(ins, "s_st5", 16)

            c.sec("sync", sy_m)
            return d_x, c.cnt["s_st5"]

        def emit_fr(sl, d_x, st_m):
            def s_fr(s, sl=sl, d_x=d_x):
                c.wait(s, "dve5", d_x)
                ins = s.activation(out=b["sq5"][:].rearrange("p a h -> p (a h)"),
                                   in_=b[sl["X"]][:].rearrange("p a h -> p (a h)"),
                                   func=AF.Square, scale=1.0)
                c.inc(ins, "act5", 1)
            c.sec("scalar", s_fr)
            a_q = c.cnt["act5"]

            def t_fr(t, a_q=a_q, d_x=d_x):
                c.wait(t, "dve5", d_x)
                COLSUM(t, 4, a_q)
            c.sec("tensor", t_fr)
            p_q = c.cnt["pe5"]

            def v_fr(v, p_q=p_q):
                c.wait(v, "pe5", p_q)
                ins = v.reduce_sum(out=b["fr1"][:, 0:1], in_=b["pb"][4][:], axis=AXX)
                c.inc(ins, "dve5", 1)
                c.wait(v, "dve5")
            c.sec("vector", v_fr)
            d_rf = c.cnt["dve5"]

            def s_fr2(s, d_rf=d_rf):
                c.wait(s, "dve5", d_rf)
                ins = s.activation(out=b["fr1"][:, 0:1], in_=b["fr1"][:, 0:1],
                                   func=AF.Sqrt, scale=1.0)
                c.inc(ins, "act5", 1)
            c.sec("scalar", s_fr2)
            a_q2 = c.cnt["act5"]

            def v_fr2(v, sl=sl, a_q2=a_q2, st_m=st_m):
                c.wait(v, "act5", a_q2)
                c.wait(v, "s_st5", st_m)
                v.tensor_scalar(out=b["fr1"][:, 0:1], in0=b["fr1"][:, 0:1], scalar1=1e-7,
                                scalar2=None, op0=ALU.add)
                ins = v.reciprocal(out=b["fr2"][:, 0:1], in_=b["fr1"][:, 0:1])
                c.inc(ins, "dve5", 1)
                c.wait(v, "dve5")
                for a_ in range(4):
                    ins = v.tensor_scalar(out=b[sl["X"]][:, a_, :], in0=b[sl["X"]][:, a_, :],
                                          scalar1=b["fr2"][:, 0:1], scalar2=None, op0=ALU.mult)
                c.inc(ins, "dve5", 1)
            c.sec("vector", v_fr2)
            return c.cnt["dve5"]

        def emit_ctr(jobs):
            a_c = []
            for sl, d_ready in jobs:
                def s_c(s, sl=sl, d_ready=d_ready):
                    c.wait(s, "dve5", d_ready)
                    ins = s.activation(out=b[sl["Xb"]][:].rearrange("p a h -> p (a h)"),
                                       in_=b[sl["X"]][:].rearrange("p a h -> p (a h)"),
                                       func=AF.Copy, scale=1.0)
                    c.inc(ins, "act5", 1)
                c.sec("scalar", s_c)
                a_c.append(c.cnt["act5"])
            a_ev = list(a_c)
            for rnd in range(2):
                p_tr = [0] * len(jobs)
                for ji, (sl, _) in enumerate(jobs):
                    def t_tr5(t, sl=sl, rnd=rnd, gate=a_ev[ji]):
                        c.wait(t, "act5", gate)
                        for ic in range(4):
                            for k2 in range(2):
                                kc = rnd * 2 + k2
                                ins = t.transpose(
                                    b[sl["trp"]][:, (ic * 2 + k2) * 128:(ic * 2 + k2 + 1) * 128],
                                    b[sl["Xb"]][:, kc, ic * 128:(ic + 1) * 128],
                                    b["idn5"][:])
                        c.inc(ins, "pe5", 1)
                    c.sec("tensor", t_tr5)
                    p_tr[ji] = c.cnt["pe5"]
                for ji, (sl, _) in enumerate(jobs):
                    def s_ev(s, sl=sl, rnd=rnd, pw=p_tr[ji]):
                        c.wait(s, "pe5", pw)
                        for ic in range(4):
                            ins = s.activation(out=b[sl["XTb"]][:, ic, rnd * 256:(rnd + 1) * 256],
                                               in_=b[sl["trp"]][:, ic * 256:(ic + 1) * 256],
                                               func=AF.Copy, scale=1.0)
                        c.inc(ins, "act5", 1)
                    c.sec("scalar", s_ev)
                    a_ev[ji] = c.cnt["act5"]
            return a_ev

        def ns_stage2(stage, xw, xa, coef):
            aa, bb_, cc_ = coef
            d_h = {}
            for w, hf in ((0, 0), (1, 0), (0, 1), (1, 1)):
                sl = SLB[w]
                bk = sl["bk"]
                ic0 = hf * 2
                dgate = xw[w] if hf == 0 else d_h[(w, 0)]
                agate = xa[w]
                lhs = {"A": sl["XTb"], "A2": sl["Ab"], "BX": sl["Bb"]}[stage]
                rhs = {"A": sl["XTb"], "A2": sl["Ab"], "BX": sl["Xb"]}[stage]

                def t_s(t, bk=bk, ic0=ic0, dgate=dgate, agate=agate, lhs=lhs, rhs=rhs):
                    c.wait(t, "dve5", dgate)
                    c.wait(t, "act5", agate)
                    for i2 in range(2):
                        for kc in range(4):
                            ins = t.matmul(b["pb"][bk[i2]][:],
                                           b[lhs][:, kc, (ic0 + i2) * 128:(ic0 + i2 + 1) * 128],
                                           b[rhs][:, kc, :],
                                           start=(kc == 0), stop=(kc == 3))
                        c.inc(ins, "pe5", 1)
                c.sec("tensor", t_s)
                p_aft = c.cnt["pe5"]

                def v_s(v, sl=sl, bk=bk, ic0=ic0, p_aft=p_aft, stage=stage,
                        aa=aa, bb_=bb_, cc_=cc_):
                    c.wait(v, "pe5", p_aft)
                    for i2 in range(2):
                        ic = ic0 + i2
                        if stage == "A":
                            ins = v.tensor_copy(out=b[sl["Ab"]][:, ic, :], in_=b["pb"][bk[i2]][:])
                        elif stage == "A2":
                            v.tensor_scalar(out=b["scr5f"][:, :], in0=b["pb"][bk[i2]][:],
                                            scalar1=cc_, scalar2=None, op0=ALU.mult)
                            ins = v.scalar_tensor_tensor(out=b[sl["Bb"]][:, ic, :],
                                                         in0=b[sl["Ab"]][:, ic, :],
                                                         scalar=bb_, in1=b["scr5f"][:, :],
                                                         op0=ALU.mult, op1=ALU.add)
                        else:
                            ins = v.scalar_tensor_tensor(out=b[sl["X"]][:, ic, :],
                                                         in0=b[sl["X"]][:, ic, :],
                                                         scalar=aa, in1=b["pb"][bk[i2]][:],
                                                         op0=ALU.mult, op1=ALU.add)
                    c.inc(ins, "dve5", 1)
                c.sec("vector", v_s)
                d_h[(w, hf)] = c.cnt["dve5"]
            return [d_h[(0, 1)], d_h[(1, 1)]]

        def ns_2way(xw, xa):
            for it in range(5):
                coef = NS_COEFFS[it]
                dA = ns_stage2("A", xw, xa, coef)
                dB = ns_stage2("A2", dA, xa, coef)
                xw = ns_stage2("BX", dB, xa, coef)
                if it < 4:
                    xa = emit_ctr([(SLB[0], xw[0]), (SLB[1], xw[1])])
            return xw

        def ns_solo(sl, xw, xa):
            for it in range(5):
                aa, bb_, cc_ = NS_COEFFS[it]
                for stage in ("A", "A2", "BX"):
                    lhs = {"A": sl["XTb"], "A2": sl["Ab"], "BX": sl["Bb"]}[stage]
                    rhs = {"A": sl["XTb"], "A2": sl["Ab"], "BX": sl["Xb"]}[stage]
                    p0 = c.cnt["pe5"]

                    def t_s(t, lhs=lhs, rhs=rhs, xw=xw, xa=xa):
                        c.wait(t, "dve5", xw)
                        c.wait(t, "act5", xa)
                        for ic in range(4):
                            for kc in range(4):
                                ins = t.matmul(b["pb"][ic][:],
                                               b[lhs][:, kc, ic * 128:(ic + 1) * 128],
                                               b[rhs][:, kc, :],
                                               start=(kc == 0), stop=(kc == 3))
                            c.inc(ins, "pe5", 1)
                    c.sec("tensor", t_s)

                    def v_s(v, sl=sl, p0=p0, stage=stage, aa=aa, bb_=bb_, cc_=cc_):
                        for ic in range(4):
                            c.wait(v, "pe5", p0 + ic + 1)
                            if stage == "A":
                                ins = v.tensor_copy(out=b[sl["Ab"]][:, ic, :], in_=b["pb"][ic][:])
                            elif stage == "A2":
                                v.tensor_scalar(out=b["scr5f"][:, :], in0=b["pb"][ic][:],
                                                scalar1=cc_, scalar2=None, op0=ALU.mult)
                                ins = v.scalar_tensor_tensor(out=b[sl["Bb"]][:, ic, :],
                                                             in0=b[sl["Ab"]][:, ic, :],
                                                             scalar=bb_, in1=b["scr5f"][:, :],
                                                             op0=ALU.mult, op1=ALU.add)
                            else:
                                ins = v.scalar_tensor_tensor(out=b[sl["X"]][:, ic, :],
                                                             in0=b[sl["X"]][:, ic, :],
                                                             scalar=aa, in1=b["pb"][ic][:],
                                                             op0=ALU.mult, op1=ALU.add)
                            c.inc(ins, "dve5", 1)
                    c.sec("vector", v_s)
                    xw = c.cnt["dve5"]
                if it < 4:
                    xa = emit_ctr([(sl, xw)])[0]
            return xw

        def emit_up(wsel, sl, bkpair, d_fin):
            p_u0 = c.cnt["pe5"]
            d_c0 = c.cnt["dve5"]
            for ic in range(4):
                bkx = bkpair[ic % 2]

                def t_tru(t, sl=sl, ic=ic, bkx=bkx, d_fin=d_fin, d_c0=d_c0):
                    if ic == 0:
                        c.wait(t, "dve5", d_fin)
                    if ic >= 2:
                        c.wait(t, "dve5", d_c0 + ic - 1)
                    for kc in range(4):
                        ins = t.transpose(b["pb"][bkx][:, kc * 128:(kc + 1) * 128],
                                          b[sl["X"]][:, kc, ic * 128:(ic + 1) * 128],
                                          b["idn5f"][:])
                    c.inc(ins, "pe5", 1)

                def v_add(v, wsel=wsel, ic=ic, bkx=bkx, p_u0=p_u0):
                    c.wait(v, "pe5", p_u0 + ic + 1)
                    ins = v.tensor_tensor(out=b["wT"][wsel][:, ic, :],
                                          in0=b["wT"][wsel][:, ic, :],
                                          in1=b["pb"][bkx][:], op=ALU.add)
                    c.inc(ins, "dve5", 1)

                c.sec("tensor", t_tru)
                c.sec("vector", v_add)
            if wsel == 1:
                def v_w1u(v, sl=sl):
                    for a_ in range(4):
                        ins = v.tensor_tensor(out=b["w1n"][:, a_, :], in0=b["w1n"][:, a_, :],
                                              in1=b[sl["X"]][:, a_, :], op=ALU.add)
                    c.inc(ins, "dve5", 1)
                c.sec("vector", v_w1u)
            d_u1 = c.cnt["dve5"]

            def s_u1(s, wsel=wsel, d_u1=d_u1):
                c.wait(s, "dve5", d_u1)
                ins = s.activation(out=b["sq5"][:].rearrange("p a h -> p (a h)"),
                                   in_=b["wT"][wsel][:].rearrange("p a h -> p (a h)"),
                                   func=AF.Square, scale=1.0)
                c.inc(ins, "act5", 1)
            c.sec("scalar", s_u1)
            a_u1 = c.cnt["act5"]

            def t_u1(t, a_u1=a_u1, d_u1=d_u1):
                c.wait(t, "dve5", d_u1)
                COLSUM(t, 4, a_u1)
            c.sec("tensor", t_u1)
            p_u1 = c.cnt["pe5"]

            def v_u2(v, p_u1=p_u1):
                c.wait(v, "pe5", p_u1)
                ins = v.tensor_copy(out=b["rc1"][:, :], in_=b["pb"][4][:])
                c.inc(ins, "dve5", 1)
            c.sec("vector", v_u2)
            d_u2 = c.cnt["dve5"]

            def s_u2(s, d_u2=d_u2):
                c.wait(s, "dve5", d_u2)
                ins = s.activation(out=b["rc1"][:, :], in_=b["rc1"][:, :],
                                   func=AF.Sqrt, scale=1.0)
                c.inc(ins, "act5", 1)
            c.sec("scalar", s_u2)
            a_u2 = c.cnt["act5"]

            def v_u3(v, wsel=wsel, a_u2=a_u2):
                c.wait(v, "act5", a_u2)
                v.tensor_scalar(out=b["rc1"][:, :], in0=b["rc1"][:, :], scalar1=1e-5,
                                scalar2=None, op0=ALU.add)
                ins = v.reciprocal(out=b["rc2"][:, :], in_=b["rc1"][:, :])
                c.inc(ins, "dve5", 1)
                c.wait(v, "dve5")
                ins = v.tensor_tensor(out=b["rc2"][:, :], in0=b["rc2"][:, :],
                                      in1=b["nwb"][wsel][:], op=ALU.mult)
                c.inc(ins, "dve5", 1)
                for a_ in range(4):
                    ins = v.tensor_tensor(out=b["wT"][wsel][:, a_, :],
                                          in0=b["wT"][wsel][:, a_, :],
                                          in1=b["rc2"][:, :], op=ALU.mult)
                c.inc(ins, "dve5", 1)
            c.sec("vector", v_u3)

            if wsel == 1:
                d_w0 = c.cnt["dve5"]

                def s_w1(s, d_w0=d_w0):
                    c.wait(s, "dve5", d_w0)
                    ins = s.activation(out=b["sq5"][:].rearrange("p a h -> p (a h)"),
                                       in_=b["w1n"][:].rearrange("p a h -> p (a h)"),
                                       func=AF.Square, scale=1.0)
                    c.inc(ins, "act5", 1)
                c.sec("scalar", s_w1)
                a_w1 = c.cnt["act5"]

                def v_w1(v, a_w1=a_w1):
                    c.wait(v, "act5", a_w1)
                    for a_ in range(4):
                        ins = v.reduce_sum(out=b["r1c"][:, a_:a_ + 1], in_=b["sq5"][:, a_, :],
                                           axis=AXX)
                    c.inc(ins, "dve5", 1)
                    c.wait(v, "dve5")
                c.sec("vector", v_w1)
                d_w1 = c.cnt["dve5"]

                def s_w1b(s, d_w1=d_w1):
                    c.wait(s, "dve5", d_w1)
                    ins = s.activation(out=b["r1c"][:, 0:4], in_=b["r1c"][:, 0:4], func=AF.Sqrt,
                                       scale=1.0)
                    c.inc(ins, "act5", 1)
                c.sec("scalar", s_w1b)
                a_w1b = c.cnt["act5"]

                def v_w1b(v, a_w1b=a_w1b):
                    c.wait(v, "act5", a_w1b)
                    v.tensor_scalar(out=b["r1c"][:, 0:4], in0=b["r1c"][:, 0:4], scalar1=1e-5,
                                    scalar2=None, op0=ALU.add)
                    ins = v.reciprocal(out=b["r1d"][:, 0:4], in_=b["r1c"][:, 0:4])
                    c.inc(ins, "dve5", 1)
                    c.wait(v, "dve5")
                    ins = v.tensor_tensor(out=b["r1d"][:, 0:4], in0=b["r1d"][:, 0:4],
                                          in1=b["n1r"][:, 0:4], op=ALU.mult)
                    c.inc(ins, "dve5", 1)
                    c.wait(v, "dve5")
                    for a_ in range(4):
                        ins = v.tensor_scalar(out=b["w1n"][:, a_, :], in0=b["w1n"][:, a_, :],
                                              scalar1=b["r1d"][:, a_:a_ + 1], scalar2=None,
                                              op0=ALU.mult)
                    c.inc(ins, "dve5", 1)
                c.sec("vector", v_w1b)

        d_x0_, st_m0 = emit_dw(0, SLB[0], d_bwd - 1)
        d_n0 = emit_fr(SLB[0], d_x0_, st_m0)
        d_x1_, st_m1 = emit_dw(1, SLB[1], d_x0_)
        d_n1 = emit_fr(SLB[1], d_x1_, st_m1)
        a_ct = emit_ctr([(SLB[0], d_n0), (SLB[1], d_n1)])
        d_fin = ns_2way([max(d_n0, d_x1_), max(d_n1, d_x1_)], a_ct)
        emit_up(0, SLB[0], (0, 1), d_fin[0])
        emit_up(1, SLB[1], (2, 3), d_fin[1])
        d_x2_, st_m2 = emit_dw(2, SLB[0], c.cnt["dve5"])
        d_n2 = emit_fr(SLB[0], d_x2_, st_m2)
        a2 = emit_ctr([(SLB[0], d_n2)])[0]
        d_f2 = ns_solo(SLB[0], d_n2, a2)
        emit_up(2, SLB[0], (0, 1), d_f2)
        d_gate_in = d_x2_



    st5_all = c.cnt["s_st5"]

    def g_cc4(g):
        c.wait(g, "s_st5", st5_all)
        with allow_ici_all_to_all():
            ins = g.collective_compute("AllToAll", ALU.bypass,
                                       replica_groups=[list(range(NCORES))],
                                       ins=[b["cc4_in"].ap().opt()],
                                       outs=[b["cc4_out"].ap().opt()])
        c.inc(ins, "cc", 1)

    c.sec("gpsimd", g_cc4)



def emit_p5b(c):
    b = c.buf
    cc_need = c.cnt["cc"]
    sp0 = c.cnt["s_sp"]

    pe5_all = c.cnt["pe5"]
    d5_all = c.cnt["dve5"]
    a5_all = c.cnt["act5"]
    st5b = c.cnt["s_st5"]

    def g_5b(g):
        c.wait(g, "cc", cc_need)
        c.wait(g, "pe5", pe5_all)
        c.wait(g, "dve5", d5_all)
        c.wait(g, "act5", a5_all)
        c.wait(g, "s_st5", st5b)
        ins = g.dma_start(out=b["ong"][:], in_=b["in_ong"])
        c.inc(ins, "s_ld", 16)

    c.sec("gpsimd", g_5b)
    gld_b = c.cnt["s_ld"]

    def sy_r(sy):
        c.wait(sy, "s_sp", sp0)
        c.wait(sy, "pe5", pe5_all)
        c.wait(sy, "dve5", d5_all)
        c.wait(sy, "act5", a5_all)
        ins = sy.dma_start(out=b["R"][:].rearrange("p (dc t) -> p dc t", dc=DC),
                           in_=b["rspill"].ap())
        c.inc(ins, "s_sp", 16)

    c.sec("sync", sy_r)

    st_h0 = c.cnt["s_st5"]
    d5b0 = c.cnt["dve5"]
    for h in range(4):
        def sy_h(sy, h=h):
            c.wait(sy, "s_ld", gld_b)
            if h >= 1:
                c.wait(sy, "act5", c.cnt["act5"])
            if h >= 2:
                c.wait(sy, "dve5", c.cnt["dve5"])
            ins = sy.dma_start(out=b["ttth"][:, h % 2, :, :],
                               in_=b["cc4_out"].ap()[2 * h * 512:(2 * h + 1) * 512, :]
                               .rearrange("(hc p) t -> p hc t", p=128))
            c.inc(ins, "s_st5", 16)
            ins = sy.dma_start(out=b["attnh"][:, h % 2, :, :],
                               in_=b["attn_loc"].ap()[h * 512:(h + 1) * 512, :]
                               .rearrange("(hc p) t -> p hc t", p=128))
            c.inc(ins, "s_st5", 16)
            ins = sy.dma_start(out=b["sc_row"][0:1, :], in_=b["scdram"].ap()[h:h + 1, :])
            c.inc(ins, "s_st5", 16)

        c.sec("sync", sy_h)
        st_h = c.cnt["s_st5"]
        a_h0 = c.cnt["act5"]

        def s_h(s, h=h, st_h=st_h):
            c.wait(s, "s_st5", st_h)
            for hc in range(4):
                if hc >= 1:
                    c.wait(s, "pe5", c.cnt["pe5"] + hc - 4 if False else c.cnt["pe5"])
                ins = s.activation(out=b["sqo"][:, hc % 2, :], in_=b["ttth"][:, h % 2, hc, :],
                                   func=AF.Square, scale=1.0)
                c.inc(ins, "act5", 1)

        a_sq0 = c.cnt["act5"]
        p_o0 = c.cnt["pe5"]

        def s_h2(s, h=h, st_h=st_h, p_o0=p_o0):
            c.wait(s, "s_st5", st_h)
            for hc in range(4):
                if hc >= 2:
                    c.wait(s, "pe5", p_o0 + hc - 1)
                ins = s.activation(out=b["sqo"][:, hc % 2, :], in_=b["ttth"][:, h % 2, hc, :],
                                   func=AF.Square, scale=1.0)
                c.inc(ins, "act5", 1)

        c.sec("scalar", s_h2)

        def t_h(t, h=h, a_sq0=a_sq0):
            if h >= 1:
                c.wait(t, "dve5", d5b0 + 6 * h)
            for hc in range(4):
                c.wait(t, "act5", a_sq0 + hc + 1)
                for j in range(2):
                    ins = t.matmul(b["pb"][j][:], b["ones128"][:],
                                   b["sqo"][:, hc % 2, j * 512:(j + 1) * 512],
                                   start=(hc == 0), stop=(hc == 3))
                c.inc(ins, "pe5", 1)
            for j in range(2):
                ins = t.matmul(b["pb"][2 + j][:], b["ones1w"][0:1, 0:128],
                               b["sc_row"][0:1, j * 512:(j + 1) * 512],
                               start=True, stop=True)
            c.inc(ins, "pe5", 1)

        c.sec("tensor", t_h)
        p_o = c.cnt["pe5"]
        a_r0 = c.cnt["act5"]

        def s_h3(s, h=h, p_o=p_o):
            c.wait(s, "pe5", p_o)
            for j in range(2):
                ins = s.activation(out=b["nrm5"][:, j * 512:(j + 1) * 512], in_=b["pb"][j][:],
                                   func=AF.Sqrt, scale=1.0 / 512.0, bias=b["c_eps"][:])
                c.inc(ins, "act5", 1)

        c.sec("scalar", s_h3)
        a_r = c.cnt["act5"]
        d_h0 = c.cnt["dve5"]

        def v_h(v, h=h, a_r=a_r, p_o=p_o):
            c.wait(v, "act5", a_r)
            c.wait(v, "pe5", p_o)
            ins = v.reciprocal(out=b["rstdb"][:, :], in_=b["scb5"][:, :] if False else b["nrm5"][:, :])
            c.inc(ins, "dve5", 1)
            v.tensor_copy(out=b["scb5"][:, 0:512], in_=b["pb"][2][:])
            ins = v.tensor_copy(out=b["scb5"][:, 512:1024], in_=b["pb"][3][:])
            c.inc(ins, "dve5", 1)
            for hc in range(4):
                v.tensor_tensor(out=b["t5"][:, :], in0=b["ttth"][:, h % 2, hc, :],
                                in1=b["rstdb"][:, :], op=ALU.mult)
                v.tensor_scalar(out=b["t5"][:, :], in0=b["t5"][:, :],
                                scalar1=b["ong"][:, hc:hc + 1], scalar2=None,
                                op0=ALU.mult)
                v.tensor_tensor(out=b["t5"][:, :], in0=b["t5"][:, :], in1=b["scb5"][:, :],
                                op=ALU.mult)
                ins = v.tensor_tensor(out=b["zo"][:, h * 4 + hc, :], in0=b["t5"][:, :],
                                      in1=b["attnh"][:, h % 2, hc, :], op=ALU.add)
                c.inc(ins, "dve5", 1)

        c.sec("vector", v_h)

    d_zo = c.cnt["dve5"]

    gw0 = c.cnt["g_ld"]

    def g_wo(g):
        for odc in range(DC):
            if odc >= 2:
                c.wait(g, "pe5", c.cnt["pe5"])
            ins = g.dma_start(
                out=b["wf5"][:, odc % 2, :, :],
                in_=b["in_Wo"][:, odc * 128:(odc + 1) * 128].rearrange("(dc p) e -> p dc e", p=128),
                max_dma_last_dim=2048)
            c.inc(ins, "g_ld", 16)

    p_wo0 = c.cnt["pe5"]

    def g_wo2(g, p_wo0=p_wo0):
        for odc in range(DC):
            if odc >= 1:
                c.wait(g, "pe5", p_wo0 + 2 * odc)
            ins = g.dma_start(
                out=b["wf5"][:, odc % 2, :, :],
                in_=b["in_Wo"][:, odc * 128:(odc + 1) * 128].rearrange("(dc p) e -> p dc e", p=128),
                max_dma_last_dim=2048)
            c.inc(ins, "g_ld", 16)

    c.sec("gpsimd", g_wo2)

    d_wo0 = c.cnt["dve5"]

    def t_wo(t, d_zo=d_zo, p_wo0=p_wo0, gw0=gw0):
        c.wait(t, "dve5", d_zo)
        for odc in range(DC):
            c.wait(t, "g_ld", gw0 + 16 * (odc + 1))
            for j in range(2):
                idx = odc * 2 + j
                if idx >= 2:
                    c.wait(t, "dve5", d_wo0 + idx - 1)
                for kc in range(DC):
                    ins = t.matmul(b["pb"][4 + j][:], b["wf5"][:, odc % 2, kc, :],
                                   b["zo"][:, kc, j * 512:(j + 1) * 512],
                                   start=(kc == 0), stop=(kc == DC - 1))
                c.inc(ins, "pe5", 1)

    c.sec("tensor", t_wo)
    p_wo = c.cnt["pe5"]
    sp_r = c.cnt["s_sp"]

    def v_wo(v, p_wo0=p_wo0, sp_r=sp_r):
        c.wait(v, "s_sp", sp_r)
        n = 0
        for odc in range(DC):
            for j in range(2):
                c.wait(v, "pe5", p_wo0 + n + 1)
                sl = slice(odc * T + j * 512, odc * T + (j + 1) * 512)
                ins = v.tensor_tensor(out=b["R"][:, sl], in0=b["R"][:, sl],
                                      in1=b["pb"][4 + j][:], op=ALU.add)
                n += 1
                if n < 2 * DC:
                    c.inc(ins, "dve5", 1)
        c.inc(ins, "res", 1)

    c.sec("vector", v_wo)

def build_nc():
    nc = bass.Bass("TRN2", target_bir_lowering=False, detect_race_conditions=False)
    stack = contextlib.ExitStack()
    b = {}

    def din(name, shape, dt=F32):
        b[name] = nc.dram_tensor(name, list(shape), dt, kind="ExternalInput").ap()

    din("xT", (D, T))
    din("in_aWqT", (D, D)); din("in_aWkT", (D, D)); din("in_aWvT", (D, D)); din("in_aWgT", (D, D))
    din("in_bq16", (128, 16)); din("in_bg16", (128, 16))
    din("in_bkv", (128, NCORES * 512))
    din("in_UTmix", (512, 512))
    din("in_m8", (NCORES, 1))
    din("in_fW1", (D, D)); din("in_fb1", (128, 16))
    din("in_fW2", (D, 8)); din("in_fb2b", (128, 8))
    din("in_VT", (D, 512)); din("in_UTf", (512, D))
    din("in_E8", (8, 512)); din("in_idn", (128, 128))
    din("in_Wqkv", (D, 3 * D))
    din("in_qks", (128, 16, 2)); din("in_qko", (128, 16, 2))
    din("in_bm", (4, 1)); din("in_bs", (4, 1))
    din("in_Wsm", (D, 20))
    din("in_dyn", (8, 1), mybir.dt.int32)
    din("in_wm", (8, 128, 1152))
    din("in_wT", (1536, 512))
    din("in_w1n", (512, 512))
    din("in_ong", (128, 4))
    din("in_Wo", (D, D))
    b["yT"] = nc.dram_tensor("yT", [D, T], F32, kind="ExternalOutput").ap()
    if DEBUG_DUMP:
        b["dbg_xn"] = nc.dram_tensor("dbg_xn", [DC, 128, T], BF16, kind="ExternalOutput").ap()
        b["dbg_attn"] = nc.dram_tensor("dbg_attn", [D, T], BF16, kind="ExternalOutput").ap()
        b["dbg_scan"] = nc.dram_tensor("dbg_scan", [3 * D, T], BF16, kind="ExternalOutput").ap()
        b["dbg_tt"] = nc.dram_tensor("dbg_tt", [D, T], BF16, kind="ExternalOutput").ap()
        b["dbg_prev"] = nc.dram_tensor("dbg_prev", [2 * D, T], BF16, kind="ExternalOutput").ap()
        b["dbg_wloc"] = nc.dram_tensor("dbg_wloc", [3 * D, T], BF16, kind="ExternalOutput").ap()
    b["cc1_in"] = nc.dram_tensor("cc1_in", [D, 1], F32)
    b["cc1_out"] = nc.dram_tensor("cc1_out", [NCORES * D, 1], F32, addr_space="Shared")
    b["wloc"] = nc.dram_tensor("wloc", [3 * D, T], BF16)
    b["cc2_in"] = nc.dram_tensor("cc2_in", [2 * D, T], BF16)
    b["cc2_out"] = nc.dram_tensor("cc2_out", [NCORES * 2 * D, T], BF16, addr_space="Shared")
    b["cc3_in"] = nc.dram_tensor("cc3_in", [3 * D, T], BF16)
    b["cc3_out"] = nc.dram_tensor("cc3_out", [NCORES * 3 * D, T], BF16, addr_space="Shared")
    b["cc3b_in"] = nc.dram_tensor("cc3b_in", [16, T], F32)
    b["cc3b_out"] = nc.dram_tensor("cc3b_out", [NCORES * 16, T], F32, addr_space="Shared")
    b["attn_loc"] = nc.dram_tensor("attn_loc", [D, T], BF16)
    b["prevloc"] = nc.dram_tensor("prevloc", [2 * D, T], BF16)
    b["rspill"] = nc.dram_tensor("rspill", [128, DC, T], F32)
    b["mdram"] = nc.dram_tensor("mdram", [6 * 512, 512], F32)
    b["scanloc"] = nc.dram_tensor("scanloc", [8 * 1536, T], BF16)
    b["scanlocb"] = nc.dram_tensor("scanlocb", [32, T], F32)
    b["cc4_in"] = nc.dram_tensor("cc4_in", [NCORES * 512, T], BF16)
    b["cc4_out"] = nc.dram_tensor("cc4_out", [NCORES * 512, T], BF16)
    b["ttloc"] = nc.dram_tensor("ttloc", [D, T], BF16)
    b["scdram"] = nc.dram_tensor("scdram", [4, T], F32)

    with stack:
        with nc.Block() as block:
            c = Ctx(nc, block, stack)
            c.buf = b
            for s in ("g_ld", "s_ld", "pe", "dve", "act", "pool", "cc", "wr", "init",
                      "res", "dve_kv", "dve_tot", "dve_offs", "act_g", "dve_c",
                      "dve_l", "dve_s", "act_s", "dve_x",
                      "dve_q", "act_q", "act_sq", "act_n", "dve_n", "dve_n0", "pe_n",
                      "act_sm", "dve_sm", "s_st", "s_st2", "s_st3",
                      "dve_w", "act_w", "pe_sc", "pe_tr", "pe_pv", "pe_av", "s_stw",
                      "pe5", "dve5", "act5", "s_st5", "s_sp", "g_cp"):
                c.new_sem(s)

            c.sb(stack, "ones128", (128, 128), BF16)
            c.sb(stack, "ones1", (128, 1), BF16)
            c.sb(stack, "ones1w", (1, 128), F32)
            c.sb(stack, "c_eps", (128, 1), F32)
            c.sb(stack, "c_one", (128, 1), F32)
            c.sb(stack, "c_lrb", (128, 1), F32)
            c.sb(stack, "scale_sb", (4, T), F32)
            c.sb(stack, "dyn", (8, 1), mybir.dt.int32)
            b["pb"] = [stack.enter_context(nc.psum_tensor(f"pb{i}", [128, 512], F32))
                       for i in range(6)]

            def v_init0(v):
                v.memset(b["ones128"][:], 1.0)
                v.memset(b["ones1"][:], 1.0)
                v.memset(b["ones1w"][:], 1.0)
                v.memset(b["c_one"][:], 1.0)
                v.memset(b["c_lrb"][:], float(0.01 + np.log(-np.expm1(-0.01))))
                ins = v.memset(b["c_eps"][:], 1e-5)
                c.inc(ins, "init", 1)

            c.sec("vector", v_init0)

            with contextlib.ExitStack() as sA:
                c.sb(sA, "R", (128, DC * T), F32)
                c.sb(sA, "xn", (128, DC, T), BF16)
                c.sb(sA, "sq_scr", (128, 2, T), BF16)
                c.sb(sA, "rb_scr", (128, 2, T), BF16)
                for n in ("lnm", "lnv", "lnr", "t1"):
                    c.sb(sA, n, (128, T), F32)

                emit_p0(c)

                with contextlib.ExitStack() as p1s:
                    c.sb(p1s, "UTmix", (128, 4, 512), BF16)
                    c.sb(p1s, "kv_tm", (128, NTC, D), BF16)
                    c.sb(p1s, "kv_scr", (128, 2, 256), F32)
                    c.sb(p1s, "q_sb", (128, T), BF16)
                    c.sb(p1s, "g_sb", (128, T), BF16)
                    c.sb(p1s, "tot_row", (1, D), F32)
                    c.sb(p1s, "tot_all", (NCORES, D), F32)
                    c.sb(p1s, "offs", (128, DC), F32)
                    c.sb(p1s, "m8", (NCORES, 1), F32)

                    def g_consts(g):
                        for dst, srcn in (("bkv", "in_bkv"), ("m8", "in_m8")):
                            ins = g.dma_start(out=b[dst][:], in_=b[srcn])
                            c.inc(ins, "g_ld", 16)
                        ins = g.dma_start(out=b["UTmix"][:],
                                          in_=b["in_UTmix"].rearrange("(tc p) t -> p tc t", p=128),
                                          max_dma_last_dim=2048)
                        c.inc(ins, "g_ld", 16)

                    with contextlib.ExitStack() as p1sb:
                        c.sb(p1sb, "bkv", (128, NCORES * 512), BF16)
                        c.sb(p1sb, "wkv", (128, 2, 2, DC, 128), BF16)
                        c.sec("gpsimd", g_consts)
                        emit_p1_B(c)
                    with contextlib.ExitStack() as p1sc:
                        c.sb(p1sc, "wqg", (128, 2, 2, DC, 128), BF16)
                        c.sb(p1sc, "ones512", (128, 512), BF16)
                        c.sb(p1sc, "bq16", (128, 16), F32)
                        c.sb(p1sc, "bg16", (128, 16), F32)

                        def v_o512(v):
                            ins = v.memset(b["ones512"][:], 1.0)
                            c.inc(ins, "init", 1)
                        c.sec("vector", v_o512)

                        def g_qgb(g):
                            for dst, srcn in (("bq16", "in_bq16"), ("bg16", "in_bg16")):
                                ins = g.dma_start(out=b[dst][:], in_=b[srcn])
                                c.inc(ins, "g_ld", 16)
                        c.sec("gpsimd", g_qgb)
                        emit_p1_C(c)

                with contextlib.ExitStack() as p2s:
                    c.sb(p2s, "h_sb", (128, DC, T), BF16)
                    c.sb(p2s, "wf", (128, 1, DC, 128), BF16)
                    c.sb(p2s, "w2sb", (128, DC, 8), BF16)
                    c.sb(p2s, "fb1", (128, 16), F32)
                    c.sb(p2s, "b2b", (128, 8), F32)
                    c.sb(p2s, "vtsb", (128, DC, 512), BF16)
                    c.sb(p2s, "utsb", (128, 4, D), BF16)
                    c.sb(p2s, "e8sb", (8, 512), BF16)
                    c.sb(p2s, "idnt", (128, 128), BF16)
                    c.sb(p2s, "lg_tm", (128, NTC, 8), F32)
                    c.sb(p2s, "alpha_tm", (128, NTC, 8), BF16)
                    c.sb(p2s, "smx", (128, NTC), F32)
                    c.sb(p2s, "ssum", (128, NTC), F32)
                    c.sb(p2s, "srec", (128, NTC), F32)
                    c.sb(p2s, "alpha_fm", (8, T), BF16)
                    c.sb(p2s, "ab_sb", (128, 512), BF16)
                    c.sb(p2s, "wvx", (128, 4, T), BF16)
                    b["pbt"] = p2s.enter_context(nc.psum_tensor("pbt", [128, 1024], BF16))
                    res_p1 = c.cnt["res"]

                    def g_fconsts(g):
                        c.wait(g, "res", res_p1)
                        for dst, srcn in (("w2sb", "in_fW2"), ("fb1", "in_fb1"),
                                          ("b2b", "in_fb2b"), ("e8sb", "in_E8"),
                                          ("idnt", "in_idn")):
                            if dst == "w2sb":
                                ins = g.dma_start(out=b[dst][:], in_=b[srcn].rearrange("(dc p) k -> p dc k", p=128))
                            else:
                                ins = g.dma_start(out=b[dst][:], in_=b[srcn])
                            c.inc(ins, "g_ld", 16)
                        ins = g.dma_start(out=b["vtsb"][:], in_=b["in_VT"].rearrange("(dc p) kr -> p dc kr", p=128),
                                          max_dma_last_dim=2048)
                        c.inc(ins, "g_ld", 16)
                        for kc in range(4):
                            ins = g.dma_start(out=b["utsb"][:, kc, :],
                                              in_=b["in_UTf"][kc * 128:(kc + 1) * 128, :],
                                              max_dma_last_dim=4096)
                            c.inc(ins, "g_ld", 16)

                    c.sec("gpsimd", g_fconsts)
                    emit_p2(c)

                with contextlib.ExitStack() as p3s:
                    c.sb(p3s, "wq3", (128, 2, DC, 128), BF16)
                    c.sb(p3s, "wout", (128, 2, T), BF16)
                    c.sb(p3s, "vout", (128, 2, T), BF16)
                    c.sb(p3s, "hbuf", (128, 2, 4, T), BF16)
                    c.sb(p3s, "sq2_scr", (128, 4, T), BF16)
                    c.sb(p3s, "nrm", (128, T), F32)
                    c.sb(p3s, "rstd", (128, T), F32)
                    c.sb(p3s, "qks", (128, 16, 2), F32)
                    c.sb(p3s, "qko", (128, 16, 2), F32)
                    c.sb(p3s, "s0e", (128, 16), F32)
                    c.sb(p3s, "o0e", (128, 16), F32)
                    c.sb(p3s, "wsm", (128, DC, 20), BF16)
                    c.sb(p3s, "spa", (12, T), F32)
                    c.sb(p3s, "spb", (12, T), F32)
                    c.sb(p3s, "lr_sb", (12, T), F32)
                    c.sb(p3s, "mom_sb", (4, T), F32)
                    c.sb(p3s, "bm1", (4, 1), F32)
                    c.sb(p3s, "bs1", (4, 1), F32)
                    emit_p3(c)
                    if DEBUG_DUMP:
                        def sy_dbg3(sy):
                            c.wait(sy, "cc", c.cnt["cc"])
                            ins = sy.dma_start(out=b["dbg_scan"], in_=b["cc3_in"].ap())
                            c.inc(ins, "wr", 16)
                        c.sec("sync", sy_dbg3)

                dve_a = c.cnt["dve"]
                act_a = c.cnt["act"]

                def sy_sp(sy):
                    c.wait(sy, "dve", dve_a)
                    c.wait(sy, "act", act_a)
                    ins = sy.dma_start(out=b["rspill"].ap(),
                                       in_=b["R"][:].rearrange("p (dc t) -> p dc t", dc=DC))
                    c.inc(ins, "s_sp", 16)

                c.sec("sync", sy_sp)
                sp_done = c.cnt["s_sp"]

                def v_spgate(v):
                    c.wait(v, "s_sp", sp_done)

                c.sec("vector", v_spgate)

            with contextlib.ExitStack() as p4s:
                c.sb(p4s, "wm", (128, 8, 1152), BF16)
                c.sb(p4s, "aqh", (128, 2, T), BF16)
                c.sb(p4s, "kwin", (128, 2, 2048), BF16)
                c.sb(p4s, "avwin", (128, 2, 2048), BF16)
                c.sb(p4s, "avtm", (128, 2048), BF16)
                c.sb(p4s, "probs", (128, 1152), BF16)
                c.sb(p4s, "probs_n", (128, 1152), BF16)
                c.sb(p4s, "probsT", (128, 1152), BF16)
                c.sb(p4s, "scores", (128, 1152), F32)
                c.sb(p4s, "wmax", (128, 1), F32)
                c.sb(p4s, "wmaxn", (128, 1), F32)
                c.sb(p4s, "wsum", (128, 1), F32)
                c.sb(p4s, "wsum2", (128, 1), F32)
                c.sb(p4s, "wrs", (128, 1), F32)
                c.sb(p4s, "wo_sb", (128, 2, T), BF16)
                c.sb(p4s, "idn4", (128, 128), BF16)
                b["pbt2"] = p4s.enter_context(nc.psum_tensor("pbt2", [128, 2048], BF16))
                emit_p4(c)
                if DEBUG_DUMP:
                    def sy_dbg4(sy):
                        c.wait(sy, "s_stw", c.cnt["s_stw"])
                        ins = sy.dma_start(out=b["dbg_attn"], in_=b["attn_loc"].ap())
                        c.inc(ins, "wr", 16)
                        ins = sy.dma_start(out=b["dbg_prev"], in_=b["prevloc"].ap())
                        c.inc(ins, "wr", 16)
                        ins = sy.dma_start(out=b["dbg_wloc"], in_=b["wloc"].ap())
                        c.inc(ins, "wr", 16)
                    c.sec("sync", sy_dbg4)

            with contextlib.ExitStack() as p5s:
                c.sb(p5s, "qiT", (128, 4, T), BF16)
                c.sb(p5s, "kiT", (128, 4, T), BF16)
                c.sb(p5s, "vi", (128, 4, T), BF16)
                b["wT"] = [p5s.enter_context(nc.sbuf_tensor(f"wT{w}", [128, 4, 512], F32))
                           for w in range(3)]
                c.sb(p5s, "w1n", (128, 4, 512), F32)
                b["wTb"] = [p5s.enter_context(nc.sbuf_tensor(f"wTb{w}", [128, 4, 512], BF16))
                            for w in range(3)]
                c.sb(p5s, "w1nb", (128, 4, 512), BF16)
                b["nwb"] = [p5s.enter_context(nc.sbuf_tensor(f"nwb{w}", [128, 512], F32))
                            for w in range(3)]
                c.sb(p5s, "n1r", (128, 4), F32)
                c.sb(p5s, "sq5", (128, 4, 512), BF16)
                c.sb(p5s, "scb", (4, T), F32)
                c.sb(p5s, "lmt", (128, 8, 4), F32)
                c.sb(p5s, "mmr", (128, 1), F32)
                c.sb(p5s, "mm1", (1, 1), F32)
                c.sb(p5s, "mmb", (128, 1), F32)
                c.sb(p5s, "scr5", (128, 2, 512), BF16)
                c.sb(p5s, "gh", (128, 4, T), BF16)
                c.sb(p5s, "ocb", (128, 4, T), BF16)
                for n in ("gba", "sg", "hbm", "dhid", "dsil"):
                    c.sb(p5s, n, (128, 8, 512), BF16)
                c.sb(p5s, "X", (128, 4, 512), F32)
                c.sb(p5s, "Xb", (128, 4, 512), BF16)
                c.sb(p5s, "XTb", (128, 4, 512), BF16)
                c.sb(p5s, "Ab", (128, 4, 512), BF16)
                c.sb(p5s, "Bb", (128, 4, 512), BF16)
                c.sb(p5s, "X2", (128, 4, 512), F32)
                c.sb(p5s, "Xb2", (128, 4, 512), BF16)
                c.sb(p5s, "XTb2", (128, 4, 512), BF16)
                c.sb(p5s, "mst", (128, 4, 512), F32)
                c.sb(p5s, "scr5f", (128, 512), F32)
                c.sb(p5s, "fr1", (128, 1), F32)
                c.sb(p5s, "fr2", (128, 1), F32)
                c.sb(p5s, "rc1", (128, 512), F32)
                c.sb(p5s, "rc2", (128, 512), F32)
                c.sb(p5s, "r1c", (128, 4), F32)
                c.sb(p5s, "r1d", (128, 4), F32)
                c.sb(p5s, "idn5", (128, 128), BF16)
                c.sb(p5s, "idn5f", (128, 128), F32)
                b["hl1"] = b["gh"][:].rearrange("p a (x t) -> p (a x) t", x=2)
                b["Ab2"] = b["vi"][:, :, 0:512]
                b["Bb2"] = b["vi"][:, :, 512:1024]
                b["pbt5"] = p5s.enter_context(nc.psum_tensor("pbt5", [128, 1024], BF16))
                b["pbt5c"] = p5s.enter_context(nc.psum_tensor("pbt5c", [128, 1024], BF16))

                stw_all = c.cnt["s_stw"]
                ppv_all = c.cnt["pe_pv"]
                dw_all = c.cnt["dve_w"]
                aw_all = c.cnt["act_w"]

                def g_idn5(g):
                    c.wait(g, "s_stw", stw_all)
                    c.wait(g, "pe_pv", ppv_all)
                    c.wait(g, "dve_w", dw_all)
                    c.wait(g, "act_w", aw_all)
                    ins = g.dma_start(out=b["idn5"][:], in_=b["in_idn"])
                    c.inc(ins, "g_ld", 16)
                    ins = g.dma_start(out=b["idn5f"][:], in_=b["in_idn"])
                    c.inc(ins, "g_ld", 16)

                c.sec("gpsimd", g_idn5)
                emit_p5(c)

            with contextlib.ExitStack() as p5bs:
                c.sb(p5bs, "R", (128, DC * T), F32)
                with contextlib.ExitStack() as pb1:
                    c.sb(pb1, "zo", (128, DC, T), BF16)
                    c.sb(pb1, "wf5", (128, 2, DC, 128), BF16)
                    c.sb(pb1, "ttth", (128, 2, 4, T), BF16)
                    c.sb(pb1, "attnh", (128, 2, 4, T), BF16)
                    c.sb(pb1, "sqo", (128, 2, T), BF16)
                    c.sb(pb1, "nrm5", (128, T), F32)
                    c.sb(pb1, "rstdb", (128, T), F32)
                    c.sb(pb1, "scb5", (128, T), F32)
                    c.sb(pb1, "t5", (128, T), F32)
                    c.sb(pb1, "ong", (128, 4), F32)
                    c.sb(pb1, "sc_row", (1, T), F32)
                    emit_p5b(c)
                    if DEBUG_DUMP:
                        def sy_dbg5(sy):
                            c.wait(sy, "res", c.cnt["res"])
                            ins = sy.dma_start(out=b["dbg_tt"], in_=b["ttloc"].ap())
                            c.inc(ins, "wr", 16)
                        c.sec("sync", sy_dbg5)

                with contextlib.ExitStack() as pb2:
                    c.sb(pb2, "xn", (128, DC, T), BF16)
                    c.sb(pb2, "sq_scr", (128, 2, T), BF16)
                    c.sb(pb2, "rb_scr", (128, 2, T), BF16)
                    for n in ("lnm", "lnv", "lnr", "t1"):
                        c.sb(pb2, n, (128, T), F32)
                    c.sb(pb2, "h_sb", (128, DC, T), BF16)
                    c.sb(pb2, "wf", (128, 1, DC, 128), BF16)
                    c.sb(pb2, "w2sb", (128, DC, 8), BF16)
                    c.sb(pb2, "fb1", (128, 16), F32)
                    c.sb(pb2, "b2b", (128, 8), F32)
                    c.sb(pb2, "vtsb", (128, DC, 512), BF16)
                    c.sb(pb2, "utsb", (128, 4, D), BF16)
                    c.sb(pb2, "e8sb", (8, 512), BF16)
                    c.sb(pb2, "idnt", (128, 128), BF16)
                    c.sb(pb2, "lg_tm", (128, NTC, 8), F32)
                    c.sb(pb2, "alpha_tm", (128, NTC, 8), BF16)
                    c.sb(pb2, "smx", (128, NTC), F32)
                    c.sb(pb2, "ssum", (128, NTC), F32)
                    c.sb(pb2, "srec", (128, NTC), F32)
                    c.sb(pb2, "alpha_fm", (8, T), BF16)
                    c.sb(pb2, "ab_sb", (128, 512), BF16)
                    c.sb(pb2, "wvx", (128, 4, T), BF16)
                    b["pbt"] = pb2.enter_context(nc.psum_tensor("pbt5b", [128, 1024], BF16))
                    res_5b = c.cnt["res"]

                    def g_fc2(g):
                        c.wait(g, "res", res_5b)
                        for dst, srcn in (("w2sb", "in_fW2"), ("fb1", "in_fb1"),
                                          ("b2b", "in_fb2b"), ("e8sb", "in_E8"),
                                          ("idnt", "in_idn")):
                            if dst == "w2sb":
                                ins = g.dma_start(out=b[dst][:], in_=b[srcn].rearrange("(dc p) k -> p dc k", p=128))
                            else:
                                ins = g.dma_start(out=b[dst][:], in_=b[srcn])
                            c.inc(ins, "g_ld", 16)
                        ins = g.dma_start(out=b["vtsb"][:], in_=b["in_VT"].rearrange("(dc p) kr -> p dc kr", p=128),
                                          max_dma_last_dim=2048)
                        c.inc(ins, "g_ld", 16)
                        for kc in range(4):
                            ins = g.dma_start(out=b["utsb"][:, kc, :],
                                              in_=b["in_UTf"][kc * 128:(kc + 1) * 128, :],
                                              max_dma_last_dim=4096)
                            c.inc(ins, "g_ld", 16)

                    c.sec("gpsimd", g_fc2)
                    emit_p2(c)

                def sy_out(sy):
                    c.wait(sy, "res", c.cnt["res"])
                    ins = sy.dma_start(out=b["yT"].rearrange("(dc p) t -> p dc t", p=128),
                                       in_=b["R"][:].rearrange("p (dc t) -> p dc t", dc=DC))
                    c.inc(ins, "wr", 16)
                    c.wait(sy, "wr")

                c.sec("sync", sy_out)
    return nc


def kernel(**inputs):
    x = np.asarray(inputs["L0_attn_a_x"])
    Wq = np.asarray(inputs["L0_attn_a_Wq"]); bq = np.asarray(inputs["L0_attn_a_bq"])
    Wk = np.asarray(inputs["L0_attn_a_Wk"]); bk = np.asarray(inputs["L0_attn_a_bk"])
    Wv = np.asarray(inputs["L0_attn_a_Wv"]); bv = np.asarray(inputs["L0_attn_a_bv"])
    Wg = np.asarray(inputs["L0_attn_a_Wg"]); bg = np.asarray(inputs["L0_attn_a_bg"])
    fU = np.asarray(inputs["L0_ffn_U"]); fV = np.asarray(inputs["L0_ffn_V"])
    fW1 = np.asarray(inputs["L0_ffn_W1"]); fb1 = np.asarray(inputs["L0_ffn_b1"])
    fW2 = np.asarray(inputs["L0_ffn_W2"]); fb2 = np.asarray(inputs["L0_ffn_b2"])
    Wqkv = np.asarray(inputs["L0_attn_b_Wqkv"])
    Wo = np.asarray(inputs["L0_attn_b_Wo"])
    w0 = np.asarray(inputs["L0_attn_b_w0"]); w1 = np.asarray(inputs["L0_attn_b_w1"])
    w2 = np.asarray(inputs["L0_attn_b_w2"]); ong = np.asarray(inputs["L0_attn_b_o_norm_g"])
    qk_s = np.asarray(inputs["L0_attn_b_qk_scale"]); qk_o = np.asarray(inputs["L0_attn_b_qk_offset"])
    Wlr = np.asarray(inputs["L0_attn_b_Wlr"]); Wm = np.asarray(inputs["L0_attn_b_Wm"])
    Ws = np.asarray(inputs["L0_attn_b_Ws"]); bm = np.asarray(inputs["L0_attn_b_bm"])
    bs = np.asarray(inputs["L0_attn_b_bs"])

    nc = build_nc()
    bkv = np.zeros((128, 16, 256), np.float32)
    for blk in range(16):
        bkv[:, blk, 0:128] = bk[blk * 128:(blk + 1) * 128][None, :]
        bkv[:, blk, 128:256] = bv[blk * 128:(blk + 1) * 128][None, :]
    ut = np.triu(np.ones((512, 512), np.float32))
    utmix = np.ascontiguousarray(ut.reshape(4, 128, 512).reshape(512, 512))
    shared = {
        "in_aWqT": np.ascontiguousarray(Wq.T),
        "in_aWkT": np.ascontiguousarray(Wk.T),
        "in_aWvT": np.ascontiguousarray(Wv.T),
        "in_aWgT": np.ascontiguousarray(Wg.T),
        "in_bq16": np.ascontiguousarray(bq.reshape(16, 128).T),
        "in_bg16": np.ascontiguousarray(bg.reshape(16, 128).T),
        "in_bkv": bkv.reshape(128, 16 * 256),
        "in_UTmix": utmix,
        "in_fW1": np.ascontiguousarray(fW1),
        "in_fb1": np.ascontiguousarray(fb1.reshape(16, 128).T),
        "in_fW2": np.ascontiguousarray(fW2),
        "in_fb2b": np.ascontiguousarray(np.tile(fb2, (128, 1))),
        "in_VT": np.ascontiguousarray(fV.transpose(2, 0, 1).reshape(D, 512)),
        "in_UTf": np.ascontiguousarray(fU.transpose(0, 2, 1).reshape(512, D)),
        "in_E8": np.ascontiguousarray((np.arange(512)[None, :] // 64 == np.arange(8)[:, None]).astype(np.float32)),
        "in_idn": np.eye(128, dtype=np.float32),
        "in_Wqkv": np.ascontiguousarray(Wqkv),
        "in_qks": np.ascontiguousarray(qk_s.reshape(16, 128, 2).transpose(1, 0, 2)),
        "in_qko": np.ascontiguousarray(qk_o.reshape(16, 128, 2).transpose(1, 0, 2)),
        "in_bm": np.ascontiguousarray(bm.reshape(4, 1)),
        "in_bs": np.ascontiguousarray(bs.reshape(4, 1)),
        "in_Wsm": np.ascontiguousarray(np.concatenate([Wlr, Wm, Ws], axis=1)),
        "in_ong": np.ascontiguousarray(ong.reshape(4, 128).T),
        "in_Wo": np.ascontiguousarray(Wo),
    }
    in_maps = []
    for cid in range(NCORES):
        m = dict(shared)
        m["xT"] = np.ascontiguousarray(x[0, cid * T:(cid + 1) * T, :].T)
        m8 = np.zeros((NCORES, 1), np.float32)
        m8[:cid, 0] = 1.0
        m["in_m8"] = m8
        t_idx = np.arange(128)[:, None]
        rk = np.arange(1152)[None, :]
        wm = np.zeros((8, 128, 1152), np.float32)
        for i in range(8):
            valid = (rk >= t_idx) & (rk <= t_idx + 1024)
            if cid == 0:
                valid = valid & ((128 * i + rk) >= 1024)
            wm[i] = np.where(valid, 0.0, -1e30)
        m["in_wm"] = wm
        dyn = np.zeros((8, 1), np.int32)
        dyn[0, 0] = (cid // 2) * 1536
        dyn[1, 0] = ((cid + 7) % 8) * 2 * D
        dyn[2, 0] = cid * T
        dyn[3, 0] = (cid // 2) * 4
        dyn[4, 0] = cid * 3 * D
        dyn[5, 0] = cid * 16
        m["in_dyn"] = dyn
        H = cid // 2
        m["in_wT"] = np.ascontiguousarray(np.concatenate(
            [w0[H].T, w1[H].T, w2[H].T], axis=0))
        m["in_w1n"] = np.ascontiguousarray(w1[H])
        in_maps.append(m)
    res = run_bass_kernel_spmd(nc, in_maps, core_ids=list(range(NCORES)))
    out = np.empty((1, NCORES * T, D), np.float32)
    for cid in range(NCORES):
        out[0, cid * T:(cid + 1) * T, :] = res.results[cid]["yT"].T
    return out


# version 11
# speedup vs baseline: 1.1708x; 1.0054x over previous
import contextlib
import os
import numpy as np
import ml_dtypes
import concourse.bass as bass
import concourse.mybir as mybir
from concourse.bass_utils import run_bass_kernel_spmd
from concourse.aot_env import allow_ici_all_to_all

F32 = mybir.dt.float32
BF16 = mybir.dt.bfloat16
AF = mybir.ActivationFunctionType
ALU = mybir.AluOpType
AXX = mybir.AxisListType.X

NCORES = 8
T = 1024
D = 2048
DC = D // 128
NTC = T // 128
bfdt = ml_dtypes.bfloat16

BUILD_STAGE = 1
DEBUG_DUMP = bool(int(os.environ.get("MK_DEBUG", "0")))
P3_PART = int(os.environ.get("MK_P3PART", "4"))
P3X = int(os.environ.get("MK_P3X", "7"))


class Ctx:
    def __init__(self, nc, block, stack):
        self.nc = nc
        self.block = block
        self.stack = stack
        self.sem = {}
        self.cnt = {}
        self.buf = {}

    def new_sem(self, name):
        self.sem[name] = self.stack.enter_context(self.nc.semaphore(name))
        self.cnt[name] = 0

    def inc(self, instr, name, n=1):
        instr.then_inc(self.sem[name], n)
        self.cnt[name] += n
        return self.cnt[name]

    def wait(self, eng, name, v=None):
        eng.wait_ge(self.sem[name], self.cnt[name] if v is None else v)

    def sec(self, engine_name, fn):
        getattr(self.block, engine_name)(fn)

    _uniq = [0]

    def sb(self, stack, name, shape, dt):
        self._uniq[0] += 1
        h = stack.enter_context(self.nc.sbuf_tensor(f"{name}_u{self._uniq[0]}", list(shape), dt))
        self.buf[name] = h
        return h


def emit_p0(c):
    b = c.buf

    def sy(sy):
        ins = sy.dma_start(out=b["R"][:].rearrange("p (dc t) -> p dc t", dc=DC),
                           in_=b["xT"].rearrange("(dc p) t -> p dc t", p=128))
        c.inc(ins, "s_ld", 16)

    def v(v):
        v.memset(b["ones128"][:], 1.0)
        v.memset(b["ones1"][:], 1.0)
        v.memset(b["c_one"][:], 1.0)
        v.memset(b["c_lrb"][:], float(0.01 + np.log(-np.expm1(-0.01))))
        ins = v.memset(b["c_eps"][:], 1e-5)
        c.inc(ins, "init", 1)

    c.sec("sync", sy)
    c.sec("vector", v)


def emit_ln(c, dst_name):
    b = c.buf
    res0 = c.cnt["res"]
    act0 = c.cnt["act"]
    dve0 = c.cnt["dve"]
    pe0 = c.cnt["pe"]

    def s_sq(s):
        c.wait(s, "res", res0)
        c.wait(s, "s_ld", 16)
        for dc in range(DC):
            if dc >= 2:
                c.wait(s, "pe", pe0 + dc - 1)
            ins = s.activation(out=b["sq_scr"][:, dc % 2, :], in_=b["R"][:, dc * T:(dc + 1) * T],
                               func=AF.Square, scale=1.0)
            c.inc(ins, "act", 1)

    def v_rb(v):
        c.wait(v, "res", res0)
        c.wait(v, "s_ld", 16)
        for dc in range(DC):
            if dc >= 2:
                c.wait(v, "pe", pe0 + dc - 1)
            ins = v.tensor_copy(out=b["rb_scr"][:, dc % 2, :], in_=b["R"][:, dc * T:(dc + 1) * T])
            c.inc(ins, "dve", 1)

    c.sec("scalar", s_sq)
    c.sec("vector", v_rb)

    def t_sums(t):
        for dc in range(DC):
            c.wait(t, "act", act0 + dc + 1)
            c.wait(t, "dve", dve0 + dc + 1)
            for j in range(2):
                t.matmul(b["pb"][j][:], b["ones128"][:],
                         b["rb_scr"][:, dc % 2, j * 512:(j + 1) * 512],
                         start=(dc == 0), stop=(dc == DC - 1))
                ins = t.matmul(b["pb"][2 + j][:], b["ones128"][:],
                               b["sq_scr"][:, dc % 2, j * 512:(j + 1) * 512],
                               start=(dc == 0), stop=(dc == DC - 1))
            c.inc(ins, "pe", 1)

    c.sec("tensor", t_sums)
    pe1 = c.cnt["pe"]

    def v_stats(v):
        c.wait(v, "pe", pe1)
        for j in range(2):
            v.tensor_scalar(out=b["lnm"][:, j * 512:(j + 1) * 512], in0=b["pb"][j][:],
                            scalar1=1.0 / D, scalar2=None, op0=ALU.mult)
        v.tensor_tensor(out=b["lnv"][:], in0=b["lnm"][:], in1=b["lnm"][:], op=ALU.mult)
        for j in range(2):
            ins = v.scalar_tensor_tensor(out=b["lnv"][:, j * 512:(j + 1) * 512],
                                         in0=b["pb"][2 + j][:], scalar=1.0 / D,
                                         in1=b["lnv"][:, j * 512:(j + 1) * 512],
                                         op0=ALU.mult, op1=ALU.subtract)
        c.inc(ins, "dve", 1)

    c.sec("vector", v_stats)
    dve2 = c.cnt["dve"]

    def s_rstd(s):
        c.wait(s, "dve", dve2)
        ins = s.activation(out=b["lnv"][:], in_=b["lnv"][:], func=AF.Sqrt, scale=1.0,
                           bias=b["c_eps"][:])
        c.inc(ins, "act", 1)

    c.sec("scalar", s_rstd)
    act2 = c.cnt["act"]

    def v_norm(v):
        c.wait(v, "act", act2)
        v.reciprocal(out=b["lnr"][:], in_=b["lnv"][:])
        for dc in range(DC):
            v.tensor_tensor(out=b["t1"][:], in0=b["R"][:, dc * T:(dc + 1) * T],
                            in1=b["lnm"][:], op=ALU.subtract)
            ins = v.tensor_tensor(out=b[dst_name][:, dc, :], in0=b["t1"][:],
                                  in1=b["lnr"][:], op=ALU.mult)
        c.inc(ins, "dve", 1)

    c.sec("vector", v_norm)


def emit_p1_B(c):
    b = c.buf
    emit_ln(c, "xn")
    ln_done = c.cnt["dve"]

    NB = 16
    gld0 = c.cnt["g_ld"]
    pe_kv0 = c.cnt["pe"]

    def g_ldkv(g):
        for blk in range(NB):
            if blk >= 1:
                c.wait(g, "pe", pe_kv0 + blk * NTC)
            ins = g.dma_start(
                out=b["wkv"][:, blk % 2, 0, :, :],
                in_=b["in_aWkT"][:, blk * 128:(blk + 1) * 128].rearrange("(dc p) e -> p dc e", p=128),
                max_dma_last_dim=2048)
            c.inc(ins, "g_ld", 16)
            ins = g.dma_start(
                out=b["wkv"][:, blk % 2, 1, :, :],
                in_=b["in_aWvT"][:, blk * 128:(blk + 1) * 128].rearrange("(dc p) e -> p dc e", p=128),
                max_dma_last_dim=2048)
            c.inc(ins, "g_ld", 16)

    c.sec("gpsimd", g_ldkv)

    def t_kv(t):
        c.wait(t, "dve", ln_done)
        n = 0
        for blk in range(NB):
            c.wait(t, "g_ld", gld0 + 32 * (blk + 1))
            for tc in range(NTC):
                bank = b["pb"][n % 4]
                if n >= 4:
                    c.wait(t, "dve_kv", n - 3)
                for kv in range(2):
                    for dc in range(DC):
                        ins = t.matmul(bank[:, kv * 128:(kv + 1) * 128],
                                       b["xn"][:, dc, tc * 128:(tc + 1) * 128],
                                       b["wkv"][:, blk % 2, kv, dc, :],
                                       start=(dc == 0), stop=(dc == DC - 1))
                c.inc(ins, "pe", 1)
                n += 1

    c.sec("tensor", t_kv)

    def v_kv(v):
        n = 0
        for blk in range(NB):
            for tc in range(NTC):
                bank = b["pb"][n % 4]
                c.wait(v, "pe", pe_kv0 + n + 1)
                v.tensor_tensor(out=b["kv_scr"][:, n % 2, :], in0=bank[:, 0:256],
                                in1=b["bkv"][:, blk * 256:(blk + 1) * 256], op=ALU.add)
                ins = v.tensor_tensor(out=b["kv_tm"][:, tc, blk * 128:(blk + 1) * 128],
                                      in0=b["kv_scr"][:, n % 2, 0:128],
                                      in1=b["kv_scr"][:, n % 2, 128:256], op=ALU.mult)
                c.inc(ins, "dve_kv", 1)
                n += 1

    c.sec("vector", v_kv)
    dve_kv_b = c.cnt["dve_kv"]

    def t_tot(t):
        c.wait(t, "dve_kv", dve_kv_b)
        for col in range(4):
            for tc in range(NTC):
                ins = t.matmul(b["pb"][col][:1, :], b["ones1"][:],
                               b["kv_tm"][:, tc, col * 512:(col + 1) * 512],
                               start=(tc == 0), stop=(tc == NTC - 1))
        c.inc(ins, "pe", 1)

    c.sec("tensor", t_tot)
    pe_tot = c.cnt["pe"]

    def v_tot(v):
        c.wait(v, "pe", pe_tot)
        for col in range(4):
            ins = v.tensor_copy(out=b["tot_row"][:1, col * 512:(col + 1) * 512],
                                in_=b["pb"][col][:1, :])
        c.inc(ins, "dve_tot", 1)

    c.sec("vector", v_tot)
    dve_tot = c.cnt["dve_tot"]

    def g_cc1(g):
        c.wait(g, "dve_tot", dve_tot)
        ins = g.dma_start(out=b["cc1_in"].ap().rearrange("(one d) x -> one (d x)", one=1),
                          in_=b["tot_row"][:1, :])
        c.inc(ins, "g_ld", 16)
        c.wait(g, "g_ld")
        ins = g.collective_compute("AllGather", ALU.bypass,
                                   replica_groups=[list(range(NCORES))],
                                   ins=[b["cc1_in"].ap().opt()], outs=[b["cc1_out"].ap().opt()])
        c.inc(ins, "cc", 1)
        c.wait(g, "cc")
        ins = g.dma_start(out=b["tot_all"][:, :],
                          in_=b["cc1_out"].ap().rearrange("(r d) one -> r (d one)", r=NCORES))
        c.inc(ins, "g_ld", 16)

    c.sec("gpsimd", g_cc1)
    gld_cc1 = c.cnt["g_ld"]

    def t_offs(t):
        c.wait(t, "g_ld", gld_cc1)
        for dc in range(DC):
            ins = t.matmul(b["pb"][4][:, dc:dc + 1], b["tot_all"][:, dc * 128:(dc + 1) * 128],
                           b["m8"][:, :], start=True, stop=True)
        c.inc(ins, "pe", 1)

    c.sec("tensor", t_offs)
    pe_offs = c.cnt["pe"]

    def v_offs(v):
        c.wait(v, "pe", pe_offs)
        ins = v.tensor_copy(out=b["offs"][:, :], in_=b["pb"][4][:, :DC])
        c.inc(ins, "dve_offs", 1)

    c.sec("vector", v_offs)


def emit_p1_C(c):
    b = c.buf
    gld1 = c.cnt["g_ld"]
    pe_c0 = c.cnt["pe"]
    dve_kv_all = c.cnt["dve_kv"]

    def g_ldqg(g):
        for ec in range(DC):
            if ec == 0:
                c.wait(g, "dve_kv", dve_kv_all)
            if ec >= 1:
                c.wait(g, "pe", pe_c0 + 2 * ec)
            ins = g.dma_start(
                out=b["wqg"][:, ec % 2, 0, :, :],
                in_=b["in_aWqT"][:, ec * 128:(ec + 1) * 128].rearrange("(dc p) e -> p dc e", p=128),
                max_dma_last_dim=2048)
            c.inc(ins, "g_ld", 16)
            ins = g.dma_start(
                out=b["wqg"][:, ec % 2, 1, :, :],
                in_=b["in_aWgT"][:, ec * 128:(ec + 1) * 128].rearrange("(dc p) e -> p dc e", p=128),
                max_dma_last_dim=2048)
            c.inc(ins, "g_ld", 16)

    c.sec("gpsimd", g_ldqg)

    def t_qg(t):
        for ec in range(DC):
            c.wait(t, "g_ld", gld1 + 32 * (ec + 1))
            for j in range(2):
                if ec >= 1:
                    c.wait(t, "dve_c", 2 * (ec - 1) + j + 1)
                for dc in range(DC):
                    t.matmul(b["pb"][j][:], b["wqg"][:, ec % 2, 0, dc, :],
                             b["xn"][:, dc, j * 512:(j + 1) * 512],
                             start=(dc == 0), stop=(dc == DC - 1))
                for dc in range(DC):
                    t.matmul(b["pb"][2 + j][:], b["wqg"][:, ec % 2, 1, dc, :],
                             b["xn"][:, dc, j * 512:(j + 1) * 512],
                             start=(dc == 0), stop=(dc == DC - 1))
                first = True
                for tc in range(4 * j):
                    t.matmul(b["pb"][4 + j][:],
                             b["kv_tm"][:, tc, ec * 128:(ec + 1) * 128],
                             b["ones512"][:],
                             start=first, stop=False)
                    first = False
                for i, tc in enumerate(range(4 * j, 4 * j + 4)):
                    ins = t.matmul(b["pb"][4 + j][:],
                                   b["kv_tm"][:, tc, ec * 128:(ec + 1) * 128],
                                   b["UTmix"][:, tc % 4, :],
                                   start=first, stop=(i == 3))
                    first = False
                c.inc(ins, "pe", 1)

    c.sec("tensor", t_qg)

    def s_g(s):
        for ec in range(DC):
            for j in range(2):
                c.wait(s, "pe", pe_c0 + 2 * ec + j + 1)
                ins = s.activation(out=b["g_sb"][:, j * 512:(j + 1) * 512], in_=b["pb"][2 + j][:],
                                   func=AF.Sigmoid, bias=b["bg16"][:, ec:ec + 1], scale=1.0)
                c.inc(ins, "act_g", 1)

    c.sec("scalar", s_g)

    def v_chain(v):
        c.wait(v, "dve_offs", 1)
        for ec in range(DC):
            for j in range(2):
                c.wait(v, "pe", pe_c0 + 2 * ec + j + 1)
                c.wait(v, "act_g", 2 * ec + j + 1)
                sl = slice(j * 512, (j + 1) * 512)
                v.tensor_scalar(out=b["q_sb"][:, sl], in0=b["pb"][j][:],
                                scalar1=b["bq16"][:, ec:ec + 1], scalar2=None, op0=ALU.add)
                v.tensor_scalar(out=b["t1"][:, sl], in0=b["pb"][4 + j][:],
                                scalar1=b["offs"][:, ec:ec + 1], scalar2=None, op0=ALU.add)
                ins = v.tensor_tensor(out=b["t1"][:, sl], in0=b["t1"][:, sl], in1=b["q_sb"][:, sl],
                                      op=ALU.mult)
                c.inc(ins, "dve_c", 1)
                v.tensor_tensor(out=b["t1"][:, sl], in0=b["t1"][:, sl], in1=b["g_sb"][:, sl],
                                op=ALU.mult)
                ins = v.tensor_tensor(out=b["R"][:, ec * T + j * 512: ec * T + (j + 1) * 512],
                                      in0=b["R"][:, ec * T + j * 512: ec * T + (j + 1) * 512],
                                      in1=b["t1"][:, sl], op=ALU.add)
        c.inc(ins, "res", 1)

    c.sec("vector", v_chain)




def emit_p2(c):
    b = c.buf
    emit_ln(c, "xn")
    ln_done = c.cnt["dve"]
    gldw = c.cnt["g_ld"]
    pe_h0 = c.cnt["pe"]
    actg0 = c.cnt["act_g"]

    def g_ldw1(g):
        for ec in range(DC):
            if ec >= 1:
                c.wait(g, "pe", pe_h0 + 2 * ec)
            ins = g.dma_start(
                out=b["wf"][:, 0, :, :],
                in_=b["in_fW1"][:, ec * 128:(ec + 1) * 128].rearrange("(dc p) e -> p dc e", p=128),
                max_dma_last_dim=2048)
            c.inc(ins, "g_ld", 16)

    c.sec("gpsimd", g_ldw1)

    def t_h(t):
        c.wait(t, "dve", ln_done)
        for ec in range(DC):
            c.wait(t, "g_ld", gldw + 16 * (ec + 1))
            for j in range(2):
                if ec >= 1:
                    c.wait(t, "act_g", actg0 + 2 * (ec - 1) + j + 1)
                for dc in range(DC):
                    ins = t.matmul(b["pb"][j][:], b["wf"][:, 0, dc, :],
                                   b["xn"][:, dc, j * 512:(j + 1) * 512],
                                   start=(dc == 0), stop=(dc == DC - 1))
                c.inc(ins, "pe", 1)

    c.sec("tensor", t_h)

    def s_h(s):
        for ec in range(DC):
            for j in range(2):
                c.wait(s, "pe", pe_h0 + 2 * ec + j + 1)
                ins = s.activation(out=b["h_sb"][:, ec, j * 512:(j + 1) * 512], in_=b["pb"][j][:],
                                   func=AF.Relu, bias=b["fb1"][:, ec:ec + 1], scale=1.0)
                c.inc(ins, "act_g", 1)

    c.sec("scalar", s_h)
    actg_h = c.cnt["act_g"]

    pe_l0 = c.cnt["pe"]
    dve_l0 = c.cnt["dve_l"]

    def t_logits(t):
        c.wait(t, "act_g", actg_h)
        for tc in range(NTC):
            if tc >= 1:
                c.wait(t, "dve_l", dve_l0 + tc)
            for ec in range(DC):
                ins = t.matmul(b["pb"][2][:, 0:8], b["h_sb"][:, ec, tc * 128:(tc + 1) * 128],
                               b["w2sb"][:, ec, :], start=(ec == 0), stop=(ec == DC - 1))
            c.inc(ins, "pe", 1)

    c.sec("tensor", t_logits)

    def v_logits(v):
        for tc in range(NTC):
            c.wait(v, "pe", pe_l0 + tc + 1)
            ins = v.tensor_tensor(out=b["lg_tm"][:, tc, :], in0=b["pb"][2][:, 0:8],
                                  in1=b["b2b"][:, :], op=ALU.add)
            c.inc(ins, "dve_l", 1)

    c.sec("vector", v_logits)

    def v_smax1(v):
        for tc in range(NTC):
            ins = v.tensor_reduce(out=b["smx"][:, tc:tc + 1], in_=b["lg_tm"][:, tc, :],
                                  axis=AXX, op=ALU.max, negate=True)
            c.inc(ins, "dve_s", 1)

    c.sec("vector", v_smax1)
    dve_s1 = c.cnt["dve_s"]

    def s_sexp(s):
        c.wait(s, "dve_s", dve_s1)
        for tc in range(NTC):
            s.activation(out=b["alpha_tm"][:, tc, :], in_=b["lg_tm"][:, tc, :],
                         func=AF.Exp, bias=b["smx"][:, tc:tc + 1], scale=1.0,
                         accum_out=b["ssum"][:, tc:tc + 1])
            ins = s.activation(out=b["ssum"][:, tc:tc + 1], in_=b["ssum"][:, tc:tc + 1],
                               func=AF.Copy, scale=1.0)
            c.inc(ins, "act_s", 1)

    c.sec("scalar", s_sexp)
    act_s1 = c.cnt["act_s"]

    def v_salpha(v):
        c.wait(v, "act_s", act_s1)
        ins = v.reciprocal(out=b["srec"][:, :], in_=b["ssum"][:, :])
        c.inc(ins, "dve_s", 1)
        c.wait(v, "dve_s")
        for tc in range(NTC):
            ins = v.tensor_scalar(out=b["alpha_tm"][:, tc, :], in0=b["alpha_tm"][:, tc, :],
                                  scalar1=b["srec"][:, tc:tc + 1], scalar2=None, op0=ALU.mult)
        c.inc(ins, "dve_s", 1)

    c.sec("vector", v_salpha)
    dve_s2 = c.cnt["dve_s"]

    def t_atr(t):
        c.wait(t, "dve_s", dve_s2)
        for tc in range(NTC):
            ins = t.transpose(b["pbt"][0:8, tc * 128:(tc + 1) * 128],
                              b["alpha_tm"][:, tc, :], b["idnt"][:])
        c.inc(ins, "pe", 1)

    c.sec("tensor", t_atr)
    pe_atr = c.cnt["pe"]

    def v_afm(v):
        c.wait(v, "pe", pe_atr)
        ins = v.tensor_copy(out=b["alpha_fm"][0:8, :], in_=b["pbt"][0:8, :])
        c.inc(ins, "dve_s", 1)

    c.sec("vector", v_afm)
    dve_s3 = c.cnt["dve_s"]

    pe_x0 = c.cnt["pe"]
    dve_x0 = c.cnt["dve_x"]

    def t_vx(t):
        c.wait(t, "dve_s", dve_s3)
        for krc in range(4):
            for j in range(2):
                if krc >= 1:
                    c.wait(t, "dve_x", dve_x0 + 2 * (krc - 1) + j + 1)
                t.matmul(b["pb"][j][:], b["e8sb"][:, krc * 128:(krc + 1) * 128],
                         b["alpha_fm"][0:8, j * 512:(j + 1) * 512],
                         start=True, stop=True)
                for dc in range(DC):
                    ins = t.matmul(b["pb"][2 + j][:], b["vtsb"][:, dc, krc * 128:(krc + 1) * 128],
                                   b["xn"][:, dc, j * 512:(j + 1) * 512],
                                   start=(dc == 0), stop=(dc == DC - 1))
                c.inc(ins, "pe", 1)

    c.sec("tensor", t_vx)

    def v_wvx(v):
        n = 0
        for krc in range(4):
            for j in range(2):
                c.wait(v, "pe", pe_x0 + n + 1)
                v.tensor_copy(out=b["ab_sb"][:, :], in_=b["pb"][j][:])
                ins = v.tensor_tensor(out=b["wvx"][:, krc, j * 512:(j + 1) * 512],
                                      in0=b["pb"][2 + j][:], in1=b["ab_sb"][:, :], op=ALU.mult)
                c.inc(ins, "dve_x", 1)
                n += 1

    c.sec("vector", v_wvx)
    dve_x1 = c.cnt["dve_x"]

    pe_o0 = c.cnt["pe"]

    def t_out(t):
        c.wait(t, "dve_x", dve_x1)
        for dc in range(DC):
            for j in range(2):
                if dc >= 1:
                    c.wait(t, "dve_x", dve_x1 + 2 * (dc - 1) + j + 1)
                po = b["pb"][4 + j]
                for krc in range(4):
                    ins = t.matmul(po[:], b["utsb"][:, krc, dc * 128:(dc + 1) * 128],
                                   b["wvx"][:, krc, j * 512:(j + 1) * 512],
                                   start=(krc == 0), stop=(krc == 3))
                c.inc(ins, "pe", 1)

    c.sec("tensor", t_out)

    def v_rout(v):
        n = 0
        for dc in range(DC):
            for j in range(2):
                c.wait(v, "pe", pe_o0 + n + 1)
                sl = slice(dc * T + j * 512, dc * T + (j + 1) * 512)
                ins = v.tensor_tensor(out=b["R"][:, sl], in0=b["R"][:, sl],
                                      in1=b["pb"][4 + j][:], op=ALU.add)
                n += 1
                if n < 2 * DC:
                    c.inc(ins, "dve_x", 1)
        c.inc(ins, "res", 1)

    c.sec("vector", v_rout)



def emit_p3(c):
    b = c.buf
    emit_ln(c, "xn")
    ln_done = c.cnt["dve"]
    res_p2 = c.cnt["res"]

    def g_c3(g):
        c.wait(g, "res", res_p2)
        for dst, srcn in (("qks", "in_qks"), ("qko", "in_qko"), ("bm1", "in_bm"),
                          ("bs1", "in_bs"), ("dyn", "in_dyn")):
            ins = g.dma_start(out=b[dst][:], in_=b[srcn])
            c.inc(ins, "g_ld", 16)
        ins = g.dma_start(out=b["wsm"][:], in_=b["in_Wsm"].rearrange("(dc p) e -> p dc e", p=128))
        c.inc(ins, "g_ld", 16)

    c.sec("gpsimd", g_c3)
    gld_c3 = c.cnt["g_ld"]

    def v_c3(v):
        c.wait(v, "g_ld", gld_c3)
        isq = 1.0 / np.sqrt(128.0)
        v.tensor_scalar(out=b["s0e"][:, :], in0=b["qks"][:, :, 0], scalar1=isq, scalar2=None, op0=ALU.mult)
        ins = v.tensor_scalar(out=b["o0e"][:, :], in0=b["qko"][:, :, 0], scalar1=isq, scalar2=None, op0=ALU.mult)
        c.inc(ins, "dve_q", 1)

    c.sec("vector", v_c3)

    if P3_PART < 1:
        return
    pe_q0 = c.cnt["pe"]
    pen0 = c.cnt["pe_n"]
    dveq0 = c.cnt["dve_q"]
    actq0 = c.cnt["act_q"]
    actsq0 = c.cnt["act_sq"]
    actn0 = c.cnt["act_n"]
    dven0 = c.cnt["dve_n"]
    st1_0 = c.cnt["s_st"]
    st2_0 = c.cnt["s_st2"]
    st3_0 = c.cnt["s_st3"]

    def g_ldq(g):
        for eco in range(48):
            if eco >= 1:
                c.wait(g, "pe", pe_q0 + 2 * eco)
            ins = g.dma_start(
                out=b["wq3"][:, eco % 2, :, :],
                in_=b["in_Wqkv"][:, eco * 128:(eco + 1) * 128].rearrange("(dc p) e -> p dc e", p=128),
                max_dma_last_dim=2048)
            c.inc(ins, "g_ld", 16)

    c.sec("gpsimd", g_ldq)

    def t_q(t):
        c.wait(t, "dve", ln_done)
        for eco in range(48):
            c.wait(t, "g_ld", gld_c3 + 16 * (eco + 1))
            for j in range(2):
                if eco >= 1:
                    c.wait(t, "dve_q", dveq0 + 2 * (eco - 1) + j + 1)
                    if P3X & 2:
                        c.wait(t, "act_q", actq0 + 2 * (eco - 1) + j + 1)
                for dc in range(DC):
                    ins = t.matmul(b["pb"][j][:], b["wq3"][:, eco % 2, dc, :],
                                   b["xn"][:, dc, j * 512:(j + 1) * 512],
                                   start=(dc == 0), stop=(dc == DC - 1))
                c.inc(ins, "pe", 1)
            if (P3X & 4) and eco < 32 and eco % 4 == 3:
                h4 = eco // 4
                c.wait(t, "act_sq", actsq0 + (h4 + 1) * 4)
                if h4 >= 1:
                    c.wait(t, "act_n", actn0 + h4)
                for j in range(2):
                    for ch in range(4):
                        ins = t.matmul(b["pb"][2 + j][:], b["ones128"][:],
                                       b["sq2_scr"][:, ch, j * 512:(j + 1) * 512],
                                       start=(ch == 0), stop=(ch == 3))
                    c.inc(ins, "pe_n", 1)

    c.sec("tensor", t_q)

    def v_all(v):
        for eco in range(48):
            w = eco % 2
            if (P3X & 1) and eco >= 2:
                nd = eco + max(0, eco - 16)
                c.wait(v, "s_st", st1_0 + 16 * nd)
            for j in range(2):
                c.wait(v, "pe", pe_q0 + 2 * eco + j + 1)
                sl = slice(j * 512, (j + 1) * 512)
                if eco < 16:
                    ins = v.tensor_scalar(out=b["wout"][:, w, sl], in0=b["pb"][j][:],
                                          scalar1=b["s0e"][:, eco:eco + 1],
                                          scalar2=b["o0e"][:, eco:eco + 1],
                                          op0=ALU.mult, op1=ALU.add)
                elif eco < 32:
                    ins = v.tensor_scalar(out=b["wout"][:, w, sl], in0=b["pb"][j][:],
                                          scalar1=b["qks"][:, eco - 16, 1:2],
                                          scalar2=b["qko"][:, eco - 16, 1:2],
                                          op0=ALU.mult, op1=ALU.add)
                else:
                    ins = v.tensor_copy(out=b["wout"][:, w, sl], in_=b["pb"][j][:])
                c.inc(ins, "dve_q", 1)
            if (P3X & 4) and eco < 32 and eco % 4 == 3:
                h4 = eco // 4
                c.wait(v, "act_n", actn0 + h4 + 1)
                v.tensor_scalar(out=b["nrm"][:, :], in0=b["nrm"][:, :], scalar1=1e-5,
                                scalar2=None, op0=ALU.add)
                ins = v.reciprocal(out=b["rstd"][:, :], in_=b["nrm"][:, :])
                c.inc(ins, "dve_n0", 1)
                c.wait(v, "dve_n0")
                for ch in range(4):
                    ins = v.tensor_tensor(out=b["hbuf"][:, h4 % 2, ch, :],
                                          in0=b["hbuf"][:, h4 % 2, ch, :],
                                          in1=b["rstd"][:, :], op=ALU.mult)
                    c.inc(ins, "dve_n", 1)

    c.sec("vector", v_all)

    def s_q(s):
        if not (P3X & 2):
            return
        for eco in range(48):
            if eco < 32:
                hb = (eco // 4) % 2
                if (P3X & 4) and eco >= 8:
                    c.wait(s, "s_st2", st2_0 + 16 * 4 * (eco // 4))
                if (P3X & 4) and eco >= 4:
                    c.wait(s, "pe_n", pen0 + 2 * (eco // 4))
                for j in range(2):
                    c.wait(s, "pe", pe_q0 + 2 * eco + j + 1)
                    c.wait(s, "dve_q", dveq0 + 2 * eco + j + 1)
                    sl = slice(j * 512, (j + 1) * 512)
                    ins = s.activation(out=b["hbuf"][:, hb, eco % 4, sl], in_=b["pb"][j][:],
                                       func=AF.Silu, scale=1.0)
                    c.inc(ins, "act_q", 1)
                if P3X & 4:
                    ins = s.activation(out=b["sq2_scr"][:, eco % 4, :], in_=b["hbuf"][:, hb, eco % 4, :],
                                       func=AF.Square, scale=1.0)
                    c.inc(ins, "act_sq", 1)
                    if eco % 4 == 3:
                        h4 = eco // 4
                        c.wait(s, "pe_n", pen0 + 2 * (h4 + 1))
                        for j in range(2):
                            ins = s.activation(out=b["nrm"][:, j * 512:(j + 1) * 512],
                                               in_=b["pb"][2 + j][:], func=AF.Sqrt, scale=1.0)
                        c.inc(ins, "act_n", 1)
            else:
                if (P3X & 2) and not (P3X & 8) and eco >= 34:
                    c.wait(s, "s_st3", st3_0 + 16 * (eco - 32))
                for j in range(2):
                    c.wait(s, "pe", pe_q0 + 2 * eco + j + 1)
                    c.wait(s, "dve_q", dveq0 + 2 * eco + j + 1)
                    sl = slice(j * 512, (j + 1) * 512)
                    ins = s.activation(out=b["vout"][:, eco % 2, sl], in_=b["pb"][j][:],
                                       func=AF.Silu, scale=1.0)
                    c.inc(ins, "act_q", 1)

    c.sec("scalar", s_q)

    def sy_all(sy):
        for eco in range(48):
            w = eco % 2
            if P3X & 1:
                c.wait(sy, "dve_q", dveq0 + 2 * (eco + 1))
                ins = sy.dma_start(out=b["wloc"].ap()[eco * 128:(eco + 1) * 128, :],
                                   in_=b["wout"][:, w, :])
                c.inc(ins, "s_st", 16)
                if eco >= 16:
                    ins = sy.dma_start(out=b["cc2_in"].ap()[(eco - 16) * 128:(eco - 15) * 128, :],
                                       in_=b["wout"][:, w, :])
                    c.inc(ins, "s_st", 16)
            if (P3X & 2) and not (P3X & 8) and eco >= 32:
                c.wait(sy, "act_q", actq0 + 2 * (eco + 1))
                h = (eco - 32) // 4
                ch = eco % 4
                base = h * 1536 + 1024 + ch * 128
                ins = sy.dma_start(out=b["cc3_in"].ap()[base:base + 128, :],
                                   in_=b["vout"][:, eco % 2, :])
                c.inc(ins, "s_st3", 16)
            if P3X & 4 and eco < 32 and eco % 4 == 3:
                h4 = eco // 4
                qk = h4 // 4
                h = h4 % 4
                for ch in range(4):
                    c.wait(sy, "dve_n", dven0 + h4 * 4 + ch + 1)
                    base = h * 1536 + qk * 512 + ch * 128
                    ins = sy.dma_start(out=b["cc3_in"].ap()[base:base + 128, :],
                                       in_=b["hbuf"][:, h4 % 2, ch, :])
                    c.inc(ins, "s_st2", 16)

    c.sec("sync", sy_all)

    if P3_PART < 2:
        return
    pe_sm0 = c.cnt["pe"]
    actsm0 = c.cnt["act_sm"]
    dveq_all = c.cnt["dve_q"] + 2 * 48
    actq_all = actq0 + 2 * 48

    def t_sm(t):
        for w in range(3):
            col0 = (0, 12, 16)[w]
            m = (12, 4, 4)[w]
            if w >= 1:
                c.wait(t, "act_sm", actsm0 + 2 * w)
            for j in range(2):
                for dc in range(DC):
                    ins = t.matmul(b["pb"][4 + j][:m, :], b["wsm"][:, dc, col0:col0 + m],
                                   b["xn"][:, dc, j * 512:(j + 1) * 512],
                                   start=(dc == 0), stop=(dc == DC - 1))
                c.inc(ins, "pe", 1)

    c.sec("tensor", t_sm)

    def s_sm(s):
        for j in range(2):
            c.wait(s, "pe", pe_sm0 + j + 1)
            sl = slice(j * 512, (j + 1) * 512)
            s.activation(out=b["spa"][:12, sl], in_=b["pb"][4 + j][:12, :], func=AF.Abs,
                         scale=1.0, bias=b["c_lrb"][:12, :])
            s.activation(out=b["spa"][:12, sl], in_=b["spa"][:12, sl], func=AF.Exp, scale=-1.0)
            s.activation(out=b["spa"][:12, sl], in_=b["spa"][:12, sl], func=AF.Ln, scale=1.0,
                         bias=b["c_one"][:12, :])
            ins = s.activation(out=b["spb"][:12, sl], in_=b["pb"][4 + j][:12, :], func=AF.Relu,
                               scale=1.0, bias=b["c_lrb"][:12, :])
            c.inc(ins, "act_sm", 1)
        for j in range(2):
            c.wait(s, "pe", pe_sm0 + 2 + j + 1)
            sl = slice(j * 512, (j + 1) * 512)
            ins = s.activation(out=b["mom_sb"][:4, sl], in_=b["pb"][4 + j][:4, :],
                               func=AF.Sigmoid, scale=1.0, bias=b["bm1"][:4, :])
            c.inc(ins, "act_sm", 1)
        for j in range(2):
            c.wait(s, "pe", pe_sm0 + 4 + j + 1)
            sl = slice(j * 512, (j + 1) * 512)
            ins = s.activation(out=b["scale_sb"][:4, sl], in_=b["pb"][4 + j][:4, :],
                               func=AF.Silu, scale=1.0, bias=b["bs1"][:4, :])
            c.inc(ins, "act_sm", 1)

    c.sec("scalar", s_sm)
    act_sm1 = c.cnt["act_sm"]

    def v_sm(v):
        c.wait(v, "act_sm", act_sm1)
        ins = v.tensor_tensor(out=b["lr_sb"][:12, :], in0=b["spa"][:12, :], in1=b["spb"][:12, :],
                              op=ALU.add)
        c.inc(ins, "dve_sm", 1)

    c.sec("vector", v_sm)
    dve_sm1 = c.cnt["dve_sm"]

    def sy_sm(sy):
        c.wait(sy, "dve_sm", dve_sm1)
        for h in range(4):
            ins = sy.dma_start(out=b["cc3b_in"].ap()[4 * h:4 * h + 3, :],
                               in_=b["lr_sb"][3 * h:3 * h + 3, :])
            c.inc(ins, "s_st2", 16)
            ins = sy.dma_start(out=b["cc3b_in"].ap()[4 * h + 3:4 * h + 4, :],
                               in_=b["mom_sb"][h:h + 1, :])
            c.inc(ins, "s_st2", 16)
        ins = sy.dma_start(out=b["scdram"].ap(), in_=b["scale_sb"][0:4, :])
        c.inc(ins, "s_st2", 16)

    c.sec("sync", sy_sm)

    st_all = (c.cnt["s_st"], c.cnt["s_st2"], c.cnt["s_st3"])

    def g_cc23(g):
        c.wait(g, "s_st", st_all[0])
        c.wait(g, "s_st2", st_all[1])
        c.wait(g, "s_st3", st_all[2])
        if P3_PART >= 3:
            ins = g.collective_compute("AllGather", ALU.bypass,
                                       replica_groups=[list(range(NCORES))],
                                       ins=[b["cc2_in"].ap().opt()], outs=[b["cc2_out"].ap().opt()])
            c.inc(ins, "cc", 1)
        c.cc_c2 = c.cnt["cc"]
        if P3_PART >= 4:
            ins = g.collective_compute("AllGather", ALU.bypass,
                                       replica_groups=[list(range(NCORES))],
                                       ins=[b["cc3_in"].ap().opt()], outs=[b["cc3_out"].ap().opt()])
            c.inc(ins, "cc", 1)
            ins = g.collective_compute("AllGather", ALU.bypass,
                                       replica_groups=[list(range(NCORES))],
                                       ins=[b["cc3b_in"].ap().opt()], outs=[b["cc3b_out"].ap().opt()])
            c.inc(ins, "cc", 1)

    c.sec("gpsimd", g_cc23)



def emit_p4(c):
    b = c.buf
    cc_need = c.cnt["cc"]
    st0 = c.cnt["s_st"]

    sp_done4 = c.cnt["s_sp"]
    pe_all4 = c.cnt["pe"]
    gld_all4 = c.cnt["g_ld"]

    def g_c4(g):
        c.wait(g, "s_sp", sp_done4)
        c.wait(g, "pe", pe_all4)
        c.wait(g, "g_ld", gld_all4)
        ins = g.dma_start(out=b["wm"][:], in_=b["in_wm"].rearrange("i p k -> p i k"),
                          max_dma_last_dim=8192)
        c.inc(ins, "g_ld", 16)
        ins = g.dma_start(out=b["idn4"][:], in_=b["in_idn"])
        c.inc(ins, "g_ld", 16)
        c.wait(g, "cc", c.cc_c2)
        with g.register("rpv4") as rpv:
            g.load(rpv, b["dyn"][1:2, 0:1])
            ins = g.dma_start(out=b["prevloc"].ap(),
                              in_=b["cc2_out"].ap()[bass.ds(g.bass.snap(rpv), 2 * D), :])
            c.inc(ins, "g_ld", 16)

    c.sec("gpsimd", g_c4)
    gld_m = c.cnt["g_ld"]

    def g_cp5(g):
        c.wait(g, "cc", cc_need)
        with g.register("rh5") as rh:
            g.load(rh, b["dyn"][0:1, 0:1])
            ins = g.dma_start(
                out=b["scanloc"].ap().rearrange("(ch r) t -> ch r t", r=1536),
                in_=b["cc3_out"].ap().rearrange("(ch r) t -> ch r t", r=6144)
                [:, bass.ds(g.bass.snap(rh), 1536), :])
            c.inc(ins, "g_cp", 16)
        with g.register("rb5") as rb5:
            g.load(rb5, b["dyn"][3:4, 0:1])
            ins = g.dma_start(
                out=b["scanlocb"].ap().rearrange("(ch r) t -> ch r t", r=4),
                in_=b["cc3b_out"].ap().rearrange("(ch r) t -> ch r t", r=16)
                [:, bass.ds(g.bass.snap(rb5), 4), :])
            c.inc(ins, "g_cp", 16)

    c.sec("gpsimd", g_cp5)

    m0 = c.cnt["dve_w"]
    a0 = c.cnt["act_w"]
    psc0 = c.cnt["pe_sc"]
    ptr0 = c.cnt["pe_tr"]
    ppv0 = c.cnt["pe_pv"]
    pav0 = c.cnt["pe_av"]
    stw0 = c.cnt["s_stw"]

    def base(n):
        return m0 + (n // 8) * 57 + 1 + (n % 8) * 7

    def sy_4(sy):
        c.wait(sy, "cc", c.cc_c2)
        c.wait(sy, "g_ld", gld_m)
        if True:
            for ah in range(16):
                if ah >= 1:
                    c.wait(sy, "pe_av", pav0 + ah)
                if ah >= 2:
                    c.wait(sy, "pe_sc", psc0 + (ah - 1) * 8)
                hb = ah % 2
                ins = sy.dma_start(out=b["aqh"][:, hb, :],
                                   in_=b["wloc"].ap()[ah * 128:(ah + 1) * 128, :])
                c.inc(ins, "s_st", 16)
                ins = sy.dma_start(out=b["kwin"][:, hb, 1024:2048],
                                   in_=b["wloc"].ap()[2048 + ah * 128:2048 + (ah + 1) * 128, :])
                c.inc(ins, "s_st", 16)
                ins = sy.dma_start(out=b["avwin"][:, hb, 1024:2048],
                                   in_=b["wloc"].ap()[4096 + ah * 128:4096 + (ah + 1) * 128, :])
                c.inc(ins, "s_st", 16)
                ins = sy.dma_start(out=b["kwin"][:, hb, 0:1024],
                                   in_=b["prevloc"].ap()[ah * 128:(ah + 1) * 128, :])
                c.inc(ins, "s_st", 16)
                ins = sy.dma_start(out=b["avwin"][:, hb, 0:1024],
                                   in_=b["prevloc"].ap()[2048 + ah * 128:2048 + (ah + 1) * 128, :])
                c.inc(ins, "s_st", 16)
                if ah >= 2:
                    prev = ah - 2
                    c.wait(sy, "dve_w", base(prev * 8 + 7) + 7)
                    ins = sy.dma_start(out=b["attn_loc"].ap()[prev * 128:(prev + 1) * 128, :],
                                       in_=b["wo_sb"][:, prev % 2, :])
                    c.inc(ins, "s_stw", 16)
            for prev in (14, 15):
                c.wait(sy, "dve_w", base(prev * 8 + 7) + 7)
                ins = sy.dma_start(out=b["attn_loc"].ap()[prev * 128:(prev + 1) * 128, :],
                                   in_=b["wo_sb"][:, prev % 2, :])
                c.inc(ins, "s_stw", 16)

    c.sec("sync", sy_4)

    def t_4(t):
        for ah in range(16):
            hb = ah % 2
            n0h = ah * 8
            c.wait(t, "s_st", st0 + 16 * 5 * (ah + 1))
            if ah >= 1:
                c.wait(t, "dve_w", base(n0h - 1) + 6)
            for kc in range(16):
                ins = t.transpose(b["pbt2"][:, kc * 128:(kc + 1) * 128],
                                  b["avwin"][:, hb, kc * 128:(kc + 1) * 128], b["idn4"][:])
            c.inc(ins, "pe_av", 1)
            for i in range(NTC):
                n = n0h + i
                if n >= 1:
                    c.wait(t, "dve_w", base(n - 1) + 1)
                ins = t.matmul(b["pb"][0][:], b["aqh"][:, hb, i * 128:(i + 1) * 128],
                               b["kwin"][:, hb, 128 * i:128 * i + 512], start=True, stop=True)
                ins = t.matmul(b["pb"][1][:], b["aqh"][:, hb, i * 128:(i + 1) * 128],
                               b["kwin"][:, hb, 128 * i + 512:128 * i + 1024], start=True, stop=True)
                ins = t.matmul(b["pb"][2][:, 0:128], b["aqh"][:, hb, i * 128:(i + 1) * 128],
                               b["kwin"][:, hb, 128 * i + 1024:128 * i + 1152], start=True, stop=True)
                c.inc(ins, "pe_sc", 1)
                c.wait(t, "dve_w", base(n) + 5)
                for kc in range(9):
                    ins = t.transpose(b["pbt2"][:, kc * 128:(kc + 1) * 128],
                                      b["probs_n"][:, kc * 128:(kc + 1) * 128], b["idn4"][:])
                c.inc(ins, "pe_tr", 1)
                c.wait(t, "dve_w", base(n) + 6)
                for kc in range(9):
                    ins = t.matmul(b["pb"][3][:, 0:128],
                                   b["avtm"][:, (i + kc) * 128:(i + kc + 1) * 128],
                                   b["probsT"][:, kc * 128:(kc + 1) * 128],
                                   start=(kc == 0), stop=(kc == 8))
                c.inc(ins, "pe_pv", 1)

    c.sec("tensor", t_4)

    def v_4(v):
        for ah in range(16):
            hb = ah % 2
            c.wait(v, "pe_av", pav0 + ah + 1)
            v.tensor_copy(out=b["avtm"][:, 0:1024], in_=b["pbt2"][:, 0:1024])
            ins = v.tensor_copy(out=b["avtm"][:, 1024:2048], in_=b["pbt2"][:, 1024:2048])
            c.inc(ins, "dve_w", 1)
            for i in range(NTC):
                n = ah * 8 + i
                c.wait(v, "pe_sc", psc0 + n + 1)
                if n >= 1:
                    c.wait(v, "act_w", a0 + n)
                if i == 0 and ah >= 1:
                    c.wait(v, "s_stw", stw0 + 16 * max(0, ah - 1))
                v.tensor_tensor(out=b["scores"][:, 0:512], in0=b["pb"][0][:],
                                in1=b["wm"][:, i, 0:512], op=ALU.add)
                v.tensor_tensor(out=b["scores"][:, 512:1024], in0=b["pb"][1][:],
                                in1=b["wm"][:, i, 512:1024], op=ALU.add)
                ins = v.tensor_tensor(out=b["scores"][:, 1024:1152], in0=b["pb"][2][:, 0:128],
                                      in1=b["wm"][:, i, 1024:1152], op=ALU.add)
                c.inc(ins, "dve_w", 1)
                ins = v.tensor_reduce(out=b["wmax"][:, :], in_=b["scores"][:, :], axis=AXX,
                                      op=ALU.max)
                c.inc(ins, "dve_w", 1)
                c.wait(v, "dve_w")
                ins = v.tensor_scalar(out=b["wmaxn"][:, :], in0=b["wmax"][:, :], scalar1=-1.0,
                                      scalar2=None, op0=ALU.mult)
                c.inc(ins, "dve_w", 1)
                c.wait(v, "act_w", a0 + n + 1)
                ins = v.reciprocal(out=b["wrs"][:, :], in_=b["wsum2"][:, :])
                c.inc(ins, "dve_w", 1)
                c.wait(v, "dve_w")
                ins = v.tensor_scalar(out=b["probs_n"][:, :], in0=b["probs"][:, :],
                                      scalar1=b["wrs"][:, :], scalar2=None, op0=ALU.mult)
                c.inc(ins, "dve_w", 1)
                c.wait(v, "pe_tr", ptr0 + n + 1)
                v.tensor_copy(out=b["probsT"][:, 0:1024], in_=b["pbt2"][:, 0:1024])
                ins = v.tensor_copy(out=b["probsT"][:, 1024:1152], in_=b["pbt2"][:, 1024:1152])
                c.inc(ins, "dve_w", 1)
                c.wait(v, "pe_pv", ppv0 + n + 1)
                ins = v.tensor_copy(out=b["wo_sb"][:, hb, i * 128:(i + 1) * 128],
                                    in_=b["pb"][3][:, 0:128])
                c.inc(ins, "dve_w", 1)

    c.sec("vector", v_4)

    def s_4(s):
        for ah in range(16):
            for i in range(NTC):
                n = ah * 8 + i
                c.wait(s, "dve_w", base(n) + 3)
                if n >= 1:
                    c.wait(s, "dve_w", base(n - 1) + 5)
                s.activation(out=b["probs"][:, :], in_=b["scores"][:, :], func=AF.Exp,
                             scale=1.0, bias=b["wmaxn"][:, :], accum_out=b["wsum"][:, :])
                ins = s.activation(out=b["wsum2"][:, :], in_=b["wsum"][:, :], func=AF.Copy,
                                   scale=1.0)
                c.inc(ins, "act_w", 1)

    c.sec("scalar", s_4)



def emit_p5(c):
    b = c.buf
    NS_COEFFS = [(4.0848, -6.8946, 2.927), (3.9505, -6.3029, 2.6377),
                 (3.7418, -5.5913, 2.3037), (2.8769, -3.1427, 1.2046),
                 (2.8366, -3.0525, 1.2012)]
    cc_need = c.cnt["cc"]

    b["ki_tm"] = b["qiT"][:].rearrange("p a (x t) -> p (a x) t", x=2)
    b["vi_tm"] = b["kiT"][:].rearrange("p a (x t) -> p (a x) t", x=2)
    b["dgl0"] = b["gba"]
    b["dhl2"] = b["dsil"]

    def g_5(g):
        for w in range(3):
            ins = g.dma_start(out=b["wT"][w][:],
                              in_=b["in_wT"][w * 512:(w + 1) * 512, :].rearrange("(dc p) h -> p dc h", p=128))
            c.inc(ins, "g_ld", 16)
        ins = g.dma_start(out=b["w1n"][:],
                          in_=b["in_w1n"].rearrange("(dc p) h -> p dc h", p=128))
        c.inc(ins, "g_ld", 16)

    c.sec("gpsimd", g_5)
    gld_5 = c.cnt["g_ld"]

    def COLSUM(t, bank, waitv):
        c.wait(t, "act5", waitv)
        for a in range(4):
            ins = t.matmul(b["pb"][bank][:], b["ones128"][:], b["sq5"][:, a, :],
                           start=(a == 0), stop=(a == 3))
        c.inc(ins, "pe5", 1)

    a0 = c.cnt["act5"]
    p0 = c.cnt["pe5"]
    d0 = c.cnt["dve5"]

    def s_ni2(s):
        c.wait(s, "g_ld", gld_5)
        for w in range(3):
            if w > 0:
                c.wait(s, "pe5", p0 + w)
            ins = s.activation(out=b["sq5"][:].rearrange("p a h -> p (a h)"),
                               in_=b["wT"][w][:].rearrange("p a h -> p (a h)"),
                               func=AF.Square, scale=1.0)
            c.inc(ins, "act5", 1)
        c.wait(s, "pe5", p0 + 3)
        ins = s.activation(out=b["sq5"][:].rearrange("p a h -> p (a h)"),
                           in_=b["w1n"][:].rearrange("p a h -> p (a h)"),
                           func=AF.Square, scale=1.0)
        c.inc(ins, "act5", 1)

    c.sec("scalar", s_ni2)

    def t_ni(t):
        for w in range(3):
            COLSUM(t, w, a0 + w + 1)
        c.wait(t, "act5", a0 + 4)

    c.sec("tensor", t_ni)

    def v_ni(v):
        c.wait(v, "pe5", p0 + 3)
        c.wait(v, "act5", a0 + 4)
        for w in range(3):
            ins = v.tensor_copy(out=b["nwb"][w][:], in_=b["pb"][w][:])
            c.inc(ins, "dve5", 1)
        for a in range(4):
            ins = v.reduce_sum(out=b["n1r"][:, a:a + 1], in_=b["sq5"][:, a, :], axis=AXX)
        c.inc(ins, "dve5", 1)

    c.sec("vector", v_ni)
    d_ni = c.cnt["dve5"]

    def s_ni3(s):
        c.wait(s, "dve5", d_ni)
        for w in range(3):
            s.activation(out=b["nwb"][w][:], in_=b["nwb"][w][:], func=AF.Sqrt, scale=1.0)
        ins = s.activation(out=b["n1r"][:, 0:4], in_=b["n1r"][:, 0:4], func=AF.Sqrt, scale=1.0)
        c.inc(ins, "act5", 1)

    c.sec("scalar", s_ni3)

    d_gate_in = c.cnt["dve5"]
    for ch in range(8):
        base_r = ch * 1536

        d_prev = c.cnt["dve5"]
        a_prev = c.cnt["act5"]

        def g_in(g, ch=ch, base_r=base_r, d_prev=d_gate_in):
            if ch == 0:
                c.wait(g, "g_cp", 32)
            c.wait(g, "dve5", d_prev)
            for wi, dst in ((0, "qiT"), (1, "kiT"), (2, "vi")):
                ins = g.dma_start(out=b[dst][:],
                                  in_=b["scanloc"].ap()[base_r + wi * 512: base_r + (wi + 1) * 512, :]
                                  .rearrange("(dc p) t -> p dc t", p=128))
                c.inc(ins, "g_ld", 16)
            ins = g.dma_start(out=b["scb"][:], in_=b["scanlocb"].ap()[ch * 4:(ch + 1) * 4, :])
            c.inc(ins, "g_ld", 16)

        c.sec("gpsimd", g_in)
        g_in_done = c.cnt["g_ld"]

        def s_cast(s, ch=ch, g_in_done=g_in_done, d_prev=d_prev):
            c.wait(s, "g_ld", g_in_done)
            c.wait(s, "dve5", d_prev)
            for w in range(3):
                s.activation(out=b["wTb"][w][:].rearrange("p a h -> p (a h)"),
                             in_=b["wT"][w][:].rearrange("p a h -> p (a h)"),
                             func=AF.Copy, scale=1.0)
            ins = s.activation(out=b["w1nb"][:].rearrange("p a h -> p (a h)"),
                               in_=b["w1n"][:].rearrange("p a h -> p (a h)"),
                               func=AF.Copy, scale=1.0)
            c.inc(ins, "act5", 1)

        c.sec("scalar", s_cast)
        a_cast = c.cnt["act5"]

        p_lm0 = c.cnt["pe5"]

        def t_lm(t, ch=ch, g_in_done=g_in_done, d_prev=d_prev):
            c.wait(t, "g_ld", g_in_done)
            c.wait(t, "dve5", d_prev)
            for tc in range(NTC):
                ins = t.transpose(b["pb"][4][:, tc * 4:(tc + 1) * 4],
                                  b["scb"][0:4, tc * 128:(tc + 1) * 128], b["idn5f"][0:4, 0:4])
            c.inc(ins, "pe5", 1)

        c.sec("tensor", t_lm)
        p_lm = c.cnt["pe5"]

        def v_lm(v, ch=ch, p_lm=p_lm):
            c.wait(v, "pe5", p_lm)
            ins = v.tensor_copy(out=b["lmt"][:].rearrange("p a w -> p (a w)"),
                                in_=b["pb"][4][:, 0:32])
            c.inc(ins, "dve5", 1)
            ins = v.reduce_sum(out=b["mmr"][:, 0:1], in_=b["lmt"][:, :, 3], axis=AXX)
            c.inc(ins, "dve5", 1)

        c.sec("vector", v_lm)
        d_lm = c.cnt["dve5"]
        p_mm0 = c.cnt["pe5"]

        def t_mm(t, ch=ch, d_lm=d_lm):
            c.wait(t, "dve5", d_lm)
            t.matmul(b["pb"][5][0:1, 0:1], b["mmr"][:, 0:1], b["c_one"][:, 0:1],
                     start=True, stop=True)
            ins = t.matmul(b["pb"][5][0:1, 4:5], b["mmr"][:, 0:1], b["c_one"][:, 0:1],
                          start=True, stop=True)
            c.inc(ins, "pe5", 1)

        c.sec("tensor", t_mm)
        p_mm = c.cnt["pe5"]

        def v_mm(v, ch=ch, p_mm=p_mm):
            c.wait(v, "pe5", p_mm)
            ins = v.tensor_scalar(out=b["mm1"][0:1, 0:1], in0=b["pb"][5][0:1, 0:1],
                                  scalar1=1.0 / 1024.0, scalar2=None, op0=ALU.mult)
            c.inc(ins, "dve5", 1)

        c.sec("vector", v_mm)
        d_mm = c.cnt["dve5"]

        def t_mmb(t, ch=ch, d_mm=d_mm):
            c.wait(t, "dve5", d_mm)
            ins = t.matmul(b["pb"][5][0:128, 8:9], b["ones1w"][0:1, 0:128], b["mm1"][0:1, 0:1],
                           start=True, stop=True)
            c.inc(ins, "pe5", 1)

        c.sec("tensor", t_mmb)
        p_mmb = c.cnt["pe5"]

        def v_mmb(v, ch=ch, p_mmb=p_mmb):
            c.wait(v, "pe5", p_mmb)
            ins = v.tensor_copy(out=b["mmb"][:, 0:1], in_=b["pb"][0:0] if False else b["pb"][5][0:128, 8:9])
            c.inc(ins, "dve5", 1)

        c.sec("vector", v_mmb)

        p_f0 = c.cnt["pe5"]
        a_f0 = c.cnt["act5"]
        d_f0 = c.cnt["dve5"]
        for hc in range(4):
            for j in range(2):
                idx = hc * 2 + j

                def t_g(t, hc=hc, j=j, idx=idx, ch=ch):
                    if idx == 0:
                        c.wait(t, "act5", a_cast)
                    if idx >= 2:
                        c.wait(t, "act5", a_f0 + idx - 1)
                        c.wait(t, "dve5", d_f0 + idx - 1)
                    for dc in range(4):
                        t.matmul(b["pb"][j][:], b["wTb"][2][:, dc, hc * 128:(hc + 1) * 128],
                                 b["qiT"][:, dc, j * 512:(j + 1) * 512],
                                 start=(dc == 0), stop=(dc == 3))
                    for dc in range(4):
                        ins = t.matmul(b["pb"][2 + j][:], b["wTb"][0][:, dc, hc * 128:(hc + 1) * 128],
                                       b["qiT"][:, dc, j * 512:(j + 1) * 512],
                                       start=(dc == 0), stop=(dc == 3))
                    c.inc(ins, "pe5", 1)

                def s_g(s, j=j, idx=idx, ch=ch):
                    c.wait(s, "pe5", p_f0 + idx + 1)
                    ins = s.activation(out=b["scr5"][:, j, :], in_=b["pb"][2 + j][:],
                                       func=AF.Silu, scale=1.0)
                    c.inc(ins, "act5", 1)

                def v_g(v, hc=hc, j=j, idx=idx, ch=ch):
                    c.wait(v, "pe5", p_f0 + idx + 1)
                    c.wait(v, "act5", a_f0 + idx + 1)
                    ins = v.tensor_tensor(out=b["gh"][:, hc, j * 512:(j + 1) * 512],
                                          in0=b["pb"][j][:], in1=b["scr5"][:, j, :], op=ALU.mult)
                    c.inc(ins, "dve5", 1)

                c.sec("tensor", t_g)
                c.sec("scalar", s_g)
                c.sec("vector", v_g)

        d_gh = c.cnt["dve5"]
        p_oc0 = c.cnt["pe5"]
        d_oc0 = c.cnt["dve5"]
        for oc in range(4):
            for j in range(2):
                idx = oc * 2 + j

                def t_o(t, oc=oc, j=j, idx=idx, ch=ch, d_gh=d_gh):
                    if idx == 0:
                        c.wait(t, "dve5", d_gh)
                    if idx >= 2:
                        c.wait(t, "dve5", d_oc0 + idx - 1)
                    for hc in range(4):
                        ins = t.matmul(b["pb"][4 + idx % 2][:],
                                       b["wTb"][1][:, hc, oc * 128:(oc + 1) * 128],
                                       b["gh"][:, hc, j * 512:(j + 1) * 512],
                                       start=(hc == 0), stop=(hc == 3))
                    c.inc(ins, "pe5", 1)

                def v_o(v, oc=oc, j=j, idx=idx, ch=ch):
                    c.wait(v, "pe5", p_oc0 + idx + 1)
                    ins = v.tensor_copy(out=b["ocb"][:, oc, j * 512:(j + 1) * 512],
                                        in_=b["pb"][4 + idx % 2][:])
                    c.inc(ins, "dve5", 1)

                c.sec("tensor", t_o)
                c.sec("vector", v_o)

        d_oc = c.cnt["dve5"]

        def sy_oc(sy, d_oc=d_oc, ch=ch):
            c.wait(sy, "dve5", d_oc)
            ins = sy.dma_start(out=b["cc4_in"].ap()[ch * 512:(ch + 1) * 512, :]
                               .rearrange("(oc p) t -> p oc t", p=128),
                               in_=b["ocb"][:])
            c.inc(ins, "s_st5", 16)

        c.sec("sync", sy_oc)

        p_k0 = c.cnt["pe5"]
        d_k0 = c.cnt["dve5"]
        n_k = 0
        for name, wsrc, xsrc in (("gba", 0, "kiT"), ("hbm", 2, "kiT"), ("dhid", 1, "vi")):
            for tc in range(NTC):
                nk = n_k

                def t_k(t, name=name, wsrc=wsrc, xsrc=xsrc, tc=tc, nk=nk, ch=ch):
                    if nk >= 2:
                        c.wait(t, "dve5", d_k0 + nk - 1)
                    wt = b["w1nb"] if name == "dhid" else b["wTb"][wsrc]
                    for dc in range(4):
                        ins = t.matmul(b["pb"][nk % 2][:],
                                       b[xsrc][:, dc, tc * 128:(tc + 1) * 128],
                                       wt[:, dc, :], start=(dc == 0), stop=(dc == 3))
                    c.inc(ins, "pe5", 1)

                def v_k(v, name=name, tc=tc, nk=nk, ch=ch):
                    c.wait(v, "pe5", p_k0 + nk + 1)
                    ins = v.tensor_copy(out=b[name][:, tc, :], in_=b["pb"][nk % 2][:])
                    c.inc(ins, "dve5", 1)

                c.sec("tensor", t_k)
                c.sec("vector", v_k)
                n_k += 1

        d_k = c.cnt["dve5"]

        p_tr0 = c.cnt["pe5"]
        ev_d = {}
        n_t = 0
        for srcn, dstn in (("kiT", "ki_tm"), ("vi", "vi_tm")):
            for dc in range(4):
                nt = n_t
                trp = "pbt5" if nt % 2 == 0 else "pbt5c"

                def t_tr(t, srcn=srcn, dc=dc, nt=nt, trp=trp, d_k=d_k,
                         prev=ev_d.get(nt - 2), a_prev=a_prev):
                    if nt == 0:
                        c.wait(t, "dve5", d_k)
                    if nt == 1:
                        c.wait(t, "act5", a_prev)
                    if prev is not None:
                        c.wait(t, prev[0], prev[1])
                    for tc in range(NTC):
                        ins = t.transpose(b[trp][:, tc * 128:(tc + 1) * 128],
                                          b[srcn][:, dc, tc * 128:(tc + 1) * 128], b["idn5"][:])
                    c.inc(ins, "pe5", 1)

                c.sec("tensor", t_tr)

                if nt % 2 == 0:
                    def v_tr(v, dstn=dstn, dc=dc, nt=nt, trp=trp):
                        c.wait(v, "pe5", p_tr0 + nt + 1)
                        for tc in range(NTC):
                            ins = v.tensor_copy(out=b[dstn][:, tc, dc * 128:(dc + 1) * 128],
                                                in_=b[trp][:, tc * 128:(tc + 1) * 128])
                        c.inc(ins, "dve5", 1)
                    c.sec("vector", v_tr)
                    ev_d[nt] = ("dve5", c.cnt["dve5"])
                else:
                    def s_tr(s, dstn=dstn, dc=dc, nt=nt, trp=trp):
                        c.wait(s, "pe5", p_tr0 + nt + 1)
                        for tc in range(NTC):
                            ins = s.activation(out=b[dstn][:, tc, dc * 128:(dc + 1) * 128],
                                               in_=b[trp][:, tc * 128:(tc + 1) * 128],
                                               func=AF.Copy, scale=1.0)
                        c.inc(ins, "act5", 1)
                    c.sec("scalar", s_tr)
                    ev_d[nt] = ("act5", c.cnt["act5"])
                n_t += 1

        d_tr = c.cnt["dve5"]

        a_b0 = c.cnt["act5"]

        def s_bwd(s, d_tr=d_tr, ch=ch):
            c.wait(s, "dve5", d_tr)
            s.activation(out=b["sg"][:].rearrange("p a h -> p (a h)"),
                         in_=b["gba"][:].rearrange("p a h -> p (a h)"), func=AF.Silu, scale=1.0)
            ins = s.activation(out=b["dsil"][:].rearrange("p a h -> p (a h)"),
                               in_=b["gba"][:].rearrange("p a h -> p (a h)"),
                               func=AF.Sigmoid, scale=1.0)
            c.inc(ins, "act5", 1)

        c.sec("scalar", s_bwd)
        a_b = c.cnt["act5"]

        def v_bwd(v, a_b=a_b, ch=ch):
            c.wait(v, "act5", a_b)
            for tc in range(NTC):
                v.tensor_tensor(out=b["hl1"][:, tc, :], in0=b["sg"][:, tc, :],
                                in1=b["dsil"][:, tc, :], op=ALU.mult)
                v.tensor_tensor(out=b["dsil"][:, tc, :], in0=b["dsil"][:, tc, :],
                                in1=b["hl1"][:, tc, :], op=ALU.subtract)
                v.tensor_tensor(out=b["dsil"][:, tc, :], in0=b["dsil"][:, tc, :],
                                in1=b["sg"][:, tc, :], op=ALU.add)
            for tc in range(NTC):
                v.tensor_tensor(out=b["dgl0"][:, tc, :], in0=b["dhid"][:, tc, :],
                                in1=b["hbm"][:, tc, :], op=ALU.mult)
                v.tensor_tensor(out=b["dgl0"][:, tc, :], in0=b["dgl0"][:, tc, :],
                                in1=b["dsil"][:, tc, :], op=ALU.mult)
                ins = v.tensor_scalar(out=b["dgl0"][:, tc, :], in0=b["dgl0"][:, tc, :],
                                      scalar1=b["lmt"][:, tc, 0:1], scalar2=None, op0=ALU.mult)
            c.inc(ins, "dve5", 1)
            for tc in range(NTC):
                v.tensor_tensor(out=b["hl1"][:, tc, :], in0=b["sg"][:, tc, :],
                                in1=b["hbm"][:, tc, :], op=ALU.mult)
                v.tensor_scalar(out=b["hl1"][:, tc, :], in0=b["hl1"][:, tc, :],
                                scalar1=b["lmt"][:, tc, 1:2], scalar2=None, op0=ALU.mult)
            for tc in range(NTC):
                v.tensor_tensor(out=b["dhl2"][:, tc, :], in0=b["dhid"][:, tc, :],
                                in1=b["sg"][:, tc, :], op=ALU.mult)
                ins = v.tensor_scalar(out=b["dhl2"][:, tc, :], in0=b["dhl2"][:, tc, :],
                                      scalar1=b["lmt"][:, tc, 2:3], scalar2=None, op0=ALU.mult)
            c.inc(ins, "dve5", 1)

        c.sec("vector", v_bwd)
        d_bwd = c.cnt["dve5"]

        SLB = [
            dict(X="X", Xb="Xb", XTb="XTb", Ab="Ab", Bb="Bb", bk=(0, 1), trp="pbt5"),
            dict(X="X2", Xb="Xb2", XTb="XTb2", Ab="Ab2", Bb="Bb2", bk=(2, 3), trp="pbt5c"),
        ]

        def emit_dw(wsel, sl, d_gate, ch=ch):
            if ch > 0:
                def g_m(g, wsel=wsel):
                    c.wait(g, "dve5", c.cnt["dve5"])
                    ins = g.dma_start(out=b["mst"][:],
                                      in_=b["mdram"].ap()[wsel * 512:(wsel + 1) * 512, :]
                                      .rearrange("(dc p) h -> p dc h", p=128))
                    c.inc(ins, "g_ld", 16)
                c.sec("gpsimd", g_m)
            gm = c.cnt["g_ld"]
            A_name, B_name = (("dgl0", "ki_tm"), ("vi_tm", "hl1"), ("dhl2", "ki_tm"))[wsel]
            p_dw0 = c.cnt["pe5"]
            for oc in range(4):
                def t_dw(t, oc=oc, A_name=A_name, B_name=B_name, d_gate=d_gate):
                    if oc == 0:
                        c.wait(t, "dve5", d_gate)
                    for tc in range(NTC):
                        ins = t.matmul(b["pb"][oc][:],
                                       b[A_name][:, tc, oc * 128:(oc + 1) * 128],
                                       b[B_name][:, tc, :],
                                       start=(tc == 0), stop=(tc == NTC - 1))
                    c.inc(ins, "pe5", 1)

                def v_dw(v, oc=oc, sl=sl, gm=gm, p_dw0=p_dw0, ch=ch):
                    c.wait(v, "pe5", p_dw0 + oc + 1)
                    if oc == 0 and ch > 0:
                        c.wait(v, "g_ld", gm)
                    if ch == 0:
                        ins = v.tensor_copy(out=b[sl["X"]][:, oc, :], in_=b["pb"][oc][:])
                    else:
                        v.tensor_scalar(out=b["scr5f"][:, :], in0=b["mst"][:, oc, :],
                                        scalar1=b["mmb"][:, 0:1], scalar2=None, op0=ALU.mult)
                        ins = v.tensor_tensor(out=b[sl["X"]][:, oc, :], in0=b["pb"][oc][:],
                                              in1=b["scr5f"][:, :], op=ALU.add)
                    c.inc(ins, "dve5", 1)

                c.sec("tensor", t_dw)
                c.sec("vector", v_dw)
            d_x = c.cnt["dve5"]

            def sy_m(sy, wsel=wsel, sl=sl, d_x=d_x):
                c.wait(sy, "dve5", d_x)
                ins = sy.dma_start(out=b["mdram"].ap()[wsel * 512:(wsel + 1) * 512, :]
                                   .rearrange("(dc p) h -> p dc h", p=128),
                                   in_=b[sl["X"]][:])
                c.inc(ins, "s_st5", 16)

            c.sec("sync", sy_m)
            return d_x, c.cnt["s_st5"]

        def emit_fr(sl, d_x, st_m):
            def s_fr(s, sl=sl, d_x=d_x):
                c.wait(s, "dve5", d_x)
                ins = s.activation(out=b["sq5"][:].rearrange("p a h -> p (a h)"),
                                   in_=b[sl["X"]][:].rearrange("p a h -> p (a h)"),
                                   func=AF.Square, scale=1.0)
                c.inc(ins, "act5", 1)
            c.sec("scalar", s_fr)
            a_q = c.cnt["act5"]

            def t_fr(t, a_q=a_q, d_x=d_x):
                c.wait(t, "dve5", d_x)
                COLSUM(t, 4, a_q)
            c.sec("tensor", t_fr)
            p_q = c.cnt["pe5"]

            def v_fr(v, p_q=p_q):
                c.wait(v, "pe5", p_q)
                ins = v.reduce_sum(out=b["fr1"][:, 0:1], in_=b["pb"][4][:], axis=AXX)
                c.inc(ins, "dve5", 1)
                c.wait(v, "dve5")
            c.sec("vector", v_fr)
            d_rf = c.cnt["dve5"]

            def s_fr2(s, d_rf=d_rf):
                c.wait(s, "dve5", d_rf)
                ins = s.activation(out=b["fr1"][:, 0:1], in_=b["fr1"][:, 0:1],
                                   func=AF.Sqrt, scale=1.0)
                c.inc(ins, "act5", 1)
            c.sec("scalar", s_fr2)
            a_q2 = c.cnt["act5"]

            def v_fr2(v, sl=sl, a_q2=a_q2, st_m=st_m):
                c.wait(v, "act5", a_q2)
                c.wait(v, "s_st5", st_m)
                v.tensor_scalar(out=b["fr1"][:, 0:1], in0=b["fr1"][:, 0:1], scalar1=1e-7,
                                scalar2=None, op0=ALU.add)
                ins = v.reciprocal(out=b["fr2"][:, 0:1], in_=b["fr1"][:, 0:1])
                c.inc(ins, "dve5", 1)
                c.wait(v, "dve5")
                for a_ in range(4):
                    ins = v.tensor_scalar(out=b[sl["X"]][:, a_, :], in0=b[sl["X"]][:, a_, :],
                                          scalar1=b["fr2"][:, 0:1], scalar2=None, op0=ALU.mult)
                c.inc(ins, "dve5", 1)
            c.sec("vector", v_fr2)
            return c.cnt["dve5"]

        def emit_ctr(jobs):
            a_c = []
            for sl, d_ready in jobs:
                def s_c(s, sl=sl, d_ready=d_ready):
                    c.wait(s, "dve5", d_ready)
                    ins = s.activation(out=b[sl["Xb"]][:].rearrange("p a h -> p (a h)"),
                                       in_=b[sl["X"]][:].rearrange("p a h -> p (a h)"),
                                       func=AF.Copy, scale=1.0)
                    c.inc(ins, "act5", 1)
                c.sec("scalar", s_c)
                a_c.append(c.cnt["act5"])
            a_ev = list(a_c)
            for rnd in range(2):
                p_tr = [0] * len(jobs)
                for ji, (sl, _) in enumerate(jobs):
                    def t_tr5(t, sl=sl, rnd=rnd, gate=a_ev[ji]):
                        c.wait(t, "act5", gate)
                        for ic in range(4):
                            for k2 in range(2):
                                kc = rnd * 2 + k2
                                ins = t.transpose(
                                    b[sl["trp"]][:, (ic * 2 + k2) * 128:(ic * 2 + k2 + 1) * 128],
                                    b[sl["Xb"]][:, kc, ic * 128:(ic + 1) * 128],
                                    b["idn5"][:])
                        c.inc(ins, "pe5", 1)
                    c.sec("tensor", t_tr5)
                    p_tr[ji] = c.cnt["pe5"]
                for ji, (sl, _) in enumerate(jobs):
                    def s_ev(s, sl=sl, rnd=rnd, pw=p_tr[ji]):
                        c.wait(s, "pe5", pw)
                        for ic in range(4):
                            ins = s.activation(out=b[sl["XTb"]][:, ic, rnd * 256:(rnd + 1) * 256],
                                               in_=b[sl["trp"]][:, ic * 256:(ic + 1) * 256],
                                               func=AF.Copy, scale=1.0)
                        c.inc(ins, "act5", 1)
                    c.sec("scalar", s_ev)
                    a_ev[ji] = c.cnt["act5"]
            return a_ev

        def ns_stage2(stage, xw, xa, coef):
            aa, bb_, cc_ = coef
            d_h = {}
            for w, hf in ((0, 0), (1, 0), (0, 1), (1, 1)):
                sl = SLB[w]
                bk = sl["bk"]
                ic0 = hf * 2
                dgate = xw[w] if hf == 0 else d_h[(w, 0)]
                agate = xa[w]
                lhs = {"A": sl["XTb"], "A2": sl["Ab"], "BX": sl["Bb"]}[stage]
                rhs = {"A": sl["XTb"], "A2": sl["Ab"], "BX": sl["Xb"]}[stage]

                def t_s(t, bk=bk, ic0=ic0, dgate=dgate, agate=agate, lhs=lhs, rhs=rhs):
                    c.wait(t, "dve5", dgate)
                    c.wait(t, "act5", agate)
                    for i2 in range(2):
                        for kc in range(4):
                            ins = t.matmul(b["pb"][bk[i2]][:],
                                           b[lhs][:, kc, (ic0 + i2) * 128:(ic0 + i2 + 1) * 128],
                                           b[rhs][:, kc, :],
                                           start=(kc == 0), stop=(kc == 3))
                        c.inc(ins, "pe5", 1)
                c.sec("tensor", t_s)
                p_aft = c.cnt["pe5"]

                def v_s(v, sl=sl, bk=bk, ic0=ic0, p_aft=p_aft, stage=stage,
                        aa=aa, bb_=bb_, cc_=cc_):
                    c.wait(v, "pe5", p_aft)
                    for i2 in range(2):
                        ic = ic0 + i2
                        if stage == "A":
                            ins = v.tensor_copy(out=b[sl["Ab"]][:, ic, :], in_=b["pb"][bk[i2]][:])
                        elif stage == "A2":
                            v.tensor_scalar(out=b["scr5f"][:, :], in0=b["pb"][bk[i2]][:],
                                            scalar1=cc_, scalar2=None, op0=ALU.mult)
                            ins = v.scalar_tensor_tensor(out=b[sl["Bb"]][:, ic, :],
                                                         in0=b[sl["Ab"]][:, ic, :],
                                                         scalar=bb_, in1=b["scr5f"][:, :],
                                                         op0=ALU.mult, op1=ALU.add)
                        else:
                            ins = v.scalar_tensor_tensor(out=b[sl["X"]][:, ic, :],
                                                         in0=b[sl["X"]][:, ic, :],
                                                         scalar=aa, in1=b["pb"][bk[i2]][:],
                                                         op0=ALU.mult, op1=ALU.add)
                    c.inc(ins, "dve5", 1)
                c.sec("vector", v_s)
                d_h[(w, hf)] = c.cnt["dve5"]
            return [d_h[(0, 1)], d_h[(1, 1)]]

        def ns_2way(xw, xa):
            for it in range(5):
                coef = NS_COEFFS[it]
                dA = ns_stage2("A", xw, xa, coef)
                dB = ns_stage2("A2", dA, xa, coef)
                xw = ns_stage2("BX", dB, xa, coef)
                if it < 4:
                    xa = emit_ctr([(SLB[0], xw[0]), (SLB[1], xw[1])])
            return xw

        def ns_solo(sl, xw, xa):
            for it in range(5):
                aa, bb_, cc_ = NS_COEFFS[it]
                for stage in ("A", "A2", "BX"):
                    lhs = {"A": sl["XTb"], "A2": sl["Ab"], "BX": sl["Bb"]}[stage]
                    rhs = {"A": sl["XTb"], "A2": sl["Ab"], "BX": sl["Xb"]}[stage]
                    p0 = c.cnt["pe5"]

                    def t_s(t, lhs=lhs, rhs=rhs, xw=xw, xa=xa):
                        c.wait(t, "dve5", xw)
                        c.wait(t, "act5", xa)
                        for ic in range(4):
                            for kc in range(4):
                                ins = t.matmul(b["pb"][ic][:],
                                               b[lhs][:, kc, ic * 128:(ic + 1) * 128],
                                               b[rhs][:, kc, :],
                                               start=(kc == 0), stop=(kc == 3))
                            c.inc(ins, "pe5", 1)
                    c.sec("tensor", t_s)

                    def v_s(v, sl=sl, p0=p0, stage=stage, aa=aa, bb_=bb_, cc_=cc_):
                        for ic in range(4):
                            c.wait(v, "pe5", p0 + ic + 1)
                            if stage == "A":
                                ins = v.tensor_copy(out=b[sl["Ab"]][:, ic, :], in_=b["pb"][ic][:])
                            elif stage == "A2":
                                v.tensor_scalar(out=b["scr5f"][:, :], in0=b["pb"][ic][:],
                                                scalar1=cc_, scalar2=None, op0=ALU.mult)
                                ins = v.scalar_tensor_tensor(out=b[sl["Bb"]][:, ic, :],
                                                             in0=b[sl["Ab"]][:, ic, :],
                                                             scalar=bb_, in1=b["scr5f"][:, :],
                                                             op0=ALU.mult, op1=ALU.add)
                            else:
                                ins = v.scalar_tensor_tensor(out=b[sl["X"]][:, ic, :],
                                                             in0=b[sl["X"]][:, ic, :],
                                                             scalar=aa, in1=b["pb"][ic][:],
                                                             op0=ALU.mult, op1=ALU.add)
                            c.inc(ins, "dve5", 1)
                    c.sec("vector", v_s)
                    xw = c.cnt["dve5"]
                if it < 4:
                    xa = emit_ctr([(sl, xw)])[0]
            return xw

        def emit_up(wsel, sl, bkpair, d_fin):
            p_u0 = c.cnt["pe5"]
            d_c0 = c.cnt["dve5"]
            for ic in range(4):
                bkx = bkpair[ic % 2]

                def t_tru(t, sl=sl, ic=ic, bkx=bkx, d_fin=d_fin, d_c0=d_c0):
                    if ic == 0:
                        c.wait(t, "dve5", d_fin)
                    if ic >= 2:
                        c.wait(t, "dve5", d_c0 + ic - 1)
                    for kc in range(4):
                        ins = t.transpose(b["pb"][bkx][:, kc * 128:(kc + 1) * 128],
                                          b[sl["X"]][:, kc, ic * 128:(ic + 1) * 128],
                                          b["idn5f"][:])
                    c.inc(ins, "pe5", 1)

                def v_add(v, wsel=wsel, ic=ic, bkx=bkx, p_u0=p_u0):
                    c.wait(v, "pe5", p_u0 + ic + 1)
                    ins = v.tensor_tensor(out=b["wT"][wsel][:, ic, :],
                                          in0=b["wT"][wsel][:, ic, :],
                                          in1=b["pb"][bkx][:], op=ALU.add)
                    c.inc(ins, "dve5", 1)

                c.sec("tensor", t_tru)
                c.sec("vector", v_add)
            if wsel == 1:
                def v_w1u(v, sl=sl):
                    for a_ in range(4):
                        ins = v.tensor_tensor(out=b["w1n"][:, a_, :], in0=b["w1n"][:, a_, :],
                                              in1=b[sl["X"]][:, a_, :], op=ALU.add)
                    c.inc(ins, "dve5", 1)
                c.sec("vector", v_w1u)
            d_u1 = c.cnt["dve5"]

            def s_u1(s, wsel=wsel, d_u1=d_u1):
                c.wait(s, "dve5", d_u1)
                ins = s.activation(out=b["sq5"][:].rearrange("p a h -> p (a h)"),
                                   in_=b["wT"][wsel][:].rearrange("p a h -> p (a h)"),
                                   func=AF.Square, scale=1.0)
                c.inc(ins, "act5", 1)
            c.sec("scalar", s_u1)
            a_u1 = c.cnt["act5"]

            def t_u1(t, a_u1=a_u1, d_u1=d_u1):
                c.wait(t, "dve5", d_u1)
                COLSUM(t, 4, a_u1)
            c.sec("tensor", t_u1)
            p_u1 = c.cnt["pe5"]

            def v_u2(v, p_u1=p_u1):
                c.wait(v, "pe5", p_u1)
                ins = v.tensor_copy(out=b["rc1"][:, :], in_=b["pb"][4][:])
                c.inc(ins, "dve5", 1)
            c.sec("vector", v_u2)
            d_u2 = c.cnt["dve5"]

            def s_u2(s, d_u2=d_u2):
                c.wait(s, "dve5", d_u2)
                ins = s.activation(out=b["rc1"][:, :], in_=b["rc1"][:, :],
                                   func=AF.Sqrt, scale=1.0)
                c.inc(ins, "act5", 1)
            c.sec("scalar", s_u2)
            a_u2 = c.cnt["act5"]

            def v_u3(v, wsel=wsel, a_u2=a_u2):
                c.wait(v, "act5", a_u2)
                v.tensor_scalar(out=b["rc1"][:, :], in0=b["rc1"][:, :], scalar1=1e-5,
                                scalar2=None, op0=ALU.add)
                ins = v.reciprocal(out=b["rc2"][:, :], in_=b["rc1"][:, :])
                c.inc(ins, "dve5", 1)
                c.wait(v, "dve5")
                ins = v.tensor_tensor(out=b["rc2"][:, :], in0=b["rc2"][:, :],
                                      in1=b["nwb"][wsel][:], op=ALU.mult)
                c.inc(ins, "dve5", 1)
                for a_ in range(4):
                    ins = v.tensor_tensor(out=b["wT"][wsel][:, a_, :],
                                          in0=b["wT"][wsel][:, a_, :],
                                          in1=b["rc2"][:, :], op=ALU.mult)
                c.inc(ins, "dve5", 1)
            c.sec("vector", v_u3)

            if wsel == 1:
                d_w0 = c.cnt["dve5"]

                def s_w1(s, d_w0=d_w0):
                    c.wait(s, "dve5", d_w0)
                    ins = s.activation(out=b["sq5"][:].rearrange("p a h -> p (a h)"),
                                       in_=b["w1n"][:].rearrange("p a h -> p (a h)"),
                                       func=AF.Square, scale=1.0)
                    c.inc(ins, "act5", 1)
                c.sec("scalar", s_w1)
                a_w1 = c.cnt["act5"]

                def v_w1(v, a_w1=a_w1):
                    c.wait(v, "act5", a_w1)
                    for a_ in range(4):
                        ins = v.reduce_sum(out=b["r1c"][:, a_:a_ + 1], in_=b["sq5"][:, a_, :],
                                           axis=AXX)
                    c.inc(ins, "dve5", 1)
                    c.wait(v, "dve5")
                c.sec("vector", v_w1)
                d_w1 = c.cnt["dve5"]

                def s_w1b(s, d_w1=d_w1):
                    c.wait(s, "dve5", d_w1)
                    ins = s.activation(out=b["r1c"][:, 0:4], in_=b["r1c"][:, 0:4], func=AF.Sqrt,
                                       scale=1.0)
                    c.inc(ins, "act5", 1)
                c.sec("scalar", s_w1b)
                a_w1b = c.cnt["act5"]

                def v_w1b(v, a_w1b=a_w1b):
                    c.wait(v, "act5", a_w1b)
                    v.tensor_scalar(out=b["r1c"][:, 0:4], in0=b["r1c"][:, 0:4], scalar1=1e-5,
                                    scalar2=None, op0=ALU.add)
                    ins = v.reciprocal(out=b["r1d"][:, 0:4], in_=b["r1c"][:, 0:4])
                    c.inc(ins, "dve5", 1)
                    c.wait(v, "dve5")
                    ins = v.tensor_tensor(out=b["r1d"][:, 0:4], in0=b["r1d"][:, 0:4],
                                          in1=b["n1r"][:, 0:4], op=ALU.mult)
                    c.inc(ins, "dve5", 1)
                    c.wait(v, "dve5")
                    for a_ in range(4):
                        ins = v.tensor_scalar(out=b["w1n"][:, a_, :], in0=b["w1n"][:, a_, :],
                                              scalar1=b["r1d"][:, a_:a_ + 1], scalar2=None,
                                              op0=ALU.mult)
                    c.inc(ins, "dve5", 1)
                c.sec("vector", v_w1b)

        d_x0_, st_m0 = emit_dw(0, SLB[0], d_bwd - 1)
        d_n0 = emit_fr(SLB[0], d_x0_, st_m0)
        d_x1_, st_m1 = emit_dw(1, SLB[1], d_x0_)
        d_n1 = emit_fr(SLB[1], d_x1_, st_m1)
        a_ct = emit_ctr([(SLB[0], d_n0), (SLB[1], d_n1)])
        d_fin = ns_2way([max(d_n0, d_x1_), max(d_n1, d_x1_)], a_ct)
        emit_up(0, SLB[0], (0, 1), d_fin[0])
        emit_up(1, SLB[1], (2, 3), d_fin[1])
        d_x2_, st_m2 = emit_dw(2, SLB[0], c.cnt["dve5"])
        d_n2 = emit_fr(SLB[0], d_x2_, st_m2)
        a2 = emit_ctr([(SLB[0], d_n2)])[0]
        d_f2 = ns_solo(SLB[0], d_n2, a2)
        emit_up(2, SLB[0], (0, 1), d_f2)
        d_gate_in = d_x2_



    st5_all = c.cnt["s_st5"]

    def g_cc4(g):
        c.wait(g, "s_st5", st5_all)
        with allow_ici_all_to_all():
            ins = g.collective_compute("AllToAll", ALU.bypass,
                                       replica_groups=[list(range(NCORES))],
                                       ins=[b["cc4_in"].ap().opt()],
                                       outs=[b["cc4_out"].ap().opt()])
        c.inc(ins, "cc", 1)

    c.sec("gpsimd", g_cc4)



def emit_p5b(c):
    b = c.buf
    cc_need = c.cnt["cc"]
    sp0 = c.cnt["s_sp"]

    pe5_all = c.cnt["pe5"]
    d5_all = c.cnt["dve5"]
    a5_all = c.cnt["act5"]
    st5b = c.cnt["s_st5"]

    def g_5b(g):
        c.wait(g, "cc", cc_need)
        c.wait(g, "pe5", pe5_all)
        c.wait(g, "dve5", d5_all)
        c.wait(g, "act5", a5_all)
        c.wait(g, "s_st5", st5b)
        ins = g.dma_start(out=b["ong"][:], in_=b["in_ong"])
        c.inc(ins, "s_ld", 16)

    c.sec("gpsimd", g_5b)
    gld_b = c.cnt["s_ld"]

    def sy_r(sy):
        c.wait(sy, "s_sp", sp0)
        c.wait(sy, "pe5", pe5_all)
        c.wait(sy, "dve5", d5_all)
        c.wait(sy, "act5", a5_all)
        ins = sy.dma_start(out=b["R"][:].rearrange("p (dc t) -> p dc t", dc=DC),
                           in_=b["rspill"].ap())
        c.inc(ins, "s_sp", 16)

    c.sec("sync", sy_r)

    st_h0 = c.cnt["s_st5"]
    d5b0 = c.cnt["dve5"]
    for h in range(4):
        def sy_h(sy, h=h):
            c.wait(sy, "s_ld", gld_b)
            if h >= 1:
                c.wait(sy, "act5", c.cnt["act5"])
            if h >= 2:
                c.wait(sy, "dve5", c.cnt["dve5"])
            ins = sy.dma_start(out=b["ttth"][:, h % 2, :, :],
                               in_=b["cc4_out"].ap()[2 * h * 512:(2 * h + 1) * 512, :]
                               .rearrange("(hc p) t -> p hc t", p=128))
            c.inc(ins, "s_st5", 16)
            ins = sy.dma_start(out=b["attnh"][:, h % 2, :, :],
                               in_=b["attn_loc"].ap()[h * 512:(h + 1) * 512, :]
                               .rearrange("(hc p) t -> p hc t", p=128))
            c.inc(ins, "s_st5", 16)
            ins = sy.dma_start(out=b["sc_row"][0:1, :], in_=b["scdram"].ap()[h:h + 1, :])
            c.inc(ins, "s_st5", 16)

        c.sec("sync", sy_h)
        st_h = c.cnt["s_st5"]
        a_h0 = c.cnt["act5"]

        def s_h(s, h=h, st_h=st_h):
            c.wait(s, "s_st5", st_h)
            for hc in range(4):
                if hc >= 1:
                    c.wait(s, "pe5", c.cnt["pe5"] + hc - 4 if False else c.cnt["pe5"])
                ins = s.activation(out=b["sqo"][:, hc % 2, :], in_=b["ttth"][:, h % 2, hc, :],
                                   func=AF.Square, scale=1.0)
                c.inc(ins, "act5", 1)

        a_sq0 = c.cnt["act5"]
        p_o0 = c.cnt["pe5"]

        def s_h2(s, h=h, st_h=st_h, p_o0=p_o0):
            c.wait(s, "s_st5", st_h)
            for hc in range(4):
                if hc >= 2:
                    c.wait(s, "pe5", p_o0 + hc - 1)
                ins = s.activation(out=b["sqo"][:, hc % 2, :], in_=b["ttth"][:, h % 2, hc, :],
                                   func=AF.Square, scale=1.0)
                c.inc(ins, "act5", 1)

        c.sec("scalar", s_h2)

        def t_h(t, h=h, a_sq0=a_sq0):
            if h >= 1:
                c.wait(t, "dve5", d5b0 + 6 * h)
            for hc in range(4):
                c.wait(t, "act5", a_sq0 + hc + 1)
                for j in range(2):
                    ins = t.matmul(b["pb"][j][:], b["ones128"][:],
                                   b["sqo"][:, hc % 2, j * 512:(j + 1) * 512],
                                   start=(hc == 0), stop=(hc == 3))
                c.inc(ins, "pe5", 1)
            for j in range(2):
                ins = t.matmul(b["pb"][2 + j][:], b["ones1w"][0:1, 0:128],
                               b["sc_row"][0:1, j * 512:(j + 1) * 512],
                               start=True, stop=True)
            c.inc(ins, "pe5", 1)

        c.sec("tensor", t_h)
        p_o = c.cnt["pe5"]
        a_r0 = c.cnt["act5"]

        def s_h3(s, h=h, p_o=p_o):
            c.wait(s, "pe5", p_o)
            for j in range(2):
                ins = s.activation(out=b["nrm5"][:, j * 512:(j + 1) * 512], in_=b["pb"][j][:],
                                   func=AF.Sqrt, scale=1.0 / 512.0, bias=b["c_eps"][:])
                c.inc(ins, "act5", 1)

        c.sec("scalar", s_h3)
        a_r = c.cnt["act5"]
        d_h0 = c.cnt["dve5"]

        def v_h(v, h=h, a_r=a_r, p_o=p_o):
            c.wait(v, "act5", a_r)
            c.wait(v, "pe5", p_o)
            ins = v.reciprocal(out=b["rstdb"][:, :], in_=b["scb5"][:, :] if False else b["nrm5"][:, :])
            c.inc(ins, "dve5", 1)
            v.tensor_copy(out=b["scb5"][:, 0:512], in_=b["pb"][2][:])
            ins = v.tensor_copy(out=b["scb5"][:, 512:1024], in_=b["pb"][3][:])
            c.inc(ins, "dve5", 1)
            for hc in range(4):
                v.tensor_tensor(out=b["t5"][:, :], in0=b["ttth"][:, h % 2, hc, :],
                                in1=b["rstdb"][:, :], op=ALU.mult)
                v.tensor_scalar(out=b["t5"][:, :], in0=b["t5"][:, :],
                                scalar1=b["ong"][:, hc:hc + 1], scalar2=None,
                                op0=ALU.mult)
                v.tensor_tensor(out=b["t5"][:, :], in0=b["t5"][:, :], in1=b["scb5"][:, :],
                                op=ALU.mult)
                ins = v.tensor_tensor(out=b["zo"][:, h * 4 + hc, :], in0=b["t5"][:, :],
                                      in1=b["attnh"][:, h % 2, hc, :], op=ALU.add)
                c.inc(ins, "dve5", 1)

        c.sec("vector", v_h)

    d_zo = c.cnt["dve5"]

    gw0 = c.cnt["g_ld"]

    def g_wo(g):
        for odc in range(DC):
            if odc >= 2:
                c.wait(g, "pe5", c.cnt["pe5"])
            ins = g.dma_start(
                out=b["wf5"][:, odc % 2, :, :],
                in_=b["in_Wo"][:, odc * 128:(odc + 1) * 128].rearrange("(dc p) e -> p dc e", p=128),
                max_dma_last_dim=2048)
            c.inc(ins, "g_ld", 16)

    p_wo0 = c.cnt["pe5"]

    def g_wo2(g, p_wo0=p_wo0):
        for odc in range(DC):
            if odc >= 1:
                c.wait(g, "pe5", p_wo0 + 2 * odc)
            ins = g.dma_start(
                out=b["wf5"][:, odc % 2, :, :],
                in_=b["in_Wo"][:, odc * 128:(odc + 1) * 128].rearrange("(dc p) e -> p dc e", p=128),
                max_dma_last_dim=2048)
            c.inc(ins, "g_ld", 16)

    c.sec("gpsimd", g_wo2)

    d_wo0 = c.cnt["dve5"]

    def t_wo(t, d_zo=d_zo, p_wo0=p_wo0, gw0=gw0):
        c.wait(t, "dve5", d_zo)
        for odc in range(DC):
            c.wait(t, "g_ld", gw0 + 16 * (odc + 1))
            for j in range(2):
                idx = odc * 2 + j
                if idx >= 2:
                    c.wait(t, "dve5", d_wo0 + idx - 1)
                for kc in range(DC):
                    ins = t.matmul(b["pb"][4 + j][:], b["wf5"][:, odc % 2, kc, :],
                                   b["zo"][:, kc, j * 512:(j + 1) * 512],
                                   start=(kc == 0), stop=(kc == DC - 1))
                c.inc(ins, "pe5", 1)

    c.sec("tensor", t_wo)
    p_wo = c.cnt["pe5"]
    sp_r = c.cnt["s_sp"]

    def v_wo(v, p_wo0=p_wo0, sp_r=sp_r):
        c.wait(v, "s_sp", sp_r)
        n = 0
        for odc in range(DC):
            for j in range(2):
                c.wait(v, "pe5", p_wo0 + n + 1)
                sl = slice(odc * T + j * 512, odc * T + (j + 1) * 512)
                ins = v.tensor_tensor(out=b["R"][:, sl], in0=b["R"][:, sl],
                                      in1=b["pb"][4 + j][:], op=ALU.add)
                n += 1
                if n < 2 * DC:
                    c.inc(ins, "dve5", 1)
        c.inc(ins, "res", 1)

    c.sec("vector", v_wo)

def build_nc():
    nc = bass.Bass("TRN2", target_bir_lowering=False, detect_race_conditions=False)
    stack = contextlib.ExitStack()
    b = {}

    def din(name, shape, dt=F32):
        b[name] = nc.dram_tensor(name, list(shape), dt, kind="ExternalInput").ap()

    din("xT", (D, T))
    din("in_aWqT", (D, D)); din("in_aWkT", (D, D)); din("in_aWvT", (D, D)); din("in_aWgT", (D, D))
    din("in_bq16", (128, 16)); din("in_bg16", (128, 16))
    din("in_bkv", (128, NCORES * 512))
    din("in_UTmix", (512, 512))
    din("in_m8", (NCORES, 1))
    din("in_fW1", (D, D)); din("in_fb1", (128, 16))
    din("in_fW2", (D, 8)); din("in_fb2b", (128, 8))
    din("in_VT", (D, 512)); din("in_UTf", (512, D))
    din("in_E8", (8, 512)); din("in_idn", (128, 128))
    din("in_Wqkv", (D, 3 * D))
    din("in_qks", (128, 16, 2)); din("in_qko", (128, 16, 2))
    din("in_bm", (4, 1)); din("in_bs", (4, 1))
    din("in_Wsm", (D, 20))
    din("in_dyn", (8, 1), mybir.dt.int32)
    din("in_wm", (8, 128, 1152))
    din("in_wT", (1536, 512))
    din("in_w1n", (512, 512))
    din("in_ong", (128, 4))
    din("in_Wo", (D, D))
    b["yT"] = nc.dram_tensor("yT", [D, T], F32, kind="ExternalOutput").ap()
    if DEBUG_DUMP:
        b["dbg_xn"] = nc.dram_tensor("dbg_xn", [DC, 128, T], BF16, kind="ExternalOutput").ap()
        b["dbg_attn"] = nc.dram_tensor("dbg_attn", [D, T], BF16, kind="ExternalOutput").ap()
        b["dbg_scan"] = nc.dram_tensor("dbg_scan", [3 * D, T], BF16, kind="ExternalOutput").ap()
        b["dbg_tt"] = nc.dram_tensor("dbg_tt", [D, T], BF16, kind="ExternalOutput").ap()
        b["dbg_prev"] = nc.dram_tensor("dbg_prev", [2 * D, T], BF16, kind="ExternalOutput").ap()
        b["dbg_wloc"] = nc.dram_tensor("dbg_wloc", [3 * D, T], BF16, kind="ExternalOutput").ap()
    b["cc1_in"] = nc.dram_tensor("cc1_in", [D, 1], F32)
    b["cc1_out"] = nc.dram_tensor("cc1_out", [NCORES * D, 1], F32, addr_space="Shared")
    b["wloc"] = nc.dram_tensor("wloc", [3 * D, T], BF16)
    b["cc2_in"] = nc.dram_tensor("cc2_in", [2 * D, T], BF16)
    b["cc2_out"] = nc.dram_tensor("cc2_out", [NCORES * 2 * D, T], BF16, addr_space="Shared")
    b["cc3_in"] = nc.dram_tensor("cc3_in", [3 * D, T], BF16)
    b["cc3_out"] = nc.dram_tensor("cc3_out", [NCORES * 3 * D, T], BF16, addr_space="Shared")
    b["cc3b_in"] = nc.dram_tensor("cc3b_in", [16, T], F32)
    b["cc3b_out"] = nc.dram_tensor("cc3b_out", [NCORES * 16, T], F32, addr_space="Shared")
    b["attn_loc"] = nc.dram_tensor("attn_loc", [D, T], BF16)
    b["prevloc"] = nc.dram_tensor("prevloc", [2 * D, T], BF16)
    b["rspill"] = nc.dram_tensor("rspill", [128, DC, T], F32)
    b["mdram"] = nc.dram_tensor("mdram", [6 * 512, 512], F32)
    b["scanloc"] = nc.dram_tensor("scanloc", [8 * 1536, T], BF16)
    b["scanlocb"] = nc.dram_tensor("scanlocb", [32, T], F32)
    b["cc4_in"] = nc.dram_tensor("cc4_in", [NCORES * 512, T], BF16)
    b["cc4_out"] = nc.dram_tensor("cc4_out", [NCORES * 512, T], BF16)
    b["ttloc"] = nc.dram_tensor("ttloc", [D, T], BF16)
    b["scdram"] = nc.dram_tensor("scdram", [4, T], F32)

    with stack:
        with nc.Block() as block:
            c = Ctx(nc, block, stack)
            c.buf = b
            for s in ("g_ld", "s_ld", "pe", "dve", "act", "pool", "cc", "wr", "init",
                      "res", "dve_kv", "dve_tot", "dve_offs", "act_g", "dve_c",
                      "dve_l", "dve_s", "act_s", "dve_x",
                      "dve_q", "act_q", "act_sq", "act_n", "dve_n", "dve_n0", "pe_n",
                      "act_sm", "dve_sm", "s_st", "s_st2", "s_st3",
                      "dve_w", "act_w", "pe_sc", "pe_tr", "pe_pv", "pe_av", "s_stw",
                      "pe5", "dve5", "act5", "s_st5", "s_sp", "g_cp"):
                c.new_sem(s)

            c.sb(stack, "ones128", (128, 128), BF16)
            c.sb(stack, "ones1", (128, 1), BF16)
            c.sb(stack, "ones1w", (1, 128), F32)
            c.sb(stack, "c_eps", (128, 1), F32)
            c.sb(stack, "c_one", (128, 1), F32)
            c.sb(stack, "c_lrb", (128, 1), F32)
            c.sb(stack, "scale_sb", (4, T), F32)
            c.sb(stack, "dyn", (8, 1), mybir.dt.int32)
            b["pb"] = [stack.enter_context(nc.psum_tensor(f"pb{i}", [128, 512], F32))
                       for i in range(6)]

            def v_init0(v):
                v.memset(b["ones128"][:], 1.0)
                v.memset(b["ones1"][:], 1.0)
                v.memset(b["ones1w"][:], 1.0)
                v.memset(b["c_one"][:], 1.0)
                v.memset(b["c_lrb"][:], float(0.01 + np.log(-np.expm1(-0.01))))
                ins = v.memset(b["c_eps"][:], 1e-5)
                c.inc(ins, "init", 1)

            c.sec("vector", v_init0)

            with contextlib.ExitStack() as sA:
                c.sb(sA, "R", (128, DC * T), F32)
                c.sb(sA, "xn", (128, DC, T), BF16)
                c.sb(sA, "sq_scr", (128, 2, T), BF16)
                c.sb(sA, "rb_scr", (128, 2, T), BF16)
                for n in ("lnm", "lnv", "lnr", "t1"):
                    c.sb(sA, n, (128, T), F32)

                emit_p0(c)

                with contextlib.ExitStack() as p1s:
                    c.sb(p1s, "UTmix", (128, 4, 512), BF16)
                    c.sb(p1s, "kv_tm", (128, NTC, D), BF16)
                    c.sb(p1s, "kv_scr", (128, 2, 256), F32)
                    c.sb(p1s, "q_sb", (128, T), BF16)
                    c.sb(p1s, "g_sb", (128, T), BF16)
                    c.sb(p1s, "tot_row", (1, D), F32)
                    c.sb(p1s, "tot_all", (NCORES, D), F32)
                    c.sb(p1s, "offs", (128, DC), F32)
                    c.sb(p1s, "m8", (NCORES, 1), F32)

                    def g_consts(g):
                        for dst, srcn in (("bkv", "in_bkv"), ("m8", "in_m8")):
                            ins = g.dma_start(out=b[dst][:], in_=b[srcn])
                            c.inc(ins, "g_ld", 16)
                        ins = g.dma_start(out=b["UTmix"][:],
                                          in_=b["in_UTmix"].rearrange("(tc p) t -> p tc t", p=128),
                                          max_dma_last_dim=2048)
                        c.inc(ins, "g_ld", 16)

                    with contextlib.ExitStack() as p1sb:
                        c.sb(p1sb, "bkv", (128, NCORES * 512), BF16)
                        c.sb(p1sb, "wkv", (128, 2, 2, DC, 128), BF16)
                        c.sec("gpsimd", g_consts)
                        emit_p1_B(c)
                    with contextlib.ExitStack() as p1sc:
                        c.sb(p1sc, "wqg", (128, 2, 2, DC, 128), BF16)
                        c.sb(p1sc, "ones512", (128, 512), BF16)
                        c.sb(p1sc, "bq16", (128, 16), F32)
                        c.sb(p1sc, "bg16", (128, 16), F32)

                        def v_o512(v):
                            ins = v.memset(b["ones512"][:], 1.0)
                            c.inc(ins, "init", 1)
                        c.sec("vector", v_o512)

                        def g_qgb(g):
                            for dst, srcn in (("bq16", "in_bq16"), ("bg16", "in_bg16")):
                                ins = g.dma_start(out=b[dst][:], in_=b[srcn])
                                c.inc(ins, "g_ld", 16)
                        c.sec("gpsimd", g_qgb)
                        emit_p1_C(c)

                with contextlib.ExitStack() as p2s:
                    c.sb(p2s, "h_sb", (128, DC, T), BF16)
                    c.sb(p2s, "wf", (128, 1, DC, 128), BF16)
                    c.sb(p2s, "w2sb", (128, DC, 8), BF16)
                    c.sb(p2s, "fb1", (128, 16), F32)
                    c.sb(p2s, "b2b", (128, 8), F32)
                    c.sb(p2s, "vtsb", (128, DC, 512), BF16)
                    c.sb(p2s, "utsb", (128, 4, D), BF16)
                    c.sb(p2s, "e8sb", (8, 512), BF16)
                    c.sb(p2s, "idnt", (128, 128), BF16)
                    c.sb(p2s, "lg_tm", (128, NTC, 8), F32)
                    c.sb(p2s, "alpha_tm", (128, NTC, 8), BF16)
                    c.sb(p2s, "smx", (128, NTC), F32)
                    c.sb(p2s, "ssum", (128, NTC), F32)
                    c.sb(p2s, "srec", (128, NTC), F32)
                    c.sb(p2s, "alpha_fm", (8, T), BF16)
                    c.sb(p2s, "ab_sb", (128, 512), BF16)
                    c.sb(p2s, "wvx", (128, 4, T), BF16)
                    b["pbt"] = p2s.enter_context(nc.psum_tensor("pbt", [128, 1024], BF16))
                    res_p1 = c.cnt["res"]

                    def g_fconsts(g):
                        c.wait(g, "res", res_p1)
                        for dst, srcn in (("w2sb", "in_fW2"), ("fb1", "in_fb1"),
                                          ("b2b", "in_fb2b"), ("e8sb", "in_E8"),
                                          ("idnt", "in_idn")):
                            if dst == "w2sb":
                                ins = g.dma_start(out=b[dst][:], in_=b[srcn].rearrange("(dc p) k -> p dc k", p=128))
                            else:
                                ins = g.dma_start(out=b[dst][:], in_=b[srcn])
                            c.inc(ins, "g_ld", 16)
                        ins = g.dma_start(out=b["vtsb"][:], in_=b["in_VT"].rearrange("(dc p) kr -> p dc kr", p=128),
                                          max_dma_last_dim=2048)
                        c.inc(ins, "g_ld", 16)
                        for kc in range(4):
                            ins = g.dma_start(out=b["utsb"][:, kc, :],
                                              in_=b["in_UTf"][kc * 128:(kc + 1) * 128, :],
                                              max_dma_last_dim=4096)
                            c.inc(ins, "g_ld", 16)

                    c.sec("gpsimd", g_fconsts)
                    emit_p2(c)

                with contextlib.ExitStack() as p3s:
                    c.sb(p3s, "wq3", (128, 2, DC, 128), BF16)
                    c.sb(p3s, "wout", (128, 2, T), BF16)
                    c.sb(p3s, "vout", (128, 2, T), BF16)
                    c.sb(p3s, "hbuf", (128, 2, 4, T), BF16)
                    c.sb(p3s, "sq2_scr", (128, 4, T), BF16)
                    c.sb(p3s, "nrm", (128, T), F32)
                    c.sb(p3s, "rstd", (128, T), F32)
                    c.sb(p3s, "qks", (128, 16, 2), F32)
                    c.sb(p3s, "qko", (128, 16, 2), F32)
                    c.sb(p3s, "s0e", (128, 16), F32)
                    c.sb(p3s, "o0e", (128, 16), F32)
                    c.sb(p3s, "wsm", (128, DC, 20), BF16)
                    c.sb(p3s, "spa", (12, T), F32)
                    c.sb(p3s, "spb", (12, T), F32)
                    c.sb(p3s, "lr_sb", (12, T), F32)
                    c.sb(p3s, "mom_sb", (4, T), F32)
                    c.sb(p3s, "bm1", (4, 1), F32)
                    c.sb(p3s, "bs1", (4, 1), F32)
                    emit_p3(c)
                    if DEBUG_DUMP:
                        def sy_dbg3(sy):
                            c.wait(sy, "cc", c.cnt["cc"])
                            ins = sy.dma_start(out=b["dbg_scan"], in_=b["cc3_in"].ap())
                            c.inc(ins, "wr", 16)
                        c.sec("sync", sy_dbg3)

                dve_a = c.cnt["dve"]
                act_a = c.cnt["act"]

                def sy_sp(sy):
                    c.wait(sy, "dve", dve_a)
                    c.wait(sy, "act", act_a)
                    ins = sy.dma_start(out=b["rspill"].ap(),
                                       in_=b["R"][:].rearrange("p (dc t) -> p dc t", dc=DC))
                    c.inc(ins, "s_sp", 16)

                c.sec("sync", sy_sp)
                sp_done = c.cnt["s_sp"]

                def v_spgate(v):
                    c.wait(v, "s_sp", sp_done)

                c.sec("vector", v_spgate)

            with contextlib.ExitStack() as p4s:
                c.sb(p4s, "wm", (128, 8, 1152), BF16)
                c.sb(p4s, "aqh", (128, 2, T), BF16)
                c.sb(p4s, "kwin", (128, 2, 2048), BF16)
                c.sb(p4s, "avwin", (128, 2, 2048), BF16)
                c.sb(p4s, "avtm", (128, 2048), BF16)
                c.sb(p4s, "probs", (128, 1152), BF16)
                c.sb(p4s, "probs_n", (128, 1152), BF16)
                c.sb(p4s, "probsT", (128, 1152), BF16)
                c.sb(p4s, "scores", (128, 1152), F32)
                c.sb(p4s, "wmax", (128, 1), F32)
                c.sb(p4s, "wmaxn", (128, 1), F32)
                c.sb(p4s, "wsum", (128, 1), F32)
                c.sb(p4s, "wsum2", (128, 1), F32)
                c.sb(p4s, "wrs", (128, 1), F32)
                c.sb(p4s, "wo_sb", (128, 2, T), BF16)
                c.sb(p4s, "idn4", (128, 128), BF16)
                b["pbt2"] = p4s.enter_context(nc.psum_tensor("pbt2", [128, 2048], BF16))
                emit_p4(c)
                if DEBUG_DUMP:
                    def sy_dbg4(sy):
                        c.wait(sy, "s_stw", c.cnt["s_stw"])
                        ins = sy.dma_start(out=b["dbg_attn"], in_=b["attn_loc"].ap())
                        c.inc(ins, "wr", 16)
                        ins = sy.dma_start(out=b["dbg_prev"], in_=b["prevloc"].ap())
                        c.inc(ins, "wr", 16)
                        ins = sy.dma_start(out=b["dbg_wloc"], in_=b["wloc"].ap())
                        c.inc(ins, "wr", 16)
                    c.sec("sync", sy_dbg4)

            with contextlib.ExitStack() as p5s:
                c.sb(p5s, "qiT", (128, 4, T), BF16)
                c.sb(p5s, "kiT", (128, 4, T), BF16)
                c.sb(p5s, "vi", (128, 4, T), BF16)
                b["wT"] = [p5s.enter_context(nc.sbuf_tensor(f"wT{w}", [128, 4, 512], F32))
                           for w in range(3)]
                c.sb(p5s, "w1n", (128, 4, 512), F32)
                b["wTb"] = [p5s.enter_context(nc.sbuf_tensor(f"wTb{w}", [128, 4, 512], BF16))
                            for w in range(3)]
                c.sb(p5s, "w1nb", (128, 4, 512), BF16)
                b["nwb"] = [p5s.enter_context(nc.sbuf_tensor(f"nwb{w}", [128, 512], F32))
                            for w in range(3)]
                c.sb(p5s, "n1r", (128, 4), F32)
                c.sb(p5s, "sq5", (128, 4, 512), BF16)
                c.sb(p5s, "scb", (4, T), F32)
                c.sb(p5s, "lmt", (128, 8, 4), F32)
                c.sb(p5s, "mmr", (128, 1), F32)
                c.sb(p5s, "mm1", (1, 1), F32)
                c.sb(p5s, "mmb", (128, 1), F32)
                c.sb(p5s, "scr5", (128, 2, 512), BF16)
                c.sb(p5s, "gh", (128, 4, T), BF16)
                c.sb(p5s, "ocb", (128, 4, T), BF16)
                for n in ("gba", "sg", "hbm", "dhid", "dsil"):
                    c.sb(p5s, n, (128, 8, 512), BF16)
                c.sb(p5s, "X", (128, 4, 512), F32)
                c.sb(p5s, "Xb", (128, 4, 512), BF16)
                c.sb(p5s, "XTb", (128, 4, 512), BF16)
                c.sb(p5s, "Ab", (128, 4, 512), BF16)
                c.sb(p5s, "Bb", (128, 4, 512), BF16)
                c.sb(p5s, "X2", (128, 4, 512), F32)
                c.sb(p5s, "Xb2", (128, 4, 512), BF16)
                c.sb(p5s, "XTb2", (128, 4, 512), BF16)
                c.sb(p5s, "mst", (128, 4, 512), F32)
                c.sb(p5s, "scr5f", (128, 512), F32)
                c.sb(p5s, "fr1", (128, 1), F32)
                c.sb(p5s, "fr2", (128, 1), F32)
                c.sb(p5s, "rc1", (128, 512), F32)
                c.sb(p5s, "rc2", (128, 512), F32)
                c.sb(p5s, "r1c", (128, 4), F32)
                c.sb(p5s, "r1d", (128, 4), F32)
                c.sb(p5s, "idn5", (128, 128), BF16)
                c.sb(p5s, "idn5f", (128, 128), F32)
                b["hl1"] = b["gh"][:].rearrange("p a (x t) -> p (a x) t", x=2)
                b["Ab2"] = b["vi"][:, :, 0:512]
                b["Bb2"] = b["vi"][:, :, 512:1024]
                b["pbt5"] = p5s.enter_context(nc.psum_tensor("pbt5", [128, 1024], BF16))
                b["pbt5c"] = p5s.enter_context(nc.psum_tensor("pbt5c", [128, 1024], BF16))

                stw_all = c.cnt["s_stw"]
                ppv_all = c.cnt["pe_pv"]
                dw_all = c.cnt["dve_w"]
                aw_all = c.cnt["act_w"]

                def g_idn5(g):
                    c.wait(g, "s_stw", stw_all)
                    c.wait(g, "pe_pv", ppv_all)
                    c.wait(g, "dve_w", dw_all)
                    c.wait(g, "act_w", aw_all)
                    ins = g.dma_start(out=b["idn5"][:], in_=b["in_idn"])
                    c.inc(ins, "g_ld", 16)
                    ins = g.dma_start(out=b["idn5f"][:], in_=b["in_idn"])
                    c.inc(ins, "g_ld", 16)

                c.sec("gpsimd", g_idn5)
                emit_p5(c)

            with contextlib.ExitStack() as p5bs:
                c.sb(p5bs, "R", (128, DC * T), F32)
                with contextlib.ExitStack() as pb1:
                    c.sb(pb1, "zo", (128, DC, T), BF16)
                    c.sb(pb1, "wf5", (128, 2, DC, 128), BF16)
                    c.sb(pb1, "ttth", (128, 2, 4, T), BF16)
                    c.sb(pb1, "attnh", (128, 2, 4, T), BF16)
                    c.sb(pb1, "sqo", (128, 2, T), BF16)
                    c.sb(pb1, "nrm5", (128, T), F32)
                    c.sb(pb1, "rstdb", (128, T), F32)
                    c.sb(pb1, "scb5", (128, T), F32)
                    c.sb(pb1, "t5", (128, T), F32)
                    c.sb(pb1, "ong", (128, 4), F32)
                    c.sb(pb1, "sc_row", (1, T), F32)
                    emit_p5b(c)
                    if DEBUG_DUMP:
                        def sy_dbg5(sy):
                            c.wait(sy, "res", c.cnt["res"])
                            ins = sy.dma_start(out=b["dbg_tt"], in_=b["ttloc"].ap())
                            c.inc(ins, "wr", 16)
                        c.sec("sync", sy_dbg5)

                with contextlib.ExitStack() as pb2:
                    c.sb(pb2, "xn", (128, DC, T), BF16)
                    c.sb(pb2, "sq_scr", (128, 2, T), BF16)
                    c.sb(pb2, "rb_scr", (128, 2, T), BF16)
                    for n in ("lnm", "lnv", "lnr", "t1"):
                        c.sb(pb2, n, (128, T), F32)
                    c.sb(pb2, "h_sb", (128, DC, T), BF16)
                    c.sb(pb2, "wf", (128, 1, DC, 128), BF16)
                    c.sb(pb2, "w2sb", (128, DC, 8), BF16)
                    c.sb(pb2, "fb1", (128, 16), F32)
                    c.sb(pb2, "b2b", (128, 8), F32)
                    c.sb(pb2, "vtsb", (128, DC, 512), BF16)
                    c.sb(pb2, "utsb", (128, 4, D), BF16)
                    c.sb(pb2, "e8sb", (8, 512), BF16)
                    c.sb(pb2, "idnt", (128, 128), BF16)
                    c.sb(pb2, "lg_tm", (128, NTC, 8), F32)
                    c.sb(pb2, "alpha_tm", (128, NTC, 8), BF16)
                    c.sb(pb2, "smx", (128, NTC), F32)
                    c.sb(pb2, "ssum", (128, NTC), F32)
                    c.sb(pb2, "srec", (128, NTC), F32)
                    c.sb(pb2, "alpha_fm", (8, T), BF16)
                    c.sb(pb2, "ab_sb", (128, 512), BF16)
                    c.sb(pb2, "wvx", (128, 4, T), BF16)
                    b["pbt"] = pb2.enter_context(nc.psum_tensor("pbt5b", [128, 1024], BF16))
                    res_5b = c.cnt["res"]

                    def g_fc2(g):
                        c.wait(g, "res", res_5b)
                        for dst, srcn in (("w2sb", "in_fW2"), ("fb1", "in_fb1"),
                                          ("b2b", "in_fb2b"), ("e8sb", "in_E8"),
                                          ("idnt", "in_idn")):
                            if dst == "w2sb":
                                ins = g.dma_start(out=b[dst][:], in_=b[srcn].rearrange("(dc p) k -> p dc k", p=128))
                            else:
                                ins = g.dma_start(out=b[dst][:], in_=b[srcn])
                            c.inc(ins, "g_ld", 16)
                        ins = g.dma_start(out=b["vtsb"][:], in_=b["in_VT"].rearrange("(dc p) kr -> p dc kr", p=128),
                                          max_dma_last_dim=2048)
                        c.inc(ins, "g_ld", 16)
                        for kc in range(4):
                            ins = g.dma_start(out=b["utsb"][:, kc, :],
                                              in_=b["in_UTf"][kc * 128:(kc + 1) * 128, :],
                                              max_dma_last_dim=4096)
                            c.inc(ins, "g_ld", 16)

                    c.sec("gpsimd", g_fc2)
                    emit_p2(c)

                def sy_out(sy):
                    c.wait(sy, "res", c.cnt["res"])
                    ins = sy.dma_start(out=b["yT"].rearrange("(dc p) t -> p dc t", p=128),
                                       in_=b["R"][:].rearrange("p (dc t) -> p dc t", dc=DC))
                    c.inc(ins, "wr", 16)
                    c.wait(sy, "wr")

                c.sec("sync", sy_out)
    return nc


def kernel(**inputs):
    x = np.asarray(inputs["L0_attn_a_x"])
    Wq = np.asarray(inputs["L0_attn_a_Wq"]); bq = np.asarray(inputs["L0_attn_a_bq"])
    Wk = np.asarray(inputs["L0_attn_a_Wk"]); bk = np.asarray(inputs["L0_attn_a_bk"])
    Wv = np.asarray(inputs["L0_attn_a_Wv"]); bv = np.asarray(inputs["L0_attn_a_bv"])
    Wg = np.asarray(inputs["L0_attn_a_Wg"]); bg = np.asarray(inputs["L0_attn_a_bg"])
    fU = np.asarray(inputs["L0_ffn_U"]); fV = np.asarray(inputs["L0_ffn_V"])
    fW1 = np.asarray(inputs["L0_ffn_W1"]); fb1 = np.asarray(inputs["L0_ffn_b1"])
    fW2 = np.asarray(inputs["L0_ffn_W2"]); fb2 = np.asarray(inputs["L0_ffn_b2"])
    Wqkv = np.asarray(inputs["L0_attn_b_Wqkv"])
    Wo = np.asarray(inputs["L0_attn_b_Wo"])
    w0 = np.asarray(inputs["L0_attn_b_w0"]); w1 = np.asarray(inputs["L0_attn_b_w1"])
    w2 = np.asarray(inputs["L0_attn_b_w2"]); ong = np.asarray(inputs["L0_attn_b_o_norm_g"])
    qk_s = np.asarray(inputs["L0_attn_b_qk_scale"]); qk_o = np.asarray(inputs["L0_attn_b_qk_offset"])
    Wlr = np.asarray(inputs["L0_attn_b_Wlr"]); Wm = np.asarray(inputs["L0_attn_b_Wm"])
    Ws = np.asarray(inputs["L0_attn_b_Ws"]); bm = np.asarray(inputs["L0_attn_b_bm"])
    bs = np.asarray(inputs["L0_attn_b_bs"])

    nc = build_nc()
    bkv = np.zeros((128, 16, 256), np.float32)
    for blk in range(16):
        bkv[:, blk, 0:128] = bk[blk * 128:(blk + 1) * 128][None, :]
        bkv[:, blk, 128:256] = bv[blk * 128:(blk + 1) * 128][None, :]
    ut = np.triu(np.ones((512, 512), np.float32))
    utmix = np.ascontiguousarray(ut.reshape(4, 128, 512).reshape(512, 512))
    shared = {
        "in_aWqT": np.ascontiguousarray(Wq.T),
        "in_aWkT": np.ascontiguousarray(Wk.T),
        "in_aWvT": np.ascontiguousarray(Wv.T),
        "in_aWgT": np.ascontiguousarray(Wg.T),
        "in_bq16": np.ascontiguousarray(bq.reshape(16, 128).T),
        "in_bg16": np.ascontiguousarray(bg.reshape(16, 128).T),
        "in_bkv": bkv.reshape(128, 16 * 256),
        "in_UTmix": utmix,
        "in_fW1": np.ascontiguousarray(fW1),
        "in_fb1": np.ascontiguousarray(fb1.reshape(16, 128).T),
        "in_fW2": np.ascontiguousarray(fW2),
        "in_fb2b": np.ascontiguousarray(np.tile(fb2, (128, 1))),
        "in_VT": np.ascontiguousarray(fV.transpose(2, 0, 1).reshape(D, 512)),
        "in_UTf": np.ascontiguousarray(fU.transpose(0, 2, 1).reshape(512, D)),
        "in_E8": np.ascontiguousarray((np.arange(512)[None, :] // 64 == np.arange(8)[:, None]).astype(np.float32)),
        "in_idn": np.eye(128, dtype=np.float32),
        "in_Wqkv": np.ascontiguousarray(Wqkv),
        "in_qks": np.ascontiguousarray(qk_s.reshape(16, 128, 2).transpose(1, 0, 2)),
        "in_qko": np.ascontiguousarray(qk_o.reshape(16, 128, 2).transpose(1, 0, 2)),
        "in_bm": np.ascontiguousarray(bm.reshape(4, 1)),
        "in_bs": np.ascontiguousarray(bs.reshape(4, 1)),
        "in_Wsm": np.ascontiguousarray(np.concatenate([Wlr, Wm, Ws], axis=1)),
        "in_ong": np.ascontiguousarray(ong.reshape(4, 128).T),
        "in_Wo": np.ascontiguousarray(Wo),
    }
    in_maps = []
    for cid in range(NCORES):
        m = dict(shared)
        m["xT"] = np.ascontiguousarray(x[0, cid * T:(cid + 1) * T, :].T)
        m8 = np.zeros((NCORES, 1), np.float32)
        m8[:cid, 0] = 1.0
        m["in_m8"] = m8
        t_idx = np.arange(128)[:, None]
        rk = np.arange(1152)[None, :]
        wm = np.zeros((8, 128, 1152), np.float32)
        for i in range(8):
            valid = (rk >= t_idx) & (rk <= t_idx + 1024)
            if cid == 0:
                valid = valid & ((128 * i + rk) >= 1024)
            wm[i] = np.where(valid, 0.0, -1e30)
        m["in_wm"] = wm
        dyn = np.zeros((8, 1), np.int32)
        dyn[0, 0] = (cid // 2) * 1536
        dyn[1, 0] = ((cid + 7) % 8) * 2 * D
        dyn[2, 0] = cid * T
        dyn[3, 0] = (cid // 2) * 4
        dyn[4, 0] = cid * 3 * D
        dyn[5, 0] = cid * 16
        m["in_dyn"] = dyn
        H = cid // 2
        m["in_wT"] = np.ascontiguousarray(np.concatenate(
            [w0[H].T, w1[H].T, w2[H].T], axis=0))
        m["in_w1n"] = np.ascontiguousarray(w1[H])
        in_maps.append(m)
    res = run_bass_kernel_spmd(nc, in_maps, core_ids=list(range(NCORES)))
    out = np.empty((1, NCORES * T, D), np.float32)
    for cid in range(NCORES):
        out[0, cid * T:(cid + 1) * T, :] = res.results[cid]["yT"].T
    return out
